# version 01
# baseline (speedup 1.0000x reference)
import numpy as np
import concourse.bass as bass
import concourse.mybir as mybir
from concourse.bass import AP

F32 = mybir.dt.float32
BF16 = mybir.dt.bfloat16
AF = mybir.ActivationFunctionType
ALU = mybir.AluOpType
AX = mybir.AxisListType

T = 2048
D = 1024
NB = 2
HP = 8
S = 1024
SCALE = float(1.0 / np.sqrt(np.float32(64 * 3)))
E = 8
EPS = 1e-5
PADW = 1408
CT = D // 128
TT = T // 128
QCN = T // 512


class Prog:
    ENGS = ("sync", "tensor", "vector", "scalar", "gpsimd")

    def __init__(self, nc):
        self.nc = nc
        self.q = {e: [] for e in self.ENGS}
        self.sems = {}
        self.cnt = {}

    def _sem(self, name):
        if name not in self.sems:
            self.sems[name] = self.nc.alloc_semaphore(name)
            self.cnt[name] = 0
        return self.sems[name]

    def op(self, eng, kind, wait=(), inc=None, **kw):
        w = []
        for item in wait:
            if item is None:
                continue
            sn, val = item
            if val is None or val <= 0:
                continue
            self._sem(sn)
            assert val <= self.cnt[sn], f"wait {sn}>={val} but only {self.cnt[sn]}"
            w.append((sn, val))
        tag = None
        if inc is not None:
            sn, k = inc
            self._sem(sn)
            self.cnt[sn] += k
            tag = self.cnt[sn]
        self.q[eng].append((kind, kw, tuple(w), (inc[0], inc[1]) if inc else None))
        return tag

    def cur(self, name):
        self._sem(name)
        return self.cnt[name]

    def snapshot(self):
        return {n: self.cnt[n] for n in ("t", "v", "sc", "g") if n in self.cnt}

    def barrier(self):
        snap = dict(self.cnt)
        for eng in self.ENGS:
            waits = [(n, v) for n, v in snap.items()]
            self.op(eng, "noop", wait=waits)

    def emit(self):
        with self.nc.Block() as block:
            for ename in self.ENGS:
                ops = self.q[ename]
                if ops:
                    getattr(block, ename)(lambda e, ops=ops: self._run(e, ops))

    def _run(self, e, ops):
        for kind, kw, waits, inc in ops:
            for (sn, val) in waits:
                e.wait_ge(self.sems[sn], val)
            if kind == "noop":
                continue
            ins = getattr(self, "_" + kind)(e, **kw)
            if inc is not None:
                sn, k = inc
                ins.then_inc(self.sems[sn], k)

    def _mm(self, e, out, lhsT, rhs, start, stop, tp=None):
        if tp is not None:
            return e.matmul(out, lhsT, rhs, start=start, stop=stop, tile_position=tp)
        return e.matmul(out, lhsT, rhs, start=start, stop=stop)

    def _dma(self, e, out, in_, accum=False, transpose=False, slow=False):
        kw = {}
        if accum:
            kw["accum_op"] = ALU.add
        if transpose:
            kw["transpose"] = True
        if slow:
            kw["allow_slow_non_contiguous"] = True
        return e.dma_start(out=out, in_=in_, **kw)

    def _tt(self, e, out, a, b, alu):
        return e.tensor_tensor(out, a, b, alu)

    def _ts(self, e, out, in0, s1, s2=None, op0=ALU.mult, op1=None):
        if op1 is None:
            return e.tensor_single_scalar(out, in0, s1, op0)
        return e.tensor_scalar(out, in0, s1, s2, op0, op1)

    def _stt(self, e, out, in0, s, in1, op0, op1):
        return e.scalar_tensor_tensor(out, in0, s, in1, op0, op1)

    def _scan(self, e, out, d0, d1, init, op0, op1):
        return e.tensor_tensor_scan(out, d0, d1, init, op0, op1)

    def _copy(self, e, out, in_):
        if hasattr(e, "tensor_copy"):
            return e.tensor_copy(out, in_)
        return e.copy(out, in_)

    def _memset(self, e, ap, val):
        return e.memset(ap, val)

    def _act(self, e, out, in_, func, bias=0.0, scale=1.0, accum_out=None):
        return e.activation(out, in_, func, bias=bias, scale=scale,
                            accum_out=accum_out)

    def _reduce(self, e, out, in_, alu):
        return e.tensor_reduce(out, in_, AX.X, alu)

    def _recip(self, e, out, in_):
        return e.reciprocal(out, in_)


def build(stages=(0, 1, 2, 3), n_batches=NB, dbg=None):
    import os as _os
    ABLIM = float(_os.environ.get("ABLIM", "6"))
    SKIP_AA_TAIL = (dbg == "aa2")
    nc = bass.Bass("TRN2", target_bir_lowering=False, detect_race_conditions=False)

    xT = nc.dram_tensor("xT", [NB, D, T], F32, kind="ExternalInput").ap()
    outT = nc.dram_tensor("outT", [NB, D, T], F32, kind="ExternalOutput").ap()
    dbgT = nc.dram_tensor("dbgT", [4, D, T], F32, kind="ExternalOutput").ap()

    def dump(idx, src_ap_fn, nt=CT):
        snap2 = P.snapshot()
        for ct in range(nt):
            sap = src_ap_fn(ct)
            p, w = sap.shape
            P.op("gpsimd", "dma",
                 out=AP(dbgT.tensor, idx * D * T + ct * 128 * T, [[T, p], [1, w]]),
                 in_=sap, wait=[(n, v) for n, v in snap2.items()],
                 inc=("dOg", 16))
    Wa = nc.dram_tensor("Wa", [D, 3 * D], F32, kind="ExternalInput").ap()
    ba_d = nc.dram_tensor("ba", [3 * D], F32, kind="ExternalInput").ap()
    WqT = nc.dram_tensor("WqT", [D, D], F32, kind="ExternalInput").ap()
    WkT = nc.dram_tensor("WkT", [D, D], F32, kind="ExternalInput").ap()
    WvT = nc.dram_tensor("WvT", [D, D], F32, kind="ExternalInput").ap()
    WoT = nc.dram_tensor("WoT", [D, D], F32, kind="ExternalInput").ap()
    WpkT = nc.dram_tensor("WpkT", [D, D], F32, kind="ExternalInput").ap()
    WpqT = nc.dram_tensor("WpqT", [D, D], F32, kind="ExternalInput").ap()
    relT = nc.dram_tensor("relT", [D, S], F32, kind="ExternalInput").ap()
    relrevT = nc.dram_tensor("relrevT", [D, S], F32, kind="ExternalInput").ap()
    bq_d = nc.dram_tensor("bq", [D], F32, kind="ExternalInput").ap()
    bk_d = nc.dram_tensor("bk", [D], F32, kind="ExternalInput").ap()
    bv_d = nc.dram_tensor("bv", [D], F32, kind="ExternalInput").ap()
    bo_d = nc.dram_tensor("bo", [D], F32, kind="ExternalInput").ap()
    bpk_d = nc.dram_tensor("bpk", [D], F32, kind="ExternalInput").ap()
    bpq_d = nc.dram_tensor("bpq", [D], F32, kind="ExternalInput").ap()
    gwT = nc.dram_tensor("gwT", [D, E], F32, kind="ExternalInput").ap()
    WeT = nc.dram_tensor("WeT", [E, D, D], F32, kind="ExternalInput").ap()
    I128c = nc.dram_tensor("I128c", [128, 128], F32, kind="ExternalInput").ap()
    SEL8c = nc.dram_tensor("SEL8c", [8, E * 128], F32, kind="ExternalInput").ap()
    ONESF = nc.dram_tensor("ONESF", [128, 384], F32, kind="ExternalInput").ap()

    cpad_d = nc.dram_tensor("cpad_d", [2, 2, TT, 128, PADW], BF16, kind="Internal")
    pk_d = nc.dram_tensor("pk_d", [HP, 128, S], BF16, kind="Internal")
    pqr_d = nc.dram_tensor("pqr_d", [HP, 128, S], BF16, kind="Internal")
    ctxn_d = nc.dram_tensor("ctxn_d", [HP, 128, T], BF16, kind="Internal")

    P = Prog(nc)

    ARENA = 206208
    _arena = nc.alloc_sbuf_tensor("manual_arena", [128, ARENA], mybir.dt.uint8)
    SB_BASE = nc.lookup_mloc(_arena).addr

    def at(name, shape, dtype, off):
        esz = 2 if dtype == BF16 else 4
        per = int(np.prod(shape[1:])) * esz
        assert off + per <= ARENA, (name, off + per)
        return nc.alloc_sbuf_tensor_at(name, shape, dtype, offset=off + SB_BASE)

    I128 = at("I128", [128, 128], BF16, 0)
    ones_b = at("ones_b", [128, 128], BF16, 256)
    onesf = at("onesf", [128, 384], F32, 512)
    sel8 = at("sel8", [8, E * 128], BF16, 2048)
    ba_sb = at("ba_sb", [128, 24], F32, 4096)
    bq_sb = at("bq_sb", [128, 8], F32, 4192)
    bk_sb = at("bk_sb", [128, 8], F32, 4224)
    bo_sb = at("bo_sb", [128, 8], F32, 4256)
    bpk_sb = at("bpk_sb", [128, 8], F32, 4288)
    bpq_sb = at("bpq_sb", [128, 8], F32, 4320)
    bv_sb = at("bv_sb", [128, 8], BF16, 4352)
    gw_sb = at("gw_sb", [128, CT * E], BF16, 4384)
    ec2 = at("ec2", [128, 64], F32, 4544)
    rs_sb = at("rs_sb", [128, 512], F32, 4800)
    edlhsT = at("edlhsT", [128, 97], BF16, 6848)
    ban_sb = at("ban_sb", [128, 8], F32, 7072)
    lnm = at("lnm", [1, 512], F32, 7104)
    lnr = at("lnr", [1, 512], F32, 9152)
    lnt = at("lnt", [1, 512], F32, 11200)
    lnt2 = at("lnt2", [1, 512], F32, 13248)
    xb_buf = at("xb_buf", [128, CT * 512], BF16, 15296)
    sq_buf = at("sq_buf", [128, CT * 512], BF16, 23488)
    t1_buf = at("t1_buf", [128, 512], F32, 31680)
    XOFF = 33728
    x_sb = at("x_sb", [128, CT * 2048], F32, XOFF)
    xn_sb = at("xn_sb", [128, CT * 2048], BF16, XOFF + 65536)
    PH = XOFF + 65536 + 32768
    LIMIT = 212863

    wa_sb = at("wa_sb", [128, 6 * 1024], BF16, PH)
    f_buf = at("f_buf", [128, 2048], F32, PH + 12288)
    g_buf = at("g_buf", [128, 2048], F32, PH + 20480)
    og_buf = at("og_buf", [128, 2048], F32, PH + 28672)
    h_buf = at("h_buf", [128, 2048], F32, PH + 36864)
    ig_t = at("ig_t", [128, 512], F32, PH + 45056)
    in_t = at("in_t", [128, 512], F32, PH + 47104)
    xm_sb = at("xm_sb", [128, 2 * CT * 512], BF16, PH)
    we_sb = at("we_sb", [128, 2 * CT * 1024], BF16, PH + 16384)
    mba_sb = at("mba_sb", [128, E * 512], BF16, PH + 49152)
    mrow_sb = at("mrow_sb", [8, T], BF16, PH + 57344)
    msk_sb = at("msk_sb", [128, 8], BF16, PH + 61440)
    mx_t = at("mx_t", [128, 1], F32, PH + 61472)
    gs_t = at("gs_t", [128, 8], F32, PH + 61504)
    mskf_t = at("mskf_t", [128, 8], F32, PH + 61536)
    mxr_t = at("mxr_t", [128, 8], F32, PH + 71936)
    gsall = at("gsall", [128, 128], F32, PH + 72000)
    gscr = at("gscr", [128, 128], F32, PH + 72512)
    mtree = at("mtree", [128, 112], F32, PH + 73024)
    mskall = at("mskall", [128, 128], BF16, PH + 73536)
    relb_sb = at("relb_sb", [128, CT * 1024], BF16, PH + 16384)
    pre_sg = at("pre_sg", [128, 2 * 512], BF16, PH + 36864)
    qb_sb = at("qb_sb", [128, 2048], BF16, PH)
    kb_sb = at("kb_sb", [128, 2048], BF16, PH + 4096)
    pk_sb = at("pk_sb", [128, 1024], BF16, PH + 8192)
    pqr_sb = at("pqr_sb", [128, 1024], BF16, PH + 10240)
    v_sb = at("v_sb", [128, TT * 194], BF16, PH + 12288)
    cpad_sb = at("cpad_sb", [128, 4 * PADW], BF16, PH + 18496)
    band_sb = at("band_sb", [128, 4 * 1152], BF16, PH + 29760)
    pt_sb = at("pt_sb", [128, 4 * 512], BF16, PH + 38976)
    ctmp_sb = at("ctmp_sb", [128, 512], F32, PH + 43072)
    ctxn_sb = at("ctxn_sb", [128, 2048], BF16, PH + 45120)
    cstr_sb = at("cstr_sb", [128, HP * 512], BF16, PH + 49216)
    edge_sb = at("edge_sb", [128, 2048], BF16, PH + 57408)
    wq_sb = at("wq_sb", [128, 3 * 1024], BF16, PH + 61568)
    wo_sb = at("wo_sb", [128, 2 * 1024], BF16, PH + 67712)

    pbT = [nc.alloc_psum_tensor(f"pb{i}", [128, 512], F32) for i in range(8)]
    pb = [t.ap() for t in pbT]

    def xtile(ct, lo=0, hi=T):
        return x_sb.ap()[:, ct * T + lo:ct * T + hi]

    def xntile(ct, lo=0, hi=T):
        return xn_sb.ap()[:, ct * T + lo:ct * T + hi]

    st = {}

    P.op("gpsimd", "dma", out=I128.ap(), in_=I128c, inc=("dCg", 16))
    P.op("gpsimd", "dma", out=sel8.ap(), in_=SEL8c, inc=("dCg", 16))
    P.op("sync", "dma", out=onesf.ap(), in_=ONESF, inc=("dC", 16))
    P.op("gpsimd", "memset", ap=ones_b.ap(), val=1.0, inc=("g", 1))
    _bias_specs = ((ba_d, ba_sb, 24, None), (bq_d, bq_sb, 8, SCALE),
                   (bk_d, bk_sb, 8, None), (bpk_d, bpk_sb, 8, None),
                   (bpq_d, bpq_sb, 8, SCALE), (bo_d, bo_sb, 8, None))
    for (src, dst, n, scl) in _bias_specs:
        P.op("sync", "dma", out=dst.ap()[:, 0:n],
             in_=AP(src.tensor, 0, [[1, 128], [128, n]]), slow=True,
             inc=("dC", 16))
    _dC_all = P.cur("dC")
    for (src, dst, n, scl) in _bias_specs:
        if scl is not None:
            P.op("vector", "ts", out=dst.ap()[:, 0:n], in0=dst.ap()[:, 0:n],
                 s1=scl, wait=[("dC", _dC_all)], inc=("v", 1))
    P.op("gpsimd", "dma", out=bv_sb.ap(),
         in_=AP(bv_d.tensor, 0, [[1, 128], [128, 8]]), slow=True, inc=("dCg", 16))
    P.op("gpsimd", "dma", out=gw_sb.ap().rearrange("p (a b) -> p a b", a=CT),
         in_=AP(gwT.tensor, 0, [[8, 128], [1024, CT], [1, 8]]), slow=True,
         inc=("dCg", 16))
    st["setup_dC"] = P.cur("dC")
    st["setup_dCg"] = P.cur("dCg")
    P.op("vector", "ts", out=ban_sb.ap(), in0=ba_sb.ap()[:, 8:16], s1=-1.0,
         wait=[("dC", st["setup_dC"])], inc=("v", 1))
    st["setup_g"] = P.cur("g")
    st["setup_v"] = P.cur("v")
    P.barrier()

    if 2 in stages:
        relb = relb_sb.ap()
        for mi, (wT, rel_src, dst_d, bias_sb, scl) in enumerate((
                (WpkT, relT, pk_d, bpk_sb, 1.0),
                (WpqT, relrevT, pqr_d, bpq_sb, SCALE))):
            rtg = P.op("gpsimd", "dma",
                       out=relb.rearrange("p (a b) -> p a b", a=CT),
                       in_=AP(rel_src.tensor, 0, [[S, 128], [128 * S, CT], [1, S]]),
                       wait=[("t", P.cur("t"))], inc=("dRel", 16))
            for ft in range(CT):
                ws = (ft % 2) * 1024
                wslot = wq_sb.ap()[:, ws:ws + 1024]
                wtg = P.op("gpsimd", "dma",
                           out=wslot.rearrange("p (a b) -> p a b", a=CT),
                           in_=AP(wT.tensor, ft * 128,
                                  [[D, 128], [128 * D, CT], [1, 128]]),
                           wait=[("t", st.get(f"pre{mi}w{ft % 2}", 0))],
                           inc=(f"dWp{ft % 2}", 16))
                for sc2 in range(2):
                    slot = (ft * 2 + sc2) % 2
                    bank = pb[slot]
                    for ct in range(CT):
                        P.op("tensor", "mm", out=bank,
                             lhsT=wslot[:, ct * 128:ct * 128 + 128],
                             rhs=relb[:, ct * 1024 + sc2 * 512:ct * 1024 + sc2 * 512 + 512],
                             start=(ct == 0), stop=(ct == CT - 1),
                             wait=([(f"dWp{ft % 2}", P.cur(f"dWp{ft % 2}")),
                                    ("dRel", P.cur("dRel")),
                                    ("v", st.get(f"preb{slot}", 0)),
                                    ("dC", st["setup_dC"]),
                                    ("dCg", st["setup_dCg"]),
                                    ("v", st["setup_v"])] if ct == 0 else []),
                             inc=(("t", 1) if ct == CT - 1 else None))
                    mmt = P.cur("t")
                    ev = P.op("vector", "stt",
                              out=pre_sg.ap()[:, slot * 512:slot * 512 + 512],
                              in0=bank, s=scl,
                              in1=bias_sb.ap()[:, ft:ft + 1].broadcast_to([128, 512]),
                              op0=ALU.mult, op1=ALU.add,
                              wait=[("t", mmt), ("dP", st.get(f"prew{slot}", 0))],
                              inc=("v", 1))
                    st[f"preb{slot}"] = ev
                    st[f"prew{slot}"] = P.op(
                        "sync", "dma",
                        out=AP(dst_d, ft * 128 * S + sc2 * 512, [[S, 128], [1, 512]]),
                        in_=pre_sg.ap()[:, slot * 512:slot * 512 + 512],
                        wait=[("v", ev)], inc=("dP", 16))
                st[f"pre{mi}w{ft % 2}"] = P.cur("t")
        st["pre_dP"] = P.cur("dP")
        P.barrier()

    def layer_norm():
        for qc in range(QCN):
            lo = qc * 512
            ctag = None
            for ct in range(CT):
                ctag = P.op("vector", "copy",
                            out=xb_buf.ap()[:, ct * 512:(ct + 1) * 512],
                            in_=xtile(ct, lo, lo + 512),
                            wait=([("t", st.get("ln_sums", 0))] if ct == 0 else []),
                            inc=("v", 1))
            sqtag = None
            for ct in range(CT):
                sqtag = P.op("scalar", "act",
                             out=sq_buf.ap()[:, ct * 512:(ct + 1) * 512],
                             in_=xtile(ct, lo, lo + 512), func=AF.Square,
                             wait=([("t", st.get("ln_sums", 0))] if ct == 0 else []),
                             inc=("sc", 1))
            for ct in range(CT):
                P.op("tensor", "mm", out=pb[0][0:1, :], lhsT=ones_b.ap()[:, 0:1],
                     rhs=xb_buf.ap()[:, ct * 512:(ct + 1) * 512],
                     start=(ct == 0), stop=(ct == CT - 1),
                     wait=([("v", ctag), ("v", st.get("ln_statv", 0))]
                           if ct == 0 else []),
                     inc=(("t", 1) if ct == CT - 1 else None))
            for ct in range(CT):
                P.op("tensor", "mm", out=pb[1][0:1, :], lhsT=ones_b.ap()[:, 0:1],
                     rhs=sq_buf.ap()[:, ct * 512:(ct + 1) * 512],
                     start=(ct == 0), stop=(ct == CT - 1),
                     wait=([("sc", sqtag)] if ct == 0 else []),
                     inc=(("t", 1) if ct == CT - 1 else None))
            st["ln_sums"] = P.cur("t")
            P.op("vector", "ts", out=lnm.ap(), in0=pb[0][0:1, :], s1=1.0 / D,
                 wait=[("t", st["ln_sums"])], inc=("v", 1))
            P.op("vector", "tt", out=lnt.ap(), a=lnm.ap(), b=lnm.ap(),
                 alu=ALU.mult, inc=("v", 1))
            vv = P.op("vector", "stt", out=lnt2.ap(), in0=pb[1][0:1, :], s=1.0 / D,
                      in1=lnt.ap(), op0=ALU.mult, op1=ALU.subtract, inc=("v", 1))
            vv = P.op("vector", "ts", out=lnt2.ap(), in0=lnt2.ap(), s1=EPS,
                      op0=ALU.add, inc=("v", 1))
            sd = P.op("scalar", "act", out=lnt.ap(), in_=lnt2.ap(), func=AF.Sqrt,
                      bias=0.0, wait=[("v", vv)], inc=("sc", 1))
            rr = P.op("vector", "recip", out=lnr.ap(), in_=lnt.ap(),
                      wait=[("sc", sd)], inc=("v", 1))
            mr = P.op("vector", "tt", out=lnt2.ap(), a=lnm.ap(), b=lnr.ap(),
                      alu=ALU.mult, inc=("v", 1))
            st["ln_statv"] = mr
            r1 = P.op("tensor", "mm", out=pb[2], lhsT=onesf.ap()[0:1, 256:384],
                      rhs=lnr.ap(), start=True, stop=True,
                      wait=[("v", mr), ("v", st.get("ln_normv", 0))], inc=("t", 1))
            r2 = P.op("tensor", "mm", out=pb[3], lhsT=onesf.ap()[0:1, 256:384],
                      rhs=lnt2.ap(), start=True, stop=True, inc=("t", 1))
            vn = None
            for ct in range(CT):
                P.op("vector", "tt", out=t1_buf.ap(), a=xtile(ct, lo, lo + 512),
                     b=pb[2], alu=ALU.mult, wait=[("t", r2)], inc=("v", 1))
                vn = P.op("vector", "tt", out=xntile(ct, lo, lo + 512),
                          a=t1_buf.ap(), b=pb[3], alu=ALU.subtract, inc=("v", 1))
            st["ln_normv"] = vn
        st["ln_done_v"] = P.cur("v")
        P.barrier()

    def attn_a():
        wtags = {}

        def load_w(g):
            for fam in range(3):
                slot = fam * 2 + (g % 2)
                wtags[(g, fam)] = P.op(
                    "gpsimd", "dma",
                    out=wa_sb.ap()[:, slot * 1024:(slot + 1) * 1024]
                    .rearrange("p (a b) -> p a b", a=CT),
                    in_=AP(Wa.tensor, fam * 1024 + g * 128,
                           [[3 * D, 128], [128 * 3 * D, CT], [1, 128]]),
                    wait=[("t", st.get(f"aa_mm_g{g - 2}", 0))],
                    inc=(f"dWa{g % 2}", 16))

        load_w(0)
        load_w(1)
        for g in range(CT):
            for tc in range(QCN):
                bs = 0 if (g * QCN + tc) % 2 == 0 else 3
                for fam in range(3):
                    slot = fam * 2 + (g % 2)
                    for ct in range(CT):
                        P.op("tensor", "mm", out=pb[bs + fam],
                             lhsT=wa_sb.ap()[:, slot * 1024 + ct * 128:
                                             slot * 1024 + ct * 128 + 128],
                             rhs=xntile(ct, tc * 512, tc * 512 + 512),
                             start=(ct == 0), stop=(ct == CT - 1),
                             wait=([(f"dWa{g % 2}", P.cur(f"dWa{g % 2}")),
                                    ("sc", st.get(f"aa_ev{bs}", 0)),
                                    ("v", st.get(f"aa_gv{bs}", 0))]
                                   if ct == 0 else []),
                             inc=(("t", 1) if ct == CT - 1 else None))
                mmt = P.cur("t")
                st[f"aa_mm_g{g}"] = mmt
                lo = tc * 512
                w0 = [("t", mmt), ("v", st.get("aa_gv_last", 0))]
                if tc == 0:
                    w0.append(("v", st.get("aa_gdone", 0)))
                P.op("scalar", "act", out=in_t.ap(), in_=pb[bs + 0], func=AF.Tanh,
                     bias=ba_sb.ap()[:, g:g + 1], wait=w0, inc=("sc", 1))
                P.op("scalar", "act", out=ig_t.ap(), in_=pb[bs + 1], func=AF.Sigmoid,
                     bias=ba_sb.ap()[:, 8 + g:8 + g + 1], inc=("sc", 1))
                P.op("scalar", "act", out=f_buf.ap()[:, lo:lo + 512], in_=pb[bs + 1],
                     func=AF.Sigmoid, scale=-1.0, bias=ban_sb.ap()[:, g:g + 1],
                     inc=("sc", 1))
                ev3 = P.op("scalar", "act", out=og_buf.ap()[:, lo:lo + 512],
                           in_=pb[bs + 2], func=AF.Sigmoid,
                           bias=ba_sb.ap()[:, 16 + g:16 + g + 1], inc=("sc", 1))
                st[f"aa_ev{bs}"] = ev3
                gv = P.op("vector", "tt", out=g_buf.ap()[:, lo:lo + 512],
                          a=in_t.ap(), b=ig_t.ap(), alu=ALU.mult,
                          wait=[("sc", ev3)], inc=("v", 1))
                st[f"aa_gv{bs}"] = gv
                st["aa_gv_last"] = gv
            if g + 2 < CT:
                load_w(g + 2)
            scv = P.op("vector", "scan", out=h_buf.ap(), d0=f_buf.ap(),
                       d1=g_buf.ap(), init=0.0, op0=ALU.mult, op1=ALU.add,
                       wait=[("sc", st.get("aa_thp", 0))], inc=("v", 1))
            if SKIP_AA_TAIL:
                continue
            th = P.op("scalar", "act", out=f_buf.ap(), in_=h_buf.ap(),
                      func=AF.Tanh, wait=[("v", scv)], inc=("sc", 1))
            st["aa_thp"] = th
            P.op("vector", "tt", out=og_buf.ap(), a=f_buf.ap(), b=og_buf.ap(),
                 alu=ALU.mult, wait=[("sc", th)], inc=("v", 1))
            st["aa_gdone"] = P.op("vector", "tt", out=xtile(g), a=xtile(g),
                                  b=og_buf.ap(), alu=ALU.add, inc=("v", 1))
        P.barrier()

    wecnt = [0]

    def ffn():
        for tt in range(TT):
            bank = pb[6] if tt % 2 == 0 else pb[7]
            for ct in range(CT):
                P.op("tensor", "mm", out=bank[:, 0:8],
                     lhsT=xntile(ct, tt * 128, tt * 128 + 128),
                     rhs=gw_sb.ap()[:, ct * E:(ct + 1) * E],
                     start=(ct == 0), stop=(ct == CT - 1),
                     wait=([("v", st.get(f"ffn_gv{tt % 2}", 0)),
                            ("dCg", st["setup_dCg"])] if ct == 0 else []),
                     inc=(("t", 1) if ct == CT - 1 else None))
            gt = P.cur("t")
            gv = P.op("vector", "copy", out=gsall.ap()[:, tt * 8:(tt + 1) * 8],
                      in_=bank[:, 0:8], wait=[("t", gt)], inc=("v", 1))
            st[f"ffn_gv{tt % 2}"] = gv
        ga = gsall
        mt = mtree
        P.op("vector", "copy", out=gscr.ap(), in_=gsall.ap(), inc=("v", 1))
        P.op("vector", "tt",
             out=mtree.ap()[:, 0:64].rearrange("p (a b) -> p a b", a=16),
             a=AP(ga, 0, [[128, 128], [8, 16], [1, 4]]),
             b=AP(ga, 4, [[128, 128], [8, 16], [1, 4]]),
             alu=ALU.max, inc=("v", 1))
        P.op("vector", "copy", out=gscr.ap(), in_=gsall.ap(), inc=("v", 1))
        P.op("vector", "tt",
             out=mtree.ap()[:, 64:96].rearrange("p (a b) -> p a b", a=16),
             a=AP(mt, 0, [[112, 128], [4, 16], [1, 2]]),
             b=AP(mt, 2, [[112, 128], [4, 16], [1, 2]]),
             alu=ALU.max, inc=("v", 1))
        P.op("vector", "copy", out=gscr.ap(), in_=gsall.ap(), inc=("v", 1))
        P.op("vector", "tt", out=mtree.ap()[:, 96:112],
             a=AP(mt, 64, [[112, 128], [2, 16], [1, 1]]),
             b=AP(mt, 65, [[112, 128], [2, 16], [1, 1]]),
             alu=ALU.max, inc=("v", 1))
        P.op("vector", "copy", out=gscr.ap(), in_=gsall.ap(), inc=("v", 1))
        P.op("vector", "copy",
             out=gscr.ap().rearrange("p (a b) -> p a b", a=16),
             in_=AP(mt, 96, [[112, 128], [1, 16], [0, 8]]), inc=("v", 1))
        P.op("vector", "copy", out=mtree.ap()[:, 0:64], in_=mtree.ap()[:, 0:64],
             inc=("v", 1))
        P.op("vector", "tt", out=gsall.ap(), a=gsall.ap(), b=gscr.ap(),
             alu=ALU.is_ge, inc=("v", 1))
        P.op("vector", "copy", out=gscr.ap()[:, 0:64], in_=mtree.ap()[:, 0:64],
             inc=("v", 1))
        mv = P.op("vector", "copy", out=mskall.ap(), in_=gsall.ap(), inc=("v", 1))
        for tt in range(TT):
            bank2 = pb[4] if tt % 2 == 0 else pb[5]
            tr = P.op("tensor", "mm", out=bank2[0:8, 0:128],
                      lhsT=mskall.ap()[:, tt * 8:(tt + 1) * 8],
                      rhs=I128.ap(), start=True, stop=True,
                      wait=[("v", mv), ("sc", st.get(f"ffn_mev{tt % 2}", 0)),
                            ("dCg", st["setup_dCg"])],
                      inc=("t", 1))
            st[f"ffn_mev{tt % 2}"] = P.op(
                "scalar", "copy", out=mrow_sb.ap()[:, tt * 128:(tt + 1) * 128],
                in_=bank2[0:8, 0:128], wait=[("t", tr)], inc=("sc", 1))
        mrow_done = P.cur("sc")

        for tc in range(QCN):
            lo = tc * 512
            for e in range(E):
                bank = pb[6] if e % 2 == 0 else pb[7]
                mmt = P.op("tensor", "mm", out=bank,
                           lhsT=sel8.ap()[:, e * 128:(e + 1) * 128],
                           rhs=mrow_sb.ap()[:, lo:lo + 512], start=True, stop=True,
                           wait=[("sc", mrow_done),
                                 ("sc", st.get(f"ffn_mbev{e % 2}", 0)),
                                 ("v", st.get("ffn_evacv", 0))],
                           inc=("t", 1))
                st[f"ffn_mbev{e % 2}"] = P.op(
                    "scalar", "copy", out=mba_sb.ap()[:, e * 512:(e + 1) * 512],
                    in_=bank, wait=[("t", mmt)], inc=("sc", 1))
            mb_done = P.cur("sc")
            for e in range(E):
                xo = (e % 2) * CT * 512
                xmv = None
                for ct in range(CT):
                    xmv = P.op("vector", "tt",
                               out=xm_sb.ap()[:, xo + ct * 512:xo + (ct + 1) * 512],
                               a=xntile(ct, lo, lo + 512),
                               b=mba_sb.ap()[:, e * 512:(e + 1) * 512],
                               alu=ALU.mult,
                               wait=([("sc", mb_done),
                                      ("t", st.get(f"ffn_xm{e % 2}", 0))]
                                     if ct == 0 else []),
                               inc=("v", 1))
                wet = P.op("gpsimd", "dma",
                           out=we_sb.ap()[:, (e % 2) * CT * 1024:
                                          ((e % 2) + 1) * CT * 1024]
                           .rearrange("p (a b) -> p a b", a=CT),
                           in_=AP(WeT.tensor, e * D * D,
                                  [[D, 128], [128 * D, CT], [1, D]]),
                           wait=[("t", st.get(f"ffn_xm{e % 2}", 0))],
                           inc=(f"dWe{e % 2}", 16))
                for ft in range(CT):
                    for ct in range(CT):
                        w = []
                        if ct == 0 and ft == 0:
                            w = [("v", xmv), (f"dWe{e % 2}", P.cur(f"dWe{e % 2}"))]
                            if e == 0:
                                w.append(("v", st.get("ffn_evacv", 0)))
                        elif ct == 0 and e == 0:
                            w = [("v", st.get("ffn_evacv", 0))]
                        P.op("tensor", "mm", out=pb[ft],
                             lhsT=we_sb.ap()[:, (e % 2) * CT * 1024 + ct * 1024 +
                                             ft * 128:(e % 2) * CT * 1024 +
                                             ct * 1024 + ft * 128 + 128],
                             rhs=xm_sb.ap()[:, xo + ct * 512:xo + (ct + 1) * 512],
                             start=(e == 0 and ct == 0),
                             stop=(e == E - 1 and ct == CT - 1),
                             wait=w,
                             inc=(("t", 1) if ct == CT - 1 else None))
                st[f"ffn_xm{e % 2}"] = P.cur("t")
            expt = P.cur("t")
            for ft in range(CT):
                P.op("vector", "stt", out=t1_buf.ap(), in0=pb[ft], s=0.0,
                     in1=xntile(ft, lo, lo + 512), op0=ALU.add, op1=ALU.add,
                     wait=[("t", expt)], inc=("v", 1))
                P.op("vector", "tt", out=xtile(ft, lo, lo + 512),
                     a=xtile(ft, lo, lo + 512), b=t1_buf.ap(), alu=ALU.add,
                     inc=("v", 1))
            st["ffn_evacv"] = P.cur("v")
        P.barrier()

    def win(kt):
        k0 = kt * 128
        return max(0, k0 - 512), min(T, k0 + 640)

    def attn_b(b):
        for hp in range(HP):
            attn_b_hp(b, hp)
        P.barrier()
        if ABLIM <= 5:
            return
        for tc in range(QCN):
            lo = tc * 512
            cs = 0
            ctag = None
            for hp in range(HP):
                ctag = P.op("sync", "dma",
                            out=cstr_sb.ap()[:, cs + hp * 512:cs + (hp + 1) * 512],
                            in_=AP(ctxn_d, hp * 128 * T + lo, [[T, 128], [1, 512]]),
                            wait=([("v", st.get("wo_used_last", 0)),
                                   ("dCXo", P.cur("dCXo"))] if hp == 0 else []),
                            inc=(f"dCX{tc % 2}", 16))
            for ft in range(CT):
                wslot = (ft % 2) * 1024
                ii = tc * CT + ft
                st[f"wo_w{ft}"] = P.op(
                    "gpsimd", "dma",
                    out=wo_sb.ap()[:, wslot:wslot + 1024]
                    .rearrange("p (a b) -> p a b", a=CT),
                    in_=AP(WoT.tensor, ft * 128,
                           [[D, 128], [128 * D, CT], [1, 128]]),
                    wait=[("t", st.get(f"wo_mmi{ii - 2}", 0))],
                    inc=(f"dWo{ft % 2}", 16))
                bank = pb[6] if ft % 2 == 0 else pb[7]
                for hp in range(HP):
                    P.op("tensor", "mm", out=bank,
                         lhsT=wo_sb.ap()[:, wslot + hp * 128:wslot + (hp + 1) * 128],
                         rhs=cstr_sb.ap()[:, cs + hp * 512:cs + (hp + 1) * 512],
                         start=(hp == 0), stop=(hp == HP - 1),
                         wait=([(f"dWo{ft % 2}", P.cur(f"dWo{ft % 2}")),
                                (f"dCX{tc % 2}", P.cur(f"dCX{tc % 2}")),
                                ("v", st.get(f"wo_ev{ft % 2}", 0))]
                               if hp == 0 else []),
                         inc=(("t", 1) if hp == HP - 1 else None))
                mmt = P.cur("t")
                st[f"wo_mm{ft}"] = mmt
                st[f"wo_mmi{ii}"] = mmt
                if b == 0 and tc == 0:
                    for hp2 in range(HP):
                        P.op("tensor", "mm", out=pb[5][:, 0:1],
                             lhsT=wo_sb.ap()[:, wslot + hp2 * 128:
                                             wslot + (hp2 + 1) * 128],
                             rhs=bv_sb.ap()[:, hp2:hp2 + 1],
                             start=(hp2 == 0), stop=(hp2 == HP - 1),
                             wait=([("v", st.get("bo_v", 0)),
                                    ("dCg", st["setup_dCg"])] if hp2 == 0 else []),
                             inc=(("t", 1) if hp2 == HP - 1 else None))
                    bot = P.cur("t")
                    st["bo_v"] = P.op("vector", "tt", out=bo_sb.ap()[:, ft:ft + 1],
                                      a=bo_sb.ap()[:, ft:ft + 1], b=pb[5][:, 0:1],
                                      alu=ALU.add, wait=[("t", bot)], inc=("v", 1))
                    st[f"bo_rdy{ft}"] = st["bo_v"]
                P.op("vector", "tt", out=t1_buf.ap(), a=bank,
                     b=bo_sb.ap()[:, ft:ft + 1].broadcast_to([128, 512]),
                     alu=ALU.add,
                     wait=[("t", mmt),
                           (("v", st.get(f"bo_rdy{ft}", 0)) if b == 0 else None)],
                     inc=("v", 1))
                ev = P.op("vector", "tt", out=xtile(ft, lo, lo + 512),
                          a=xtile(ft, lo, lo + 512), b=t1_buf.ap(), alu=ALU.add,
                          inc=("v", 1))
                st[f"wo_ev{ft % 2}"] = ev
            st["wo_used_last"] = P.cur("v")
        P.barrier()

    def attn_b_hp(b, hp):
        if hp == 0:
            P.op("gpsimd", "memset", ap=v_sb.ap(), val=0.0, inc=("g", 1))
            for kt in range(TT):
                b0 = kt * 194
                P.op("gpsimd", "memset", ap=v_sb.ap()[:, b0 + 64:b0 + 65], val=1.0,
                     inc=("g", 1))
                P.op("gpsimd", "memset", ap=v_sb.ap()[:, b0 + 98:b0 + 99], val=1.0,
                     inc=("g", 1))
            st["vconst_g"] = P.cur("g")
        wq_tags = []
        for i, wT in enumerate((WqT, WkT, WvT)):
            wq_tags.append(P.op(
                "gpsimd", "dma",
                out=wq_sb.ap()[:, i * 1024:(i + 1) * 1024]
                .rearrange("p (a b) -> p a b", a=CT),
                in_=AP(wT.tensor, hp * 128, [[D, 128], [128 * D, CT], [1, 128]]),
                wait=[("t", st.get(f"ab_qkv{hp - 1}", 0))], inc=("dWq", 16)))
        pkt = P.op("sync", "dma", out=pk_sb.ap(),
                   in_=AP(pk_d, hp * 128 * S, [[S, 128], [1, S]]),
                   wait=[("t", st.get(f"ab_cc{hp - 1}", 0)),
                         ("v", st.get(f"ab_ed{hp - 1}", 0))], inc=("dPK", 16))
        pqt = P.op("sync", "dma", out=pqr_sb.ap(),
                   in_=AP(pqr_d, hp * 128 * S, [[S, 128], [1, S]]),
                   inc=("dPK", 16))
        for (wi, dst, bias, scl) in ((0, qb_sb, bq_sb, SCALE), (1, kb_sb, bk_sb, 1.0)):
            for tc in range(QCN):
                bank = pb[6] if (wi * QCN + tc) % 2 == 0 else pb[7]
                for ct in range(CT):
                    P.op("tensor", "mm", out=bank,
                         lhsT=wq_sb.ap()[:, wi * 1024 + ct * 128:
                                         wi * 1024 + ct * 128 + 128],
                         rhs=xntile(ct, tc * 512, tc * 512 + 512),
                         start=(ct == 0), stop=(ct == CT - 1),
                         wait=([("dWq", P.cur("dWq")),
                                ("v", st.get(f"ab_pev{(wi * QCN + tc) % 2}", 0)),
                                ("t", st.get(f"ab_qkuse{hp - 1}", 0))]
                               if ct == 0 else []),
                         inc=(("t", 1) if ct == CT - 1 else None))
                mmt = P.cur("t")
                st[f"ab_pev{(wi * QCN + tc) % 2}"] = P.op(
                    "vector", "stt", out=dst.ap()[:, tc * 512:(tc + 1) * 512],
                    in0=bank, s=scl,
                    in1=bias.ap()[:, hp:hp + 1].broadcast_to([128, 512]),
                    op0=ALU.mult, op1=ALU.add,
                    wait=[("t", mmt)], inc=("v", 1))
        for kt in range(TT):
            bank = pb[6] if kt % 2 == 0 else pb[7]
            for ct in range(CT):
                P.op("tensor", "mm", out=bank[:, 0:128],
                     lhsT=xntile(ct, kt * 128, kt * 128 + 128),
                     rhs=wq_sb.ap()[:, 2 * 1024 + ct * 128:2 * 1024 + ct * 128 + 128],
                     start=(ct == 0), stop=(ct == CT - 1),
                     wait=([("dWq", P.cur("dWq")), ("g", st["vconst_g"]),
                            ("sc", st.get(f"ab_pevV{kt % 2}", 0)),
                            ("v", st.get(f"ab_pev{kt % 2}", 0)),
                            ("t", st.get(f"ab_vuse{hp - 1}", 0))]
                           if ct == 0 else []),
                     inc=(("t", 1) if ct == CT - 1 else None))
            mmt = P.cur("t")
            b0 = kt * 194
            P.op("scalar", "copy", out=v_sb.ap()[:, b0:b0 + 64], in_=bank[:, 0:64],
                 wait=[("t", mmt)], inc=("sc", 1))
            st[f"ab_pevV{kt % 2}"] = P.op(
                "scalar", "copy", out=v_sb.ap()[:, b0 + 130:b0 + 194],
                in_=bank[:, 64:128], inc=("sc", 1))
        st[f"ab_qkv{hp}"] = P.cur("t")
        qkv_sc = P.cur("sc")
        if ABLIM <= 1:
            return

        edv = P.op("vector", "memset", ap=edlhsT.ap(), val=0.0,
                   wait=[("dPK", P.cur("dPK")),
                         ("t", st.get(f"ab_eduse{hp - 1}", 0))],
                   inc=("v", 1))
        P.op("vector", "copy", out=edlhsT.ap()[0:64, 0:1], in_=pk_sb.ap()[0:64, 0:1],
             inc=("v", 1))
        P.op("vector", "copy", out=edlhsT.ap()[0:64, 32:33],
             in_=pk_sb.ap()[0:64, 1023:1024], inc=("v", 1))
        P.op("vector", "copy", out=edlhsT.ap()[64:128, 64:65],
             in_=pk_sb.ap()[64:128, 0:1], inc=("v", 1))
        edv = P.op("vector", "copy", out=edlhsT.ap()[64:128, 96:97],
                   in_=pk_sb.ap()[64:128, 1023:1024], inc=("v", 1))
        for qc in range(QCN if ABLIM >= 3 else 0):
            emm = P.op("tensor", "mm", out=pb[4][0:97, :], lhsT=edlhsT.ap(),
                       rhs=qb_sb.ap()[:, qc * 512:(qc + 1) * 512],
                       start=True, stop=True,
                       wait=[("v", edv), ("sc", st.get("ab_edev", 0)),
                             ("sc", qkv_sc)],
                       inc=("t", 1))
            st["ab_edev"] = P.op("scalar", "copy",
                                 out=edge_sb.ap()[0:97, qc * 512:(qc + 1) * 512],
                                 in_=pb[4][0:97, :], wait=[("t", emm)], inc=("sc", 1))
        st[f"ab_ed{hp}"] = P.cur("v")
        ed_sc = P.cur("sc")
        if ABLIM <= 2.2:
            return

        wrtags = {}
        for mi in range(2):
            lb = qb_sb if mi == 0 else kb_sb
            rb = pk_sb if mi == 0 else pqr_sb
            for qt in range(TT):
                slot2 = qt % 2
                for h in range(2):
                    slot = slot2 * 2 + h
                    soff = slot * PADW
                    li = mi * TT + qt
                    for sc2 in range(2):
                        bank = pb[h * 2 + sc2]
                        P.op("tensor", "mm", out=bank,
                             lhsT=lb.ap()[h * 64:h * 64 + 64,
                                          qt * 128:qt * 128 + 128],
                             rhs=rb.ap()[h * 64:h * 64 + 64,
                                         sc2 * 512:sc2 * 512 + 512],
                             start=True, stop=True,
                             wait=([("sc", qkv_sc), ("dPK", P.cur("dPK")),
                                    st.get(f"ab_cbank{h}", None),
                                    ("v", st.get(f"ab_cpv{slot}", 0)),
                                    ("sc", st.get(f"ab_cpev{slot}", 0)),
                                    ("v", st.get("ab_ctxnv", 0)),
                                    (f"dP{slot}", P.cur(f"dP{slot}"))]
                                   if sc2 == 0 else []),
                             inc=("t", 1))
                    mmt = P.cur("t")
                    if ABLIM <= 2.3:
                        continue
                    if mi == 0:
                        P.op("vector", "copy",
                             out=cpad_sb.ap()[:, soff + 256:soff + 256 + 512],
                             in_=pb[h * 2], wait=[("t", mmt)], inc=("v", 1))
                        cpv2 = P.op("vector", "copy",
                                    out=cpad_sb.ap()[:, soff + 768:soff + 768 + 512],
                                    in_=pb[h * 2 + 1], inc=("v", 1))
                        P.op("vector", "copy",
                             out=cpad_sb.ap()[:, soff:soff + 256],
                             in_=cpad_sb.ap()[:, soff + 256:soff + 257]
                             .broadcast_to([128, 256]),
                             wait=[("v", cpv2)], inc=("v", 1))
                        pv = P.op("vector", "copy",
                                  out=cpad_sb.ap()[:, soff + 1280:soff + 1408],
                                  in_=cpad_sb.ap()[:, soff + 1279:soff + 1280]
                                  .broadcast_to([128, 128]), inc=("v", 1))
                        st[f"ab_cpv{slot}"] = pv
                        st[f"ab_cbank{h}"] = ("v", pv)
                        wdep = ("v", pv)
                    else:
                        P.op("scalar", "copy",
                             out=cpad_sb.ap()[:, soff + 256:soff + 256 + 512],
                             in_=pb[h * 2], wait=[("t", mmt)], inc=("sc", 1))
                        P.op("scalar", "copy",
                             out=cpad_sb.ap()[:, soff + 768:soff + 768 + 512],
                             in_=pb[h * 2 + 1], inc=("sc", 1))
                        P.op("scalar", "copy", out=ec2.ap()[:, h * 32 + 2 * qt:
                                                            h * 32 + 2 * qt + 1],
                             in_=pb[h * 2][:, 0:1], inc=("sc", 1))
                        P.op("scalar", "copy", out=ec2.ap()[:, h * 32 + 2 * qt + 1:
                                                            h * 32 + 2 * qt + 2],
                             in_=pb[h * 2 + 1][:, 511:512], inc=("sc", 1))
                        P.op("scalar", "copy",
                             out=cpad_sb.ap()[:, soff:soff + 256],
                             in_=pb[h * 2][:, 0:1].broadcast_to([128, 256]),
                             inc=("sc", 1))
                        pv = P.op("scalar", "copy",
                                  out=cpad_sb.ap()[:, soff + 1280:soff + 1408],
                                  in_=pb[h * 2 + 1][:, 511:512]
                                  .broadcast_to([128, 128]), inc=("sc", 1))
                        st[f"ab_cpev{slot}"] = pv
                        st[f"ab_cbank{h}"] = ("sc", pv)
                        wdep = ("sc", pv)
                    if ABLIM <= 2.4:
                        continue
                    wrtags[(li, h)] = P.op(
                        "sync", "dma",
                        out=AP(cpad_d, ((mi * 2 + h) * TT + qt) * 128 * PADW,
                               [[PADW, 128], [1, PADW]]),
                        in_=cpad_sb.ap()[:, soff:soff + PADW],
                        wait=[wdep], inc=(f"dP{slot}", 16))
        if ABLIM <= 3:
            return
        cc_sc, cc_v = P.cur("sc"), P.cur("v")

        for half in range(2):
            qlo_h, qhi_h = half * 1024, half * 1024 + 1024
            ctx_started = False
            for kt in range(TT):
                A, B = win(kt)
                a, bb = max(A, qlo_h), min(B, qhi_h)
                bslots = {}
                if a < bb:
                    for h in range(2):
                        bslot = h * 2 + (kt % 2)
                        boff = bslot * 1152
                        bslots[h] = (bslot, boff)
                        k0 = kt * 128
                        src = AP(cpad_d,
                                 (0 * 2 + h) * TT * 128 * PADW +
                                 a * (PADW - 1) + k0 + 768,
                                 [[PADW - 1, bb - a], [1, 128]])
                        wdeps = [(f"dP{i}", P.cur(f"dP{i}")) for i in range(4)]
                        wdeps.append(("t", st.get(f"ab_buse{bslot}", 0)))
                        c2pt = P.op("sync", "dma",
                                    out=band_sb.ap()[:, boff:boff + (bb - a)],
                                    in_=src, transpose=True,
                                    wait=wdeps, inc=(f"dB{kt % 2}", 16))
                        src2 = AP(cpad_d,
                                  (1 * 2 + h) * TT * 128 * PADW +
                                  k0 * (PADW - 1) + a + 767,
                                  [[PADW - 1, 128], [1, bb - a]])
                        P.op("gpsimd", "dma",
                             out=band_sb.ap()[:, boff:boff + (bb - a)],
                             in_=src2, accum=True,
                             wait=[(f"dB{kt % 2}", P.cur(f"dB{kt % 2}"))],
                             inc=(f"dB2{kt % 2}", 16))
                for qci in range(2):
                    if ABLIM <= 4:
                        continue
                    qc = half * 2 + qci
                    qlo = qc * 512
                    for h in range(2):
                        sbank = pb[(kt % 2) * 2 + h]
                        adds = []
                        if a < bb:
                            a2, b2 = max(a, qlo), min(bb, qlo + 512)
                            if a2 < b2:
                                boff = bslots[h][1]
                                adds.append(("band", a2, b2, boff))
                        ecut1 = min(max(A, qlo), qlo + 512)
                        ecut2 = min(max(B, qlo), qlo + 512)
                        if qlo < ecut1:
                            adds.append(("edgeR", qlo, ecut1, None))
                        if ecut2 < qlo + 512:
                            adds.append(("edgeL", ecut2, qlo + 512, None))
                        qkw = [("sc", st.get(f"ab_sexp{h * 2 + (kt % 2)}", 0))]
                        if kt < 2:
                            qkw.append(("v", st.get("ab_ctxnv", 0)))
                            qkw.append(("sc", cc_sc))
                            qkw.append(("v", cc_v))
                        P.op("tensor", "mm", out=sbank,
                             lhsT=kb_sb.ap()[h * 64:h * 64 + 64,
                                             kt * 128:kt * 128 + 128],
                             rhs=qb_sb.ap()[h * 64:h * 64 + 64, qlo:qlo + 512],
                             start=True, stop=(len(adds) == 0),
                             wait=qkw, inc=("t", 1))
                        for ai, (knd, ra, rb2, boff) in enumerate(adds):
                            last = (ai == len(adds) - 1)
                            if knd == "band":
                                P.op("tensor", "mm",
                                     out=sbank[:, ra - qlo:rb2 - qlo],
                                     lhsT=I128.ap(),
                                     rhs=band_sb.ap()[:, boff + ra - a:
                                                      boff + rb2 - a],
                                     start=False, stop=last,
                                     wait=[(f"dB2{kt % 2}",
                                            P.cur(f"dB2{kt % 2}")),
                                           ("dCg", st["setup_dCg"])],
                                     inc=("t", 1))
                            else:
                                r = (32 if knd == "edgeR" else 0) + h * 64
                                tp = (96, 0) if r == 96 else None
                                P.op("tensor", "mm",
                                     out=sbank[:, ra - qlo:rb2 - qlo],
                                     lhsT=ones_b.ap()[r:r + 1, 0:128],
                                     rhs=edge_sb.ap()[r:r + 1, ra:rb2],
                                     start=False, stop=last, tp=tp,
                                     wait=[("sc", ed_sc)], inc=("t", 1))
                        smm = P.cur("t")
                        st[f"ab_buse{h * 2 + (kt % 2)}"] = smm
                        pslot = h * 2 + (kt % 2)
                        poff = pslot * 512
                        edeps = [("t", smm), ("t", st.get(f"ab_ptuse{pslot}", 0))]
                        cut1 = min(max(A, qlo), qlo + 512)
                        cut2 = min(max(B, qlo), qlo + 512)
                        segs = []
                        if qlo < cut1:
                            segs.append((qlo, cut1,
                                         ec2.ap()[:, h * 32 + 2 * kt:
                                                  h * 32 + 2 * kt + 1]))
                        if cut1 < cut2:
                            segs.append((cut1, cut2, 0.0))
                        if cut2 < qlo + 512:
                            segs.append((cut2, qlo + 512,
                                         ec2.ap()[:, h * 32 + 2 * kt + 1:
                                                  h * 32 + 2 * kt + 2]))
                        last = None
                        for (ra, rb_, bias) in segs:
                            last = P.op("scalar", "act",
                                        out=pt_sb.ap()[:, poff + ra - qlo:
                                                       poff + rb_ - qlo],
                                        in_=sbank[:, ra - qlo:rb_ - qlo],
                                        func=AF.Exp, bias=bias,
                                        wait=edeps, inc=("sc", 1))
                            edeps = []
                        st[f"ab_sexp{pslot}"] = last
                        cb = pb[4 + qci * 2 + h]
                        vb = v_sb.ap()
                        if h == 0:
                            lhsT = vb[:, kt * 194:kt * 194 + 65]
                        else:
                            lhsT = vb[:, kt * 194 + 66:kt * 194 + 194]
                        ctxw = [("sc", last)]
                        if kt == 0:
                            ctxw.append(("sc", st.get(f"ab_ctmp{qci * 2 + h}", 0)))
                            ctxw.append(("sc", ed_sc))
                        P.op("tensor", "mm", out=cb[0:65 if h == 0 else 128, :],
                             lhsT=lhsT,
                             rhs=pt_sb.ap()[:, poff:poff + 512],
                             start=(kt == 0), stop=(kt == TT - 1),
                             wait=ctxw,
                             inc=("t", 1))
                        st[f"ab_ptuse{pslot}"] = P.cur("t")
            for qci in range(2 if ABLIM > 4 else 0):
                qc = half * 2 + qci
                for h in range(2):
                    cb = pb[4 + qci * 2 + h]
                    rrow = 64 if h == 0 else 32
                    ctxmm = P.cur("t")
                    rv = P.op("scalar", "copy",
                              out=rs_sb.ap()[rrow:rrow + 1, 0:512],
                              in_=cb[rrow:rrow + 1, :], wait=[("t", ctxmm)],
                              inc=("sc", 1))
                    rc = P.op("vector", "recip",
                              out=rs_sb.ap()[rrow:rrow + 1, 0:512],
                              in_=rs_sb.ap()[rrow:rrow + 1, 0:512],
                              wait=[("sc", rv)], inc=("v", 1))
                    lhsT = (onesf.ap()[64:65, 0:128] if h == 0
                            else onesf.ap()[32:33, 128:256])
                    rep = P.op("tensor", "mm", out=pb[h],
                               lhsT=lhsT, rhs=rs_sb.ap()[rrow:rrow + 1, 0:512],
                               start=True, stop=True,
                               wait=[("v", rc),
                                     ("sc", st.get(f"ab_sexp{h * 2}", 0)),
                                     ("sc", st.get(f"ab_sexp{h * 2 + 1}", 0))],
                               inc=("t", 1))
                    cv = P.op("scalar", "copy",
                              out=(ctmp_sb.ap()[0:64, :] if h == 0
                                   else ctmp_sb.ap()[64:128, :]),
                              in_=(cb[0:64, :] if h == 0 else cb[64:128, :]),
                              wait=[("v", st.get("ab_ctxnv", 0))], inc=("sc", 1))
                    st[f"ab_ctmp{qci * 2 + h}"] = cv
                    vv = P.op("vector", "tt",
                              out=ctxn_sb.ap()[0:64 if h == 0 else slice(64, 128),
                                               qc * 512:(qc + 1) * 512]
                              if False else
                              (ctxn_sb.ap()[0:64, qc * 512:(qc + 1) * 512]
                               if h == 0 else
                               ctxn_sb.ap()[64:128, qc * 512:(qc + 1) * 512]),
                              a=(ctmp_sb.ap()[0:64, :] if h == 0
                                 else ctmp_sb.ap()[64:128, :]),
                              b=(pb[h][0:64, :] if h == 0 else pb[h][64:128, :]),
                              alu=ALU.mult,
                              wait=[("t", rep), ("sc", cv),
                                    ("dCXo", st.get(f"ab_ctxout{hp - 1}", 0))],
                              inc=("v", 1))
                    st["ab_ctxnv"] = vv
        st[f"ab_vuse{hp}"] = P.cur("t")
        st[f"ab_eduse{hp}"] = P.cur("t")
        st[f"ab_qkuse{hp}"] = P.cur("t")
        st[f"ab_cc{hp}"] = P.cur("t")
        if ABLIM <= 4:
            return
        P.op("sync", "dma", out=AP(ctxn_d, hp * 128 * T, [[T, 128], [1, T]]),
             in_=ctxn_sb.ap(), wait=[("v", st["ab_ctxnv"])], inc=("dCXo", 16))
        st[f"ab_ctxout{hp}"] = P.cur("dCXo")

    for b in range(n_batches):
        xl = None
        for ct in range(CT):
            xl = P.op("sync", "dma", out=xtile(ct),
                      in_=AP(xT.tensor, b * D * T + ct * 128 * T, [[T, 128], [1, T]]),
                      inc=("dX", 16))
        P.op("vector", "noop", wait=[("dX", xl)])
        P.op("tensor", "noop", wait=[("dX", xl)])
        P.op("scalar", "noop", wait=[("dX", xl)])
        if 0 in stages:
            layer_norm()
            if dbg == "ln" and b == 0:
                dump(0, lambda ct: xntile(ct))
            attn_a()
            if dbg in ("aa", "aa2") and b == 0:
                dump(0, lambda ct: f_buf.ap(), nt=1)
                dump(1, lambda ct: g_buf.ap(), nt=1)
                dump(2, lambda ct: h_buf.ap(), nt=1)
                dump(3, lambda ct: og_buf.ap(), nt=1)
        if 1 in stages:
            layer_norm()
            if dbg == "ffn1pre" and b == 0:
                dump(3, lambda ct: xntile(ct))
            ffn()
            if dbg == "ffn1" and b == 0:
                dump(0, lambda ct: mrow_sb.ap(), nt=1)
                dump(1, lambda ct: mba_sb.ap()[0:128, 0:2048], nt=1)
                dump(2, lambda ct: xm_sb.ap()[:, 0:2048], nt=1)
        if 2 in stages:
            layer_norm()
            attn_b(b)
        if 3 in stages:
            layer_norm()
            ffn()
        snap = P.snapshot()
        for ct in range(CT):
            P.op("sync", "dma",
                 out=AP(outT.tensor, b * D * T + ct * 128 * T, [[T, 128], [1, T]]),
                 in_=xtile(ct), wait=[(n, v) for n, v in snap.items()],
                 inc=("dO", 16))
        P.op("sync", "noop", wait=[("dO", P.cur("dO"))])
    P.op("sync", "noop", wait=[("dO", P.cur("dO"))])

    P.emit()
    return nc


def host_consts():
    I = np.eye(128, dtype=np.float32)
    sel = np.zeros((8, E * 128), np.float32)
    for e in range(E):
        sel[e, e * 128:(e + 1) * 128] = 1.0
    onesf = np.zeros((128, 384), np.float32)
    onesf[:, 0:64] = 1.0
    onesf[:, 128 + 64:256] = 1.0
    onesf[:, 256:384] = 1.0
    return I, sel, onesf


def make_inputs(inp, core):
    x = np.asarray(inp["L0_attn_a_x"])
    xb = np.ascontiguousarray(
        x[core * NB:(core + 1) * NB].transpose(0, 2, 1)).astype(np.float32, copy=False)
    I, sel, onesf = host_consts()
    rel = np.asarray(inp["L0_attn_b_rel_emb"])
    d = {
        "xT": xb,
        "Wa": np.ascontiguousarray(inp["L0_attn_a_W"]),
        "ba": np.ascontiguousarray(inp["L0_attn_a_b"]),
        "WqT": np.ascontiguousarray(np.asarray(inp["L0_attn_b_Wq"]).T),
        "WkT": np.ascontiguousarray(np.asarray(inp["L0_attn_b_Wk"]).T),
        "WvT": np.ascontiguousarray(np.asarray(inp["L0_attn_b_Wv"]).T),
        "WoT": np.ascontiguousarray(np.asarray(inp["L0_attn_b_Wo"]).T),
        "WpkT": np.ascontiguousarray(np.asarray(inp["L0_attn_b_Wpk"]).T),
        "WpqT": np.ascontiguousarray(np.asarray(inp["L0_attn_b_Wpq"]).T),
        "relT": np.ascontiguousarray(rel.T),
        "relrevT": np.ascontiguousarray(rel[::-1].T),
        "bq": np.ascontiguousarray(inp["L0_attn_b_bq"]),
        "bk": np.ascontiguousarray(inp["L0_attn_b_bk"]),
        "bv": np.ascontiguousarray(inp["L0_attn_b_bv"]),
        "bo": np.ascontiguousarray(inp["L0_attn_b_bo"]),
        "bpk": np.ascontiguousarray(inp["L0_attn_b_bpk"]),
        "bpq": np.ascontiguousarray(inp["L0_attn_b_bpq"]),
        "gwT": np.ascontiguousarray(np.asarray(inp["L0_ffn_gate_w"]).T),
        "WeT": np.ascontiguousarray(np.asarray(inp["L0_ffn_expert_w"]).transpose(0, 2, 1)),
        "I128c": I, "SEL8c": sel, "ONESF": onesf,
    }
    return {k: np.asarray(v, dtype=np.float32) for k, v in d.items()}


from concourse.bass_utils import run_bass_kernel_spmd

_NCORES = 8


def kernel(**inputs):
    nc = build(stages=(0, 1, 2, 3), n_batches=NB)
    in_maps = [make_inputs(inputs, c) for c in range(_NCORES)]
    res = run_bass_kernel_spmd(nc, in_maps, core_ids=list(range(_NCORES)))
    out = np.empty((16, T, D), dtype=np.float32)
    for c in range(_NCORES):
        o = np.asarray(res.results[c]["outT"])
        out[c * NB:(c + 1) * NB] = o.transpose(0, 2, 1)
    return out


# version 02
# speedup vs baseline: 1.1877x; 1.1877x over previous
import numpy as np
import concourse.bass as bass
import concourse.mybir as mybir
from concourse.bass import AP

F32 = mybir.dt.float32
BF16 = mybir.dt.bfloat16
AF = mybir.ActivationFunctionType
ALU = mybir.AluOpType
AX = mybir.AxisListType

T = 2048
D = 1024
NB = 2
HP = 8
S = 1024
SCALE = float(1.0 / np.sqrt(np.float32(64 * 3)))
E = 8
EPS = 1e-5
PADW = 1408
CT = D // 128
TT = T // 128
QCN = T // 512


class Prog:
    ENGS = ("sync", "tensor", "vector", "scalar", "gpsimd")

    def __init__(self, nc):
        self.nc = nc
        self.q = {e: [] for e in self.ENGS}
        self.sems = {}
        self.cnt = {}

    def _sem(self, name):
        if name not in self.sems:
            self.sems[name] = self.nc.alloc_semaphore(name)
            self.cnt[name] = 0
        return self.sems[name]

    def op(self, eng, kind, wait=(), inc=None, **kw):
        w = []
        for item in wait:
            if item is None:
                continue
            sn, val = item
            if val is None or val <= 0:
                continue
            self._sem(sn)
            assert val <= self.cnt[sn], f"wait {sn}>={val} but only {self.cnt[sn]}"
            w.append((sn, val))
        tag = None
        if inc is not None:
            sn, k = inc
            self._sem(sn)
            self.cnt[sn] += k
            tag = self.cnt[sn]
        self.q[eng].append((kind, kw, tuple(w), (inc[0], inc[1]) if inc else None))
        return tag

    def cur(self, name):
        self._sem(name)
        return self.cnt[name]

    def snapshot(self):
        return {n: self.cnt[n] for n in ("t", "v", "sc", "g") if n in self.cnt}

    def barrier(self):
        snap = dict(self.cnt)
        for eng in self.ENGS:
            waits = [(n, v) for n, v in snap.items()]
            self.op(eng, "noop", wait=waits)

    def emit(self):
        with self.nc.Block() as block:
            for ename in self.ENGS:
                ops = self.q[ename]
                if ops:
                    getattr(block, ename)(lambda e, ops=ops: self._run(e, ops))

    def _run(self, e, ops):
        for kind, kw, waits, inc in ops:
            for (sn, val) in waits:
                e.wait_ge(self.sems[sn], val)
            if kind == "noop":
                continue
            ins = getattr(self, "_" + kind)(e, **kw)
            if inc is not None:
                sn, k = inc
                ins.then_inc(self.sems[sn], k)

    def _mm(self, e, out, lhsT, rhs, start, stop, tp=None):
        if tp is not None:
            return e.matmul(out, lhsT, rhs, start=start, stop=stop, tile_position=tp)
        return e.matmul(out, lhsT, rhs, start=start, stop=stop)

    def _dma(self, e, out, in_, accum=False, transpose=False, slow=False):
        kw = {}
        if accum:
            kw["accum_op"] = ALU.add
        if transpose:
            kw["transpose"] = True
        if slow:
            kw["allow_slow_non_contiguous"] = True
        return e.dma_start(out=out, in_=in_, **kw)

    def _tt(self, e, out, a, b, alu):
        return e.tensor_tensor(out, a, b, alu)

    def _ts(self, e, out, in0, s1, s2=None, op0=ALU.mult, op1=None):
        if op1 is None:
            return e.tensor_single_scalar(out, in0, s1, op0)
        return e.tensor_scalar(out, in0, s1, s2, op0, op1)

    def _stt(self, e, out, in0, s, in1, op0, op1):
        return e.scalar_tensor_tensor(out, in0, s, in1, op0, op1)

    def _scan(self, e, out, d0, d1, init, op0, op1):
        return e.tensor_tensor_scan(out, d0, d1, init, op0, op1)

    def _copy(self, e, out, in_):
        if hasattr(e, "tensor_copy"):
            return e.tensor_copy(out, in_)
        return e.copy(out, in_)

    def _memset(self, e, ap, val):
        return e.memset(ap, val)

    def _act(self, e, out, in_, func, bias=0.0, scale=1.0, accum_out=None):
        return e.activation(out, in_, func, bias=bias, scale=scale,
                            accum_out=accum_out)

    def _reduce(self, e, out, in_, alu):
        return e.tensor_reduce(out, in_, AX.X, alu)

    def _recip(self, e, out, in_):
        return e.reciprocal(out, in_)


def build(stages=(0, 1, 2, 3), n_batches=NB, dbg=None):
    import os as _os
    ABLIM = float(_os.environ.get("ABLIM", "6"))
    SKIP_AA_TAIL = (dbg == "aa2")
    nc = bass.Bass("TRN2", target_bir_lowering=False, detect_race_conditions=False)

    xT = nc.dram_tensor("xT", [NB, D, T], F32, kind="ExternalInput").ap()
    outT = nc.dram_tensor("outT", [NB, D, T], F32, kind="ExternalOutput").ap()
    dbgT = nc.dram_tensor("dbgT", [4, D, T], F32, kind="ExternalOutput").ap()

    def dump(idx, src_ap_fn, nt=CT):
        snap2 = P.snapshot()
        for ct in range(nt):
            sap = src_ap_fn(ct)
            p, w = sap.shape
            P.op("gpsimd", "dma",
                 out=AP(dbgT.tensor, idx * D * T + ct * 128 * T, [[T, p], [1, w]]),
                 in_=sap, wait=[(n, v) for n, v in snap2.items()],
                 inc=("dOg", 16))
    Wa = nc.dram_tensor("Wa", [D, 3 * D], F32, kind="ExternalInput").ap()
    ba_d = nc.dram_tensor("ba", [3 * D], F32, kind="ExternalInput").ap()
    WqT = nc.dram_tensor("WqT", [D, D], F32, kind="ExternalInput").ap()
    WkT = nc.dram_tensor("WkT", [D, D], F32, kind="ExternalInput").ap()
    WvT = nc.dram_tensor("WvT", [D, D], F32, kind="ExternalInput").ap()
    WoT = nc.dram_tensor("WoT", [D, D], F32, kind="ExternalInput").ap()
    WpkT = nc.dram_tensor("WpkT", [D, D], F32, kind="ExternalInput").ap()
    WpqT = nc.dram_tensor("WpqT", [D, D], F32, kind="ExternalInput").ap()
    relT = nc.dram_tensor("relT", [D, S], F32, kind="ExternalInput").ap()
    relrevT = nc.dram_tensor("relrevT", [D, S], F32, kind="ExternalInput").ap()
    bq_d = nc.dram_tensor("bq", [D], F32, kind="ExternalInput").ap()
    bk_d = nc.dram_tensor("bk", [D], F32, kind="ExternalInput").ap()
    bv_d = nc.dram_tensor("bv", [D], F32, kind="ExternalInput").ap()
    bo_d = nc.dram_tensor("bo", [D], F32, kind="ExternalInput").ap()
    bpk_d = nc.dram_tensor("bpk", [D], F32, kind="ExternalInput").ap()
    bpq_d = nc.dram_tensor("bpq", [D], F32, kind="ExternalInput").ap()
    gwT = nc.dram_tensor("gwT", [D, E], F32, kind="ExternalInput").ap()
    WeT = nc.dram_tensor("WeT", [E, D, D], F32, kind="ExternalInput").ap()
    I128c = nc.dram_tensor("I128c", [128, 128], F32, kind="ExternalInput").ap()
    SEL8c = nc.dram_tensor("SEL8c", [8, E * 128], F32, kind="ExternalInput").ap()
    ONESF = nc.dram_tensor("ONESF", [128, 384], F32, kind="ExternalInput").ap()

    cpad_d = nc.dram_tensor("cpad_d", [2, 2, TT, 128, PADW], BF16, kind="Internal")
    pk_d = nc.dram_tensor("pk_d", [HP, 128, S], BF16, kind="Internal")
    pqr_d = nc.dram_tensor("pqr_d", [HP, 128, S], BF16, kind="Internal")
    ctxn_d = nc.dram_tensor("ctxn_d", [HP, 128, T], BF16, kind="Internal")

    P = Prog(nc)

    ARENA = 209920
    _arena = nc.alloc_sbuf_tensor("manual_arena", [128, ARENA], mybir.dt.uint8)
    SB_BASE = nc.lookup_mloc(_arena).addr

    def at(name, shape, dtype, off):
        esz = 2 if dtype == BF16 else 4
        per = int(np.prod(shape[1:])) * esz
        assert off + per <= ARENA, (name, off + per)
        return nc.alloc_sbuf_tensor_at(name, shape, dtype, offset=off + SB_BASE)

    I128 = at("I128", [128, 128], BF16, 0)
    ones_b = at("ones_b", [128, 128], BF16, 256)
    onesf = at("onesf", [128, 384], F32, 512)
    sel8 = at("sel8", [8, E * 128], BF16, 2048)
    ba_sb = at("ba_sb", [128, 24], F32, 4096)
    bq_sb = at("bq_sb", [128, 8], F32, 4192)
    bk_sb = at("bk_sb", [128, 8], F32, 4224)
    bo_sb = at("bo_sb", [128, 8], F32, 4256)
    bpk_sb = at("bpk_sb", [128, 8], F32, 4288)
    bpq_sb = at("bpq_sb", [128, 8], F32, 4320)
    bv_sb = at("bv_sb", [128, 8], BF16, 4352)
    gw_sb = at("gw_sb", [128, CT * E], BF16, 4384)
    ec2 = at("ec2", [128, 64], F32, 4544)
    rs_sb = at("rs_sb", [128, 512], F32, 4800)
    edlhsT = at("edlhsT", [128, 97], BF16, 6848)
    ban_sb = at("ban_sb", [128, 8], F32, 7072)
    lnm = at("lnm", [1, 512], F32, 7104)
    lnr = at("lnr", [1, 512], F32, 9152)
    lnt = at("lnt", [1, 512], F32, 11200)
    lnt2 = at("lnt2", [1, 512], F32, 13248)
    xb_buf = at("xb_buf", [128, CT * 512], BF16, 15296)
    sq_buf = at("sq_buf", [128, CT * 512], BF16, 23488)
    t1_buf = at("t1_buf", [128, 512], F32, 31680)
    XOFF = 33728
    x_sb = at("x_sb", [128, CT * 2048], F32, XOFF)
    xn_sb = at("xn_sb", [128, CT * 2048], BF16, XOFF + 65536)
    PH = XOFF + 65536 + 32768
    LIMIT = 212863

    wa_sb = at("wa_sb", [128, 6 * 1024], BF16, PH)
    f_buf = at("f_buf", [128, 2048], F32, PH + 12288)
    g_buf = at("g_buf", [128, 2048], F32, PH + 20480)
    og_buf = at("og_buf", [128, 2048], F32, PH + 28672)
    h_buf = at("h_buf", [128, 2048], F32, PH + 36864)
    ig_t = at("ig_t", [128, 512], F32, PH + 45056)
    in_t = at("in_t", [128, 512], F32, PH + 47104)
    xm_sb = at("xm_sb", [128, 2 * CT * 512], BF16, PH)
    we_sb = at("we_sb", [128, 2 * CT * 1024], BF16, PH + 16384)
    mba_sb = at("mba_sb", [128, E * 512], BF16, PH + 49152)
    mrow_sb = at("mrow_sb", [8, T], BF16, PH + 57344)
    msk_sb = at("msk_sb", [128, 8], BF16, PH + 61440)
    mx_t = at("mx_t", [128, 1], F32, PH + 61472)
    gs_t = at("gs_t", [128, 8], F32, PH + 61504)
    mskf_t = at("mskf_t", [128, 8], F32, PH + 61536)
    mxr_t = at("mxr_t", [128, 8], F32, PH + 71936)
    gsall = at("gsall", [128, 128], F32, PH + 72000)
    gscr = at("gscr", [128, 128], F32, PH + 72512)
    mtree = at("mtree", [128, 112], F32, PH + 73024)
    mskall = at("mskall", [128, 128], BF16, PH + 73536)
    relb_sb = at("relb_sb", [128, CT * 1024], BF16, PH + 16384)
    pre_sg = at("pre_sg", [128, 2 * 512], BF16, PH + 36864)
    qb_sb = at("qb_sb", [128, 2048], BF16, PH)
    kb_sb = at("kb_sb", [128, 2048], BF16, PH + 4096)
    pk_sb = at("pk_sb", [128, 1024], BF16, PH + 8192)
    pqr_sb = at("pqr_sb", [128, 1024], BF16, PH + 10240)
    v_sb = at("v_sb", [128, TT * 194], BF16, PH + 12288)
    cpad_sb = at("cpad_sb", [128, 4 * PADW], BF16, PH + 18496)
    band_sb = at("band_sb", [128, 4 * 1152], BF16, PH + 29760)
    pt_sb = at("pt_sb", [128, 4 * 512], BF16, PH + 38976)
    pt2_sb = at("pt2_sb", [128, 4 * 512], BF16, PH + 73792)
    ctmp_sb = at("ctmp_sb", [128, 512], F32, PH + 43072)
    ctxn_sb = at("ctxn_sb", [128, 2048], BF16, PH + 45120)
    cstr_sb = at("cstr_sb", [128, HP * 512], BF16, PH + 49216)
    edge_sb = at("edge_sb", [128, 2048], BF16, PH + 57408)
    wq_sb = at("wq_sb", [128, 3 * 1024], BF16, PH + 61568)
    wo_sb = at("wo_sb", [128, 2 * 1024], BF16, PH + 67712)
    wofull = at("wofull", [128, CT * 8 * 128], BF16, PH)

    pbT = [nc.alloc_psum_tensor(f"pb{i}", [128, 512], F32) for i in range(8)]
    pb = [t.ap() for t in pbT]

    def xtile(ct, lo=0, hi=T):
        return x_sb.ap()[:, ct * T + lo:ct * T + hi]

    def xntile(ct, lo=0, hi=T):
        return xn_sb.ap()[:, ct * T + lo:ct * T + hi]

    st = {}

    P.op("gpsimd", "dma", out=I128.ap(), in_=I128c, inc=("dCg", 16))
    P.op("gpsimd", "dma", out=sel8.ap(), in_=SEL8c, inc=("dCg", 16))
    P.op("sync", "dma", out=onesf.ap(), in_=ONESF, inc=("dC", 16))
    P.op("gpsimd", "memset", ap=ones_b.ap(), val=1.0, inc=("g", 1))
    _bias_specs = ((ba_d, ba_sb, 24, None), (bq_d, bq_sb, 8, SCALE),
                   (bk_d, bk_sb, 8, None), (bpk_d, bpk_sb, 8, None),
                   (bpq_d, bpq_sb, 8, SCALE), (bo_d, bo_sb, 8, None))
    for (src, dst, n, scl) in _bias_specs:
        P.op("sync", "dma", out=dst.ap()[:, 0:n],
             in_=AP(src.tensor, 0, [[1, 128], [128, n]]), slow=True,
             inc=("dC", 16))
    _dC_all = P.cur("dC")
    for (src, dst, n, scl) in _bias_specs:
        if scl is not None:
            P.op("vector", "ts", out=dst.ap()[:, 0:n], in0=dst.ap()[:, 0:n],
                 s1=scl, wait=[("dC", _dC_all)], inc=("v", 1))
    P.op("gpsimd", "dma", out=bv_sb.ap(),
         in_=AP(bv_d.tensor, 0, [[1, 128], [128, 8]]), slow=True, inc=("dCg", 16))
    P.op("gpsimd", "dma", out=gw_sb.ap().rearrange("p (a b) -> p a b", a=CT),
         in_=AP(gwT.tensor, 0, [[8, 128], [1024, CT], [1, 8]]), slow=True,
         inc=("dCg", 16))
    st["setup_dC"] = P.cur("dC")
    st["setup_dCg"] = P.cur("dCg")
    P.op("vector", "ts", out=ban_sb.ap(), in0=ba_sb.ap()[:, 8:16], s1=-1.0,
         wait=[("dC", st["setup_dC"])], inc=("v", 1))
    st["setup_g"] = P.cur("g")
    st["setup_v"] = P.cur("v")
    P.barrier()

    if 2 in stages:
        relb = relb_sb.ap()
        for mi, (wT, rel_src, dst_d, bias_sb, scl) in enumerate((
                (WpkT, relT, pk_d, bpk_sb, 1.0),
                (WpqT, relrevT, pqr_d, bpq_sb, SCALE))):
            rtg = P.op("gpsimd", "dma",
                       out=relb.rearrange("p (a b) -> p a b", a=CT),
                       in_=AP(rel_src.tensor, 0, [[S, 128], [128 * S, CT], [1, S]]),
                       wait=[("t", P.cur("t"))], inc=("dRel", 16))
            for ft in range(CT):
                ws = (ft % 2) * 1024
                wslot = wq_sb.ap()[:, ws:ws + 1024]
                wtg = P.op("gpsimd", "dma",
                           out=wslot.rearrange("p (a b) -> p a b", a=CT),
                           in_=AP(wT.tensor, ft * 128,
                                  [[D, 128], [128 * D, CT], [1, 128]]),
                           wait=[("t", st.get(f"pre{mi}w{ft % 2}", 0))],
                           inc=(f"dWp{ft % 2}", 16))
                for sc2 in range(2):
                    slot = (ft * 2 + sc2) % 2
                    bank = pb[slot]
                    for ct in range(CT):
                        P.op("tensor", "mm", out=bank,
                             lhsT=wslot[:, ct * 128:ct * 128 + 128],
                             rhs=relb[:, ct * 1024 + sc2 * 512:ct * 1024 + sc2 * 512 + 512],
                             start=(ct == 0), stop=(ct == CT - 1),
                             wait=([(f"dWp{ft % 2}", P.cur(f"dWp{ft % 2}")),
                                    ("dRel", P.cur("dRel")),
                                    ("v", st.get(f"preb{slot}", 0)),
                                    ("dC", st["setup_dC"]),
                                    ("dCg", st["setup_dCg"]),
                                    ("v", st["setup_v"])] if ct == 0 else []),
                             inc=(("t", 1) if ct == CT - 1 else None))
                    mmt = P.cur("t")
                    ev = P.op("vector", "stt",
                              out=pre_sg.ap()[:, slot * 512:slot * 512 + 512],
                              in0=bank, s=scl,
                              in1=bias_sb.ap()[:, ft:ft + 1].broadcast_to([128, 512]),
                              op0=ALU.mult, op1=ALU.add,
                              wait=[("t", mmt), ("dP", st.get(f"prew{slot}", 0))],
                              inc=("v", 1))
                    st[f"preb{slot}"] = ev
                    st[f"prew{slot}"] = P.op(
                        "sync", "dma",
                        out=AP(dst_d, ft * 128 * S + sc2 * 512, [[S, 128], [1, 512]]),
                        in_=pre_sg.ap()[:, slot * 512:slot * 512 + 512],
                        wait=[("v", ev)], inc=("dP", 16))
                st[f"pre{mi}w{ft % 2}"] = P.cur("t")
        st["pre_dP"] = P.cur("dP")
        P.barrier()

    def layer_norm():
        for qc in range(QCN):
            lo = qc * 512
            ctag = None
            for ct in range(CT):
                ctag = P.op("vector", "copy",
                            out=xb_buf.ap()[:, ct * 512:(ct + 1) * 512],
                            in_=xtile(ct, lo, lo + 512),
                            wait=([("t", st.get("ln_sums", 0))] if ct == 0 else []),
                            inc=("v", 1))
            sqtag = None
            for ct in range(CT):
                sqtag = P.op("scalar", "act",
                             out=sq_buf.ap()[:, ct * 512:(ct + 1) * 512],
                             in_=xtile(ct, lo, lo + 512), func=AF.Square,
                             wait=([("t", st.get("ln_sums", 0))] if ct == 0 else []),
                             inc=("sc", 1))
            for ct in range(CT):
                P.op("tensor", "mm", out=pb[0][0:1, :], lhsT=ones_b.ap()[:, 0:1],
                     rhs=xb_buf.ap()[:, ct * 512:(ct + 1) * 512],
                     start=(ct == 0), stop=(ct == CT - 1),
                     wait=([("v", ctag), ("v", st.get("ln_statv", 0))]
                           if ct == 0 else []),
                     inc=(("t", 1) if ct == CT - 1 else None))
            for ct in range(CT):
                P.op("tensor", "mm", out=pb[1][0:1, :], lhsT=ones_b.ap()[:, 0:1],
                     rhs=sq_buf.ap()[:, ct * 512:(ct + 1) * 512],
                     start=(ct == 0), stop=(ct == CT - 1),
                     wait=([("sc", sqtag)] if ct == 0 else []),
                     inc=(("t", 1) if ct == CT - 1 else None))
            st["ln_sums"] = P.cur("t")
            P.op("vector", "ts", out=lnm.ap(), in0=pb[0][0:1, :], s1=1.0 / D,
                 wait=[("t", st["ln_sums"])], inc=("v", 1))
            P.op("vector", "tt", out=lnt.ap(), a=lnm.ap(), b=lnm.ap(),
                 alu=ALU.mult, inc=("v", 1))
            vv = P.op("vector", "stt", out=lnt2.ap(), in0=pb[1][0:1, :], s=1.0 / D,
                      in1=lnt.ap(), op0=ALU.mult, op1=ALU.subtract, inc=("v", 1))
            vv = P.op("vector", "ts", out=lnt2.ap(), in0=lnt2.ap(), s1=EPS,
                      op0=ALU.add, inc=("v", 1))
            sd = P.op("scalar", "act", out=lnt.ap(), in_=lnt2.ap(), func=AF.Sqrt,
                      bias=0.0, wait=[("v", vv)], inc=("sc", 1))
            rr = P.op("vector", "recip", out=lnr.ap(), in_=lnt.ap(),
                      wait=[("sc", sd)], inc=("v", 1))
            mr = P.op("vector", "tt", out=lnt2.ap(), a=lnm.ap(), b=lnr.ap(),
                      alu=ALU.mult, inc=("v", 1))
            st["ln_statv"] = mr
            r1 = P.op("tensor", "mm", out=pb[2], lhsT=onesf.ap()[0:1, 256:384],
                      rhs=lnr.ap(), start=True, stop=True,
                      wait=[("v", mr), ("v", st.get("ln_normv", 0))], inc=("t", 1))
            r2 = P.op("tensor", "mm", out=pb[3], lhsT=onesf.ap()[0:1, 256:384],
                      rhs=lnt2.ap(), start=True, stop=True, inc=("t", 1))
            vn = None
            for ct in range(CT):
                P.op("vector", "tt", out=t1_buf.ap(), a=xtile(ct, lo, lo + 512),
                     b=pb[2], alu=ALU.mult, wait=[("t", r2)], inc=("v", 1))
                vn = P.op("vector", "tt", out=xntile(ct, lo, lo + 512),
                          a=t1_buf.ap(), b=pb[3], alu=ALU.subtract, inc=("v", 1))
            st["ln_normv"] = vn
        st["ln_done_v"] = P.cur("v")
        P.barrier()

    def attn_a():
        wtags = {}

        def load_w(g):
            for fam in range(3):
                slot = fam * 2 + (g % 2)
                wtags[(g, fam)] = P.op(
                    "gpsimd", "dma",
                    out=wa_sb.ap()[:, slot * 1024:(slot + 1) * 1024]
                    .rearrange("p (a b) -> p a b", a=CT),
                    in_=AP(Wa.tensor, fam * 1024 + g * 128,
                           [[3 * D, 128], [128 * 3 * D, CT], [1, 128]]),
                    wait=[("t", st.get(f"aa_mm_g{g - 2}", 0))],
                    inc=(f"dWa{g % 2}", 16))

        load_w(0)
        load_w(1)
        for g in range(CT):
            for tc in range(QCN):
                bs = 0 if (g * QCN + tc) % 2 == 0 else 3
                for fam in range(3):
                    slot = fam * 2 + (g % 2)
                    for ct in range(CT):
                        P.op("tensor", "mm", out=pb[bs + fam],
                             lhsT=wa_sb.ap()[:, slot * 1024 + ct * 128:
                                             slot * 1024 + ct * 128 + 128],
                             rhs=xntile(ct, tc * 512, tc * 512 + 512),
                             start=(ct == 0), stop=(ct == CT - 1),
                             wait=([(f"dWa{g % 2}", P.cur(f"dWa{g % 2}")),
                                    ("sc", st.get(f"aa_ev{bs}", 0)),
                                    ("v", st.get(f"aa_gv{bs}", 0))]
                                   if ct == 0 else []),
                             inc=(("t", 1) if ct == CT - 1 else None))
                mmt = P.cur("t")
                st[f"aa_mm_g{g}"] = mmt
                lo = tc * 512
                w0 = [("t", mmt), ("v", st.get("aa_gv_last", 0))]
                if tc == 0:
                    w0.append(("v", st.get("aa_gdone", 0)))
                P.op("scalar", "act", out=in_t.ap(), in_=pb[bs + 0], func=AF.Tanh,
                     bias=ba_sb.ap()[:, g:g + 1], wait=w0, inc=("sc", 1))
                P.op("scalar", "act", out=ig_t.ap(), in_=pb[bs + 1], func=AF.Sigmoid,
                     bias=ba_sb.ap()[:, 8 + g:8 + g + 1], inc=("sc", 1))
                P.op("scalar", "act", out=f_buf.ap()[:, lo:lo + 512], in_=pb[bs + 1],
                     func=AF.Sigmoid, scale=-1.0, bias=ban_sb.ap()[:, g:g + 1],
                     inc=("sc", 1))
                ev3 = P.op("scalar", "act", out=og_buf.ap()[:, lo:lo + 512],
                           in_=pb[bs + 2], func=AF.Sigmoid,
                           bias=ba_sb.ap()[:, 16 + g:16 + g + 1], inc=("sc", 1))
                st[f"aa_ev{bs}"] = ev3
                gv = P.op("vector", "tt", out=g_buf.ap()[:, lo:lo + 512],
                          a=in_t.ap(), b=ig_t.ap(), alu=ALU.mult,
                          wait=[("sc", ev3)], inc=("v", 1))
                st[f"aa_gv{bs}"] = gv
                st["aa_gv_last"] = gv
            if g + 2 < CT:
                load_w(g + 2)
            scv = P.op("vector", "scan", out=h_buf.ap(), d0=f_buf.ap(),
                       d1=g_buf.ap(), init=0.0, op0=ALU.mult, op1=ALU.add,
                       wait=[("sc", st.get("aa_thp", 0))], inc=("v", 1))
            if SKIP_AA_TAIL:
                continue
            th = P.op("scalar", "act", out=f_buf.ap(), in_=h_buf.ap(),
                      func=AF.Tanh, wait=[("v", scv)], inc=("sc", 1))
            st["aa_thp"] = th
            P.op("vector", "tt", out=og_buf.ap(), a=f_buf.ap(), b=og_buf.ap(),
                 alu=ALU.mult, wait=[("sc", th)], inc=("v", 1))
            st["aa_gdone"] = P.op("vector", "tt", out=xtile(g), a=xtile(g),
                                  b=og_buf.ap(), alu=ALU.add, inc=("v", 1))
        P.barrier()

    wecnt = [0]

    def ffn():
        for tt in range(TT):
            bank = pb[6] if tt % 2 == 0 else pb[7]
            for ct in range(CT):
                P.op("tensor", "mm", out=bank[:, 0:8],
                     lhsT=xntile(ct, tt * 128, tt * 128 + 128),
                     rhs=gw_sb.ap()[:, ct * E:(ct + 1) * E],
                     start=(ct == 0), stop=(ct == CT - 1),
                     wait=([("v", st.get(f"ffn_gv{tt % 2}", 0)),
                            ("dCg", st["setup_dCg"])] if ct == 0 else []),
                     inc=(("t", 1) if ct == CT - 1 else None))
            gt = P.cur("t")
            gv = P.op("vector", "copy", out=gsall.ap()[:, tt * 8:(tt + 1) * 8],
                      in_=bank[:, 0:8], wait=[("t", gt)], inc=("v", 1))
            st[f"ffn_gv{tt % 2}"] = gv
        ga = gsall
        mt = mtree
        P.op("vector", "copy", out=gscr.ap(), in_=gsall.ap(), inc=("v", 1))
        P.op("vector", "tt",
             out=mtree.ap()[:, 0:64].rearrange("p (a b) -> p a b", a=16),
             a=AP(ga, 0, [[128, 128], [8, 16], [1, 4]]),
             b=AP(ga, 4, [[128, 128], [8, 16], [1, 4]]),
             alu=ALU.max, inc=("v", 1))
        P.op("vector", "copy", out=gscr.ap(), in_=gsall.ap(), inc=("v", 1))
        P.op("vector", "tt",
             out=mtree.ap()[:, 64:96].rearrange("p (a b) -> p a b", a=16),
             a=AP(mt, 0, [[112, 128], [4, 16], [1, 2]]),
             b=AP(mt, 2, [[112, 128], [4, 16], [1, 2]]),
             alu=ALU.max, inc=("v", 1))
        P.op("vector", "copy", out=gscr.ap(), in_=gsall.ap(), inc=("v", 1))
        P.op("vector", "tt", out=mtree.ap()[:, 96:112],
             a=AP(mt, 64, [[112, 128], [2, 16], [1, 1]]),
             b=AP(mt, 65, [[112, 128], [2, 16], [1, 1]]),
             alu=ALU.max, inc=("v", 1))
        P.op("vector", "copy", out=gscr.ap(), in_=gsall.ap(), inc=("v", 1))
        P.op("vector", "copy",
             out=gscr.ap().rearrange("p (a b) -> p a b", a=16),
             in_=AP(mt, 96, [[112, 128], [1, 16], [0, 8]]), inc=("v", 1))
        P.op("vector", "copy", out=mtree.ap()[:, 0:64], in_=mtree.ap()[:, 0:64],
             inc=("v", 1))
        P.op("vector", "tt", out=gsall.ap(), a=gsall.ap(), b=gscr.ap(),
             alu=ALU.is_ge, inc=("v", 1))
        P.op("vector", "copy", out=gscr.ap()[:, 0:64], in_=mtree.ap()[:, 0:64],
             inc=("v", 1))
        mv = P.op("vector", "copy", out=mskall.ap(), in_=gsall.ap(), inc=("v", 1))
        for tt in range(TT):
            bank2 = pb[4] if tt % 2 == 0 else pb[5]
            tr = P.op("tensor", "mm", out=bank2[0:8, 0:128],
                      lhsT=mskall.ap()[:, tt * 8:(tt + 1) * 8],
                      rhs=I128.ap(), start=True, stop=True,
                      wait=[("v", mv), ("sc", st.get(f"ffn_mev{tt % 2}", 0)),
                            ("dCg", st["setup_dCg"])],
                      inc=("t", 1))
            st[f"ffn_mev{tt % 2}"] = P.op(
                "scalar", "copy", out=mrow_sb.ap()[:, tt * 128:(tt + 1) * 128],
                in_=bank2[0:8, 0:128], wait=[("t", tr)], inc=("sc", 1))
        mrow_done = P.cur("sc")

        for tc in range(QCN):
            lo = tc * 512
            for e in range(E):
                bank = pb[6] if e % 2 == 0 else pb[7]
                mmt = P.op("tensor", "mm", out=bank,
                           lhsT=sel8.ap()[:, e * 128:(e + 1) * 128],
                           rhs=mrow_sb.ap()[:, lo:lo + 512], start=True, stop=True,
                           wait=[("sc", mrow_done),
                                 ("sc", st.get(f"ffn_mbev{e % 2}", 0)),
                                 ("v", st.get("ffn_evacv", 0))],
                           inc=("t", 1))
                st[f"ffn_mbev{e % 2}"] = P.op(
                    "scalar", "copy", out=mba_sb.ap()[:, e * 512:(e + 1) * 512],
                    in_=bank, wait=[("t", mmt)], inc=("sc", 1))
            mb_done = P.cur("sc")
            for e in range(E):
                xo = (e % 2) * CT * 512
                xmv = None
                for ct in range(CT):
                    xmv = P.op("vector", "tt",
                               out=xm_sb.ap()[:, xo + ct * 512:xo + (ct + 1) * 512],
                               a=xntile(ct, lo, lo + 512),
                               b=mba_sb.ap()[:, e * 512:(e + 1) * 512],
                               alu=ALU.mult,
                               wait=([("sc", mb_done),
                                      ("t", st.get(f"ffn_xm{e % 2}", 0))]
                                     if ct == 0 else []),
                               inc=("v", 1))
                wet = P.op("gpsimd", "dma",
                           out=we_sb.ap()[:, (e % 2) * CT * 1024:
                                          ((e % 2) + 1) * CT * 1024]
                           .rearrange("p (a b) -> p a b", a=CT),
                           in_=AP(WeT.tensor, e * D * D,
                                  [[D, 128], [128 * D, CT], [1, D]]),
                           wait=[("t", st.get(f"ffn_xm{e % 2}", 0))],
                           inc=(f"dWe{e % 2}", 16))
                for ft in range(CT):
                    for ct in range(CT):
                        w = []
                        if ct == 0 and ft == 0:
                            w = [("v", xmv), (f"dWe{e % 2}", P.cur(f"dWe{e % 2}"))]
                            if e == 0:
                                w.append(("v", st.get("ffn_evacv", 0)))
                        elif ct == 0 and e == 0:
                            w = [("v", st.get("ffn_evacv", 0))]
                        P.op("tensor", "mm", out=pb[ft],
                             lhsT=we_sb.ap()[:, (e % 2) * CT * 1024 + ct * 1024 +
                                             ft * 128:(e % 2) * CT * 1024 +
                                             ct * 1024 + ft * 128 + 128],
                             rhs=xm_sb.ap()[:, xo + ct * 512:xo + (ct + 1) * 512],
                             start=(e == 0 and ct == 0),
                             stop=(e == E - 1 and ct == CT - 1),
                             wait=w,
                             inc=(("t", 1) if ct == CT - 1 else None))
                st[f"ffn_xm{e % 2}"] = P.cur("t")
            expt = P.cur("t")
            for ft in range(CT):
                P.op("vector", "stt", out=t1_buf.ap(), in0=pb[ft], s=0.0,
                     in1=xntile(ft, lo, lo + 512), op0=ALU.add, op1=ALU.add,
                     wait=[("t", expt)], inc=("v", 1))
                P.op("vector", "tt", out=xtile(ft, lo, lo + 512),
                     a=xtile(ft, lo, lo + 512), b=t1_buf.ap(), alu=ALU.add,
                     inc=("v", 1))
            st["ffn_evacv"] = P.cur("v")
        P.barrier()

    def win(kt):
        k0 = kt * 128
        return max(0, k0 - 512), min(T, k0 + 640)

    def attn_b(b):
        for hp in range(HP):
            attn_b_hp(b, hp)
        P.barrier()
        if ABLIM <= 5:
            return
        st["wo_prev_t"] = P.cur("t")
        for tc in range(QCN):
            lo = tc * 512
            cs = 0
            ctag = None
            for hp in range(HP):
                ctag = P.op("sync", "dma",
                            out=cstr_sb.ap()[:, cs + hp * 512:cs + (hp + 1) * 512],
                            in_=AP(ctxn_d, hp * 128 * T + lo, [[T, 128], [1, 512]]),
                            wait=([("v", st.get("wo_used_last", 0)),
                                   ("dCXo", P.cur("dCXo"))] if hp == 0 else []),
                            inc=(f"dCX{tc % 2}", 16))
            if tc == 0:
                st["wo_wf"] = P.op(
                    "gpsimd", "dma",
                    out=wofull.ap().rearrange("p (a b c) -> p a b c", a=8, b=CT),
                    in_=AP(WoT.tensor, 0,
                           [[D, 128], [128 * D, 8], [128, CT], [1, 128]]),
                    wait=[("t", st.get("wo_prev_t", 0))],
                    inc=("dWof", 16))
            for ft in range(CT):
                ii = tc * CT + ft
                bank = pb[6] if ft % 2 == 0 else pb[7]
                for hp in range(HP):
                    P.op("tensor", "mm", out=bank,
                         lhsT=wofull.ap()[:, (hp * CT + ft) * 128:
                                          (hp * CT + ft) * 128 + 128],
                         rhs=cstr_sb.ap()[:, cs + hp * 512:cs + (hp + 1) * 512],
                         start=(hp == 0), stop=(hp == HP - 1),
                         wait=([("dWof", P.cur("dWof")),
                                (f"dCX{tc % 2}", P.cur(f"dCX{tc % 2}")),
                                ("v", st.get(f"wo_ev{ft % 2}", 0))]
                               if hp == 0 else []),
                         inc=(("t", 1) if hp == HP - 1 else None))
                mmt = P.cur("t")
                st[f"wo_mm{ft}"] = mmt
                st[f"wo_mmi{ii}"] = mmt
                if b == 0 and tc == 0:
                    for hp2 in range(HP):
                        P.op("tensor", "mm", out=pb[5][:, 0:1],
                             lhsT=wofull.ap()[:, (hp2 * CT + ft) * 128:
                                              (hp2 * CT + ft) * 128 + 128],
                             rhs=bv_sb.ap()[:, hp2:hp2 + 1],
                             start=(hp2 == 0), stop=(hp2 == HP - 1),
                             wait=([("v", st.get("bo_v", 0)),
                                    ("dCg", st["setup_dCg"])] if hp2 == 0 else []),
                             inc=(("t", 1) if hp2 == HP - 1 else None))
                    bot = P.cur("t")
                    st["bo_v"] = P.op("vector", "tt", out=bo_sb.ap()[:, ft:ft + 1],
                                      a=bo_sb.ap()[:, ft:ft + 1], b=pb[5][:, 0:1],
                                      alu=ALU.add, wait=[("t", bot)], inc=("v", 1))
                    st[f"bo_rdy{ft}"] = st["bo_v"]
                P.op("vector", "tt", out=t1_buf.ap(), a=bank,
                     b=bo_sb.ap()[:, ft:ft + 1].broadcast_to([128, 512]),
                     alu=ALU.add,
                     wait=[("t", mmt),
                           (("v", st.get(f"bo_rdy{ft}", 0)) if b == 0 else None)],
                     inc=("v", 1))
                ev = P.op("vector", "tt", out=xtile(ft, lo, lo + 512),
                          a=xtile(ft, lo, lo + 512), b=t1_buf.ap(), alu=ALU.add,
                          inc=("v", 1))
                st[f"wo_ev{ft % 2}"] = ev
            st["wo_used_last"] = P.cur("v")
        P.barrier()

    def attn_b_hp(b, hp):
        if hp == 0:
            P.op("gpsimd", "memset", ap=v_sb.ap(), val=0.0, inc=("g", 1))
            for kt in range(TT):
                b0 = kt * 194
                P.op("gpsimd", "memset", ap=v_sb.ap()[:, b0 + 64:b0 + 65], val=1.0,
                     inc=("g", 1))
                P.op("gpsimd", "memset", ap=v_sb.ap()[:, b0 + 98:b0 + 99], val=1.0,
                     inc=("g", 1))
            st["vconst_g"] = P.cur("g")
        wq_tags = []
        for i, wT in enumerate((WqT, WkT, WvT)):
            wq_tags.append(P.op(
                "gpsimd", "dma",
                out=wq_sb.ap()[:, i * 1024:(i + 1) * 1024]
                .rearrange("p (a b) -> p a b", a=CT),
                in_=AP(wT.tensor, hp * 128, [[D, 128], [128 * D, CT], [1, 128]]),
                wait=[("t", st.get(f"ab_qkv{hp - 1}", 0))], inc=("dWq", 16)))
        pkt = P.op("sync", "dma", out=pk_sb.ap(),
                   in_=AP(pk_d, hp * 128 * S, [[S, 128], [1, S]]),
                   wait=[("t", st.get(f"ab_cc{hp - 1}", 0)),
                         ("v", st.get(f"ab_ed{hp - 1}", 0))], inc=("dPK", 16))
        pqt = P.op("sync", "dma", out=pqr_sb.ap(),
                   in_=AP(pqr_d, hp * 128 * S, [[S, 128], [1, S]]),
                   inc=("dPK", 16))
        for (wi, dst, bias, scl) in ((0, qb_sb, bq_sb, SCALE), (1, kb_sb, bk_sb, 1.0)):
            for tc in range(QCN):
                bank = pb[6] if (wi * QCN + tc) % 2 == 0 else pb[7]
                for ct in range(CT):
                    P.op("tensor", "mm", out=bank,
                         lhsT=wq_sb.ap()[:, wi * 1024 + ct * 128:
                                         wi * 1024 + ct * 128 + 128],
                         rhs=xntile(ct, tc * 512, tc * 512 + 512),
                         start=(ct == 0), stop=(ct == CT - 1),
                         wait=([("dWq", P.cur("dWq")),
                                ("v", st.get(f"ab_pev{(wi * QCN + tc) % 2}", 0)),
                                ("t", st.get(f"ab_qkuse{hp - 1}", 0))]
                               if ct == 0 else []),
                         inc=(("t", 1) if ct == CT - 1 else None))
                mmt = P.cur("t")
                st[f"ab_pev{(wi * QCN + tc) % 2}"] = P.op(
                    "vector", "stt", out=dst.ap()[:, tc * 512:(tc + 1) * 512],
                    in0=bank, s=scl,
                    in1=bias.ap()[:, hp:hp + 1].broadcast_to([128, 512]),
                    op0=ALU.mult, op1=ALU.add,
                    wait=[("t", mmt)], inc=("v", 1))
        for kt in range(TT):
            bank = pb[6] if kt % 2 == 0 else pb[7]
            for ct in range(CT):
                P.op("tensor", "mm", out=bank[:, 0:128],
                     lhsT=xntile(ct, kt * 128, kt * 128 + 128),
                     rhs=wq_sb.ap()[:, 2 * 1024 + ct * 128:2 * 1024 + ct * 128 + 128],
                     start=(ct == 0), stop=(ct == CT - 1),
                     wait=([("dWq", P.cur("dWq")), ("g", st["vconst_g"]),
                            ("sc", st.get(f"ab_pevV{kt % 2}", 0)),
                            ("v", st.get(f"ab_pev{kt % 2}", 0)),
                            ("t", st.get(f"ab_vuse{hp - 1}", 0))]
                           if ct == 0 else []),
                     inc=(("t", 1) if ct == CT - 1 else None))
            mmt = P.cur("t")
            b0 = kt * 194
            P.op("scalar", "copy", out=v_sb.ap()[:, b0:b0 + 64], in_=bank[:, 0:64],
                 wait=[("t", mmt)], inc=("sc", 1))
            st[f"ab_pevV{kt % 2}"] = P.op(
                "scalar", "copy", out=v_sb.ap()[:, b0 + 130:b0 + 194],
                in_=bank[:, 64:128], inc=("sc", 1))
        st[f"ab_qkv{hp}"] = P.cur("t")
        qkv_sc = P.cur("sc")
        if ABLIM <= 1:
            return

        edv = P.op("vector", "memset", ap=edlhsT.ap(), val=0.0,
                   wait=[("dPK", P.cur("dPK")),
                         ("t", st.get(f"ab_eduse{hp - 1}", 0))],
                   inc=("v", 1))
        P.op("vector", "copy", out=edlhsT.ap()[0:64, 0:1], in_=pk_sb.ap()[0:64, 0:1],
             inc=("v", 1))
        P.op("vector", "copy", out=edlhsT.ap()[0:64, 32:33],
             in_=pk_sb.ap()[0:64, 1023:1024], inc=("v", 1))
        P.op("vector", "copy", out=edlhsT.ap()[64:128, 64:65],
             in_=pk_sb.ap()[64:128, 0:1], inc=("v", 1))
        edv = P.op("vector", "copy", out=edlhsT.ap()[64:128, 96:97],
                   in_=pk_sb.ap()[64:128, 1023:1024], inc=("v", 1))
        for qc in range(QCN if ABLIM >= 3 else 0):
            emm = P.op("tensor", "mm", out=pb[4][0:97, :], lhsT=edlhsT.ap(),
                       rhs=qb_sb.ap()[:, qc * 512:(qc + 1) * 512],
                       start=True, stop=True,
                       wait=[("v", edv), ("sc", st.get("ab_edev", 0)),
                             ("sc", qkv_sc)],
                       inc=("t", 1))
            st["ab_edev"] = P.op("scalar", "copy",
                                 out=edge_sb.ap()[0:97, qc * 512:(qc + 1) * 512],
                                 in_=pb[4][0:97, :], wait=[("t", emm)], inc=("sc", 1))
        st[f"ab_ed{hp}"] = P.cur("v")
        ed_sc = P.cur("sc")
        if ABLIM <= 2.2:
            return

        wrtags = {}
        for mi in range(2):
            lb = qb_sb if mi == 0 else kb_sb
            rb = pk_sb if mi == 0 else pqr_sb
            for qt in range(TT):
                slot2 = qt % 2
                for h in range(2):
                    slot = slot2 * 2 + h
                    soff = slot * PADW
                    li = mi * TT + qt
                    for sc2 in range(2):
                        bank = pb[h * 2 + sc2]
                        P.op("tensor", "mm", out=bank,
                             lhsT=lb.ap()[h * 64:h * 64 + 64,
                                          qt * 128:qt * 128 + 128],
                             rhs=rb.ap()[h * 64:h * 64 + 64,
                                         sc2 * 512:sc2 * 512 + 512],
                             start=True, stop=True,
                             wait=([("sc", qkv_sc), ("dPK", P.cur("dPK")),
                                    st.get(f"ab_cbank{h}", None),
                                    ("v", st.get(f"ab_cpv{slot}", 0)),
                                    ("sc", st.get(f"ab_cpev{slot}", 0)),
                                    ("v", st.get("ab_ctxnv", 0)),
                                    (f"dP{slot}", P.cur(f"dP{slot}"))]
                                   if sc2 == 0 else []),
                             inc=("t", 1))
                    mmt = P.cur("t")
                    if ABLIM <= 2.3:
                        continue
                    if mi == 0:
                        P.op("vector", "copy",
                             out=cpad_sb.ap()[:, soff + 256:soff + 256 + 512],
                             in_=pb[h * 2], wait=[("t", mmt)], inc=("v", 1))
                        cpv2 = P.op("vector", "copy",
                                    out=cpad_sb.ap()[:, soff + 768:soff + 768 + 512],
                                    in_=pb[h * 2 + 1], inc=("v", 1))
                        P.op("vector", "copy",
                             out=cpad_sb.ap()[:, soff:soff + 256],
                             in_=cpad_sb.ap()[:, soff + 256:soff + 257]
                             .broadcast_to([128, 256]),
                             wait=[("v", cpv2)], inc=("v", 1))
                        pv = P.op("vector", "copy",
                                  out=cpad_sb.ap()[:, soff + 1280:soff + 1408],
                                  in_=cpad_sb.ap()[:, soff + 1279:soff + 1280]
                                  .broadcast_to([128, 128]), inc=("v", 1))
                        st[f"ab_cpv{slot}"] = pv
                        st[f"ab_cbank{h}"] = ("v", pv)
                        wdep = ("v", pv)
                    else:
                        P.op("scalar", "copy",
                             out=cpad_sb.ap()[:, soff + 256:soff + 256 + 512],
                             in_=pb[h * 2], wait=[("t", mmt)], inc=("sc", 1))
                        P.op("scalar", "copy",
                             out=cpad_sb.ap()[:, soff + 768:soff + 768 + 512],
                             in_=pb[h * 2 + 1], inc=("sc", 1))
                        P.op("scalar", "copy", out=ec2.ap()[:, h * 32 + 2 * qt:
                                                            h * 32 + 2 * qt + 1],
                             in_=pb[h * 2][:, 0:1], inc=("sc", 1))
                        P.op("scalar", "copy", out=ec2.ap()[:, h * 32 + 2 * qt + 1:
                                                            h * 32 + 2 * qt + 2],
                             in_=pb[h * 2 + 1][:, 511:512], inc=("sc", 1))
                        P.op("scalar", "copy",
                             out=cpad_sb.ap()[:, soff:soff + 256],
                             in_=pb[h * 2][:, 0:1].broadcast_to([128, 256]),
                             inc=("sc", 1))
                        pv = P.op("scalar", "copy",
                                  out=cpad_sb.ap()[:, soff + 1280:soff + 1408],
                                  in_=pb[h * 2 + 1][:, 511:512]
                                  .broadcast_to([128, 128]), inc=("sc", 1))
                        st[f"ab_cpev{slot}"] = pv
                        st[f"ab_cbank{h}"] = ("sc", pv)
                        wdep = ("sc", pv)
                    if ABLIM <= 2.4:
                        continue
                    wrtags[(li, h)] = P.op(
                        "sync", "dma",
                        out=AP(cpad_d, ((mi * 2 + h) * TT + qt) * 128 * PADW,
                               [[PADW, 128], [1, PADW]]),
                        in_=cpad_sb.ap()[:, soff:soff + PADW],
                        wait=[wdep], inc=(f"dP{slot}", 16))
        if ABLIM <= 3:
            return
        cc_sc, cc_v = P.cur("sc"), P.cur("v")

        for half in range(2):
            qlo_h, qhi_h = half * 1024, half * 1024 + 1024

            def emit_bands(kt):
                A, B = win(kt)
                a, bb = max(A, qlo_h), min(B, qhi_h)
                if a >= bb:
                    return None
                k0 = kt * 128
                for h in range(2):
                    bslot = h * 2 + (kt % 2)
                    boff = bslot * 1152
                    src = AP(cpad_d,
                             (0 * 2 + h) * TT * 128 * PADW +
                             a * (PADW - 1) + k0 + 768,
                             [[PADW - 1, bb - a], [1, 128]])
                    wdeps = [(f"dP{i}", P.cur(f"dP{i}")) for i in range(4)]
                    wdeps.append(("t", st.get(f"ab_buse{bslot}", 0)))
                    P.op("sync", "dma",
                         out=band_sb.ap()[:, boff:boff + (bb - a)],
                         in_=src, transpose=True,
                         wait=wdeps, inc=(f"dB{kt % 2}", 16))
                    src2 = AP(cpad_d,
                              (1 * 2 + h) * TT * 128 * PADW +
                              k0 * (PADW - 1) + a + 767,
                              [[PADW - 1, 128], [1, bb - a]])
                    P.op("gpsimd", "dma",
                         out=band_sb.ap()[:, boff:boff + (bb - a)],
                         in_=src2, accum=True,
                         wait=[(f"dB{kt % 2}", P.cur(f"dB{kt % 2}"))],
                         inc=(f"dB2{kt % 2}", 16))
                return (a, bb)

            def emit_qk_exp(kt, ab):
                A, B = win(kt)
                for qci in range(2):
                    qc = half * 2 + qci
                    qlo = qc * 512
                    for h in range(2):
                        sbank = pb[(kt % 2) * 2 + h]
                        adds = []
                        if ab is not None:
                            a, bb = ab
                            a2, b2 = max(a, qlo), min(bb, qlo + 512)
                            if a2 < b2:
                                adds.append(("band", a2, b2, h * 2 + (kt % 2)))
                        ecut1 = min(max(A, qlo), qlo + 512)
                        ecut2 = min(max(B, qlo), qlo + 512)
                        if qlo < ecut1:
                            adds.append(("edgeR", qlo, ecut1, None))
                        if ecut2 < qlo + 512:
                            adds.append(("edgeL", ecut2, qlo + 512, None))
                        qkw = [("sc", st.get(f"ab_sbank{h * 2 + (kt % 2)}", 0))]
                        if kt < 2:
                            qkw.append(("v", st.get("ab_ctxnv", 0)))
                            qkw.append(("sc", cc_sc))
                            qkw.append(("v", cc_v))
                        P.op("tensor", "mm", out=sbank,
                             lhsT=kb_sb.ap()[h * 64:h * 64 + 64,
                                             kt * 128:kt * 128 + 128],
                             rhs=qb_sb.ap()[h * 64:h * 64 + 64, qlo:qlo + 512],
                             start=True, stop=(len(adds) == 0),
                             wait=qkw, inc=("t", 1))
                        for ai, (knd, ra, rb2, bs_) in enumerate(adds):
                            lastf = (ai == len(adds) - 1)
                            if knd == "band":
                                boff = bs_ * 1152
                                P.op("tensor", "mm",
                                     out=sbank[:, ra - qlo:rb2 - qlo],
                                     lhsT=I128.ap(),
                                     rhs=band_sb.ap()[:, boff + ra - ab[0]:
                                                      boff + rb2 - ab[0]],
                                     start=False, stop=lastf,
                                     wait=[(f"dB2{kt % 2}",
                                            P.cur(f"dB2{kt % 2}")),
                                           ("dCg", st["setup_dCg"])],
                                     inc=("t", 1))
                            else:
                                r = (32 if knd == "edgeR" else 0) + h * 64
                                tp = (96, 0) if r == 96 else None
                                P.op("tensor", "mm",
                                     out=sbank[:, ra - qlo:rb2 - qlo],
                                     lhsT=ones_b.ap()[r:r + 1, 0:128],
                                     rhs=edge_sb.ap()[r:r + 1, ra:rb2],
                                     start=False, stop=lastf, tp=tp,
                                     wait=[("sc", ed_sc)], inc=("t", 1))
                        smm = P.cur("t")
                        st[f"ab_buse{h * 2 + (kt % 2)}"] = smm
                        pslot = (kt % 2) * 4 + qci * 2 + h
                        ptbuf = pt_sb if pslot < 4 else pt2_sb
                        poff = (pslot % 4) * 512
                        edeps = [("t", smm), ("t", st.get(f"ab_ptuse{pslot}", 0))]
                        cut1 = min(max(A, qlo), qlo + 512)
                        cut2 = min(max(B, qlo), qlo + 512)
                        segs = []
                        if qlo < cut1:
                            segs.append((qlo, cut1,
                                         ec2.ap()[:, h * 32 + 2 * kt:
                                                  h * 32 + 2 * kt + 1]))
                        if cut1 < cut2:
                            segs.append((cut1, cut2, 0.0))
                        if cut2 < qlo + 512:
                            segs.append((cut2, qlo + 512,
                                         ec2.ap()[:, h * 32 + 2 * kt + 1:
                                                  h * 32 + 2 * kt + 2]))
                        lastt = None
                        for (ra, rb_, bias) in segs:
                            lastt = P.op("scalar", "act",
                                         out=ptbuf.ap()[:, poff + ra - qlo:
                                                        poff + rb_ - qlo],
                                         in_=sbank[:, ra - qlo:rb_ - qlo],
                                         func=AF.Exp, bias=bias,
                                         wait=edeps, inc=("sc", 1))
                            edeps = []
                        st[f"ab_sbank{h * 2 + (kt % 2)}"] = lastt
                        st[("expk", kt, qci, h)] = lastt

            def emit_ctx(ktc):
                for qci in range(2):
                    for h in range(2):
                        cb = pb[4 + qci * 2 + h]
                        vb = v_sb.ap()
                        if h == 0:
                            lhsT = vb[:, ktc * 194:ktc * 194 + 65]
                        else:
                            lhsT = vb[:, ktc * 194 + 66:ktc * 194 + 194]
                        pslot = (ktc % 2) * 4 + qci * 2 + h
                        ptbuf = pt_sb if pslot < 4 else pt2_sb
                        poff = (pslot % 4) * 512
                        ctxw = [("sc", st[("expk", ktc, qci, h)])]
                        if ktc == 0:
                            ctxw.append(("sc", st.get(f"ab_ctmp{qci * 2 + h}", 0)))
                            ctxw.append(("sc", ed_sc))
                        P.op("tensor", "mm", out=cb[0:65 if h == 0 else 128, :],
                             lhsT=lhsT, rhs=ptbuf.ap()[:, poff:poff + 512],
                             start=(ktc == 0), stop=(ktc == TT - 1),
                             wait=ctxw, inc=("t", 1))
                        st[f"ab_ptuse{pslot}"] = P.cur("t")

            BPREF = int(_os.environ.get("BPREF", "1"))
            CLAG = int(_os.environ.get("CLAG", "1"))
            if BPREF:
                ab_cur = emit_bands(0)
            for kt in range(TT):
                if BPREF:
                    ab_next = emit_bands(kt + 1) if kt + 1 < TT else None
                else:
                    ab_cur = emit_bands(kt)
                emit_qk_exp(kt, ab_cur)
                if CLAG:
                    if kt > 0:
                        emit_ctx(kt - 1)
                else:
                    emit_ctx(kt)
                if BPREF:
                    ab_cur = ab_next
            if CLAG:
                emit_ctx(TT - 1)
            for qci in range(2 if ABLIM > 4 else 0):
                qc = half * 2 + qci
                for h in range(2):
                    cb = pb[4 + qci * 2 + h]
                    rrow = 64 if h == 0 else 32
                    ctxmm = P.cur("t")
                    rv = P.op("scalar", "copy",
                              out=rs_sb.ap()[rrow:rrow + 1, 0:512],
                              in_=cb[rrow:rrow + 1, :], wait=[("t", ctxmm)],
                              inc=("sc", 1))
                    rc = P.op("vector", "recip",
                              out=rs_sb.ap()[rrow:rrow + 1, 0:512],
                              in_=rs_sb.ap()[rrow:rrow + 1, 0:512],
                              wait=[("sc", rv)], inc=("v", 1))
                    lhsT = (onesf.ap()[64:65, 0:128] if h == 0
                            else onesf.ap()[32:33, 128:256])
                    rep = P.op("tensor", "mm", out=pb[h],
                               lhsT=lhsT, rhs=rs_sb.ap()[rrow:rrow + 1, 0:512],
                               start=True, stop=True,
                               wait=[("v", rc),
                                     ("sc", st.get(f"ab_sbank{h * 2}", 0)),
                                     ("sc", st.get(f"ab_sbank{h * 2 + 1}", 0))],
                               inc=("t", 1))
                    cv = P.op("scalar", "copy",
                              out=(ctmp_sb.ap()[0:64, :] if h == 0
                                   else ctmp_sb.ap()[64:128, :]),
                              in_=(cb[0:64, :] if h == 0 else cb[64:128, :]),
                              wait=[("v", st.get("ab_ctxnv", 0))], inc=("sc", 1))
                    st[f"ab_ctmp{qci * 2 + h}"] = cv
                    vv = P.op("vector", "tt",
                              out=(ctxn_sb.ap()[0:64, qc * 512:(qc + 1) * 512]
                                   if h == 0 else
                                   ctxn_sb.ap()[64:128, qc * 512:(qc + 1) * 512]),
                              a=(ctmp_sb.ap()[0:64, :] if h == 0
                                 else ctmp_sb.ap()[64:128, :]),
                              b=(pb[h][0:64, :] if h == 0 else pb[h][64:128, :]),
                              alu=ALU.mult,
                              wait=[("t", rep), ("sc", cv),
                                    ("dCXo", st.get(f"ab_ctxout{hp - 1}", 0))],
                              inc=("v", 1))
                    st["ab_ctxnv"] = vv
        st[f"ab_vuse{hp}"] = P.cur("t")
        st[f"ab_eduse{hp}"] = P.cur("t")
        st[f"ab_qkuse{hp}"] = P.cur("t")
        st[f"ab_cc{hp}"] = P.cur("t")

        if ABLIM <= 4:
            return
        P.op("sync", "dma", out=AP(ctxn_d, hp * 128 * T, [[T, 128], [1, T]]),
             in_=ctxn_sb.ap(), wait=[("v", st["ab_ctxnv"])], inc=("dCXo", 16))
        st[f"ab_ctxout{hp}"] = P.cur("dCXo")

    for b in range(n_batches):
        xl = None
        for ct in range(CT):
            xl = P.op("sync", "dma", out=xtile(ct),
                      in_=AP(xT.tensor, b * D * T + ct * 128 * T, [[T, 128], [1, T]]),
                      inc=("dX", 16))
        P.op("vector", "noop", wait=[("dX", xl)])
        P.op("tensor", "noop", wait=[("dX", xl)])
        P.op("scalar", "noop", wait=[("dX", xl)])
        if 0 in stages:
            layer_norm()
            if dbg == "ln" and b == 0:
                dump(0, lambda ct: xntile(ct))
            attn_a()
            if dbg in ("aa", "aa2") and b == 0:
                dump(0, lambda ct: f_buf.ap(), nt=1)
                dump(1, lambda ct: g_buf.ap(), nt=1)
                dump(2, lambda ct: h_buf.ap(), nt=1)
                dump(3, lambda ct: og_buf.ap(), nt=1)
        if 1 in stages:
            layer_norm()
            if dbg == "ffn1pre" and b == 0:
                dump(3, lambda ct: xntile(ct))
            ffn()
            if dbg == "ffn1" and b == 0:
                dump(0, lambda ct: mrow_sb.ap(), nt=1)
                dump(1, lambda ct: mba_sb.ap()[0:128, 0:2048], nt=1)
                dump(2, lambda ct: xm_sb.ap()[:, 0:2048], nt=1)
        if 2 in stages:
            layer_norm()
            attn_b(b)
        if 3 in stages:
            layer_norm()
            ffn()
        snap = P.snapshot()
        for ct in range(CT):
            P.op("sync", "dma",
                 out=AP(outT.tensor, b * D * T + ct * 128 * T, [[T, 128], [1, T]]),
                 in_=xtile(ct), wait=[(n, v) for n, v in snap.items()],
                 inc=("dO", 16))
        P.op("sync", "noop", wait=[("dO", P.cur("dO"))])
    P.op("sync", "noop", wait=[("dO", P.cur("dO"))])

    P.emit()
    nc._dbg_cnt = dict(P.cnt)
    return nc


def host_consts():
    I = np.eye(128, dtype=np.float32)
    sel = np.zeros((8, E * 128), np.float32)
    for e in range(E):
        sel[e, e * 128:(e + 1) * 128] = 1.0
    onesf = np.zeros((128, 384), np.float32)
    onesf[:, 0:64] = 1.0
    onesf[:, 128 + 64:256] = 1.0
    onesf[:, 256:384] = 1.0
    return I, sel, onesf


def make_inputs(inp, core):
    x = np.asarray(inp["L0_attn_a_x"])
    xb = np.ascontiguousarray(
        x[core * NB:(core + 1) * NB].transpose(0, 2, 1)).astype(np.float32, copy=False)
    I, sel, onesf = host_consts()
    rel = np.asarray(inp["L0_attn_b_rel_emb"])
    d = {
        "xT": xb,
        "Wa": np.ascontiguousarray(inp["L0_attn_a_W"]),
        "ba": np.ascontiguousarray(inp["L0_attn_a_b"]),
        "WqT": np.ascontiguousarray(np.asarray(inp["L0_attn_b_Wq"]).T),
        "WkT": np.ascontiguousarray(np.asarray(inp["L0_attn_b_Wk"]).T),
        "WvT": np.ascontiguousarray(np.asarray(inp["L0_attn_b_Wv"]).T),
        "WoT": np.ascontiguousarray(np.asarray(inp["L0_attn_b_Wo"]).T),
        "WpkT": np.ascontiguousarray(np.asarray(inp["L0_attn_b_Wpk"]).T),
        "WpqT": np.ascontiguousarray(np.asarray(inp["L0_attn_b_Wpq"]).T),
        "relT": np.ascontiguousarray(rel.T),
        "relrevT": np.ascontiguousarray(rel[::-1].T),
        "bq": np.ascontiguousarray(inp["L0_attn_b_bq"]),
        "bk": np.ascontiguousarray(inp["L0_attn_b_bk"]),
        "bv": np.ascontiguousarray(inp["L0_attn_b_bv"]),
        "bo": np.ascontiguousarray(inp["L0_attn_b_bo"]),
        "bpk": np.ascontiguousarray(inp["L0_attn_b_bpk"]),
        "bpq": np.ascontiguousarray(inp["L0_attn_b_bpq"]),
        "gwT": np.ascontiguousarray(np.asarray(inp["L0_ffn_gate_w"]).T),
        "WeT": np.ascontiguousarray(np.asarray(inp["L0_ffn_expert_w"]).transpose(0, 2, 1)),
        "I128c": I, "SEL8c": sel, "ONESF": onesf,
    }
    return {k: np.asarray(v, dtype=np.float32) for k, v in d.items()}


from concourse.bass_utils import run_bass_kernel_spmd

_NCORES = 8


def kernel(**inputs):
    nc = build(stages=(0, 1, 2, 3), n_batches=NB)
    in_maps = [make_inputs(inputs, c) for c in range(_NCORES)]
    res = run_bass_kernel_spmd(nc, in_maps, core_ids=list(range(_NCORES)))
    out = np.empty((16, T, D), dtype=np.float32)
    for c in range(_NCORES):
        o = np.asarray(res.results[c]["outT"])
        out[c * NB:(c + 1) * NB] = o.transpose(0, 2, 1)
    return out


# version 03
# speedup vs baseline: 1.2175x; 1.0250x over previous
import numpy as np
import concourse.bass as bass
import concourse.mybir as mybir
from concourse.bass import AP

F32 = mybir.dt.float32
BF16 = mybir.dt.bfloat16
AF = mybir.ActivationFunctionType
ALU = mybir.AluOpType
AX = mybir.AxisListType

T = 2048
D = 1024
NB = 2
HP = 8
S = 1024
SCALE = float(1.0 / np.sqrt(np.float32(64 * 3)))
E = 8
EPS = 1e-5
PADW = 1408
CT = D // 128
TT = T // 128
QCN = T // 512


class Prog:
    ENGS = ("sync", "tensor", "vector", "scalar", "gpsimd")

    def __init__(self, nc):
        self.nc = nc
        self.q = {e: [] for e in self.ENGS}
        self.sems = {}
        self.cnt = {}

    def _sem(self, name):
        if name not in self.sems:
            self.sems[name] = self.nc.alloc_semaphore(name)
            self.cnt[name] = 0
        return self.sems[name]

    def op(self, eng, kind, wait=(), inc=None, **kw):
        w = []
        for item in wait:
            if item is None:
                continue
            sn, val = item
            if val is None or val <= 0:
                continue
            self._sem(sn)
            assert val <= self.cnt[sn], f"wait {sn}>={val} but only {self.cnt[sn]}"
            w.append((sn, val))
        tag = None
        if inc is not None:
            sn, k = inc
            self._sem(sn)
            self.cnt[sn] += k
            tag = self.cnt[sn]
        self.q[eng].append((kind, kw, tuple(w), (inc[0], inc[1]) if inc else None))
        return tag

    def cur(self, name):
        self._sem(name)
        return self.cnt[name]

    def snapshot(self):
        return {n: self.cnt[n] for n in ("t", "v", "sc", "g") if n in self.cnt}

    def barrier(self):
        snap = dict(self.cnt)
        for eng in self.ENGS:
            waits = [(n, v) for n, v in snap.items()]
            self.op(eng, "noop", wait=waits)

    def emit(self):
        with self.nc.Block() as block:
            for ename in self.ENGS:
                ops = self.q[ename]
                if ops:
                    getattr(block, ename)(lambda e, ops=ops: self._run(e, ops))

    def _run(self, e, ops):
        for kind, kw, waits, inc in ops:
            for (sn, val) in waits:
                e.wait_ge(self.sems[sn], val)
            if kind == "noop":
                continue
            ins = getattr(self, "_" + kind)(e, **kw)
            if inc is not None:
                sn, k = inc
                ins.then_inc(self.sems[sn], k)

    def _mm(self, e, out, lhsT, rhs, start, stop, tp=None):
        if tp is not None:
            return e.matmul(out, lhsT, rhs, start=start, stop=stop, tile_position=tp)
        return e.matmul(out, lhsT, rhs, start=start, stop=stop)

    def _dma(self, e, out, in_, accum=False, transpose=False, slow=False):
        kw = {}
        if accum:
            kw["accum_op"] = ALU.add
        if transpose:
            kw["transpose"] = True
        if slow:
            kw["allow_slow_non_contiguous"] = True
        return e.dma_start(out=out, in_=in_, **kw)

    def _tt(self, e, out, a, b, alu):
        return e.tensor_tensor(out, a, b, alu)

    def _ts(self, e, out, in0, s1, s2=None, op0=ALU.mult, op1=None):
        if op1 is None:
            return e.tensor_single_scalar(out, in0, s1, op0)
        return e.tensor_scalar(out, in0, s1, s2, op0, op1)

    def _stt(self, e, out, in0, s, in1, op0, op1):
        return e.scalar_tensor_tensor(out, in0, s, in1, op0, op1)

    def _scan(self, e, out, d0, d1, init, op0, op1):
        return e.tensor_tensor_scan(out, d0, d1, init, op0, op1)

    def _copy(self, e, out, in_):
        if hasattr(e, "tensor_copy"):
            return e.tensor_copy(out, in_)
        return e.copy(out, in_)

    def _memset(self, e, ap, val):
        return e.memset(ap, val)

    def _act(self, e, out, in_, func, bias=0.0, scale=1.0, accum_out=None):
        return e.activation(out, in_, func, bias=bias, scale=scale,
                            accum_out=accum_out)

    def _reduce(self, e, out, in_, alu):
        return e.tensor_reduce(out, in_, AX.X, alu)

    def _recip(self, e, out, in_):
        return e.reciprocal(out, in_)


def build(stages=(0, 1, 2, 3), n_batches=NB, dbg=None):
    import os as _os
    ABLIM = float(_os.environ.get("ABLIM", "6"))
    SKIP_AA_TAIL = (dbg == "aa2")
    nc = bass.Bass("TRN2", target_bir_lowering=False, detect_race_conditions=False)

    xT = nc.dram_tensor("xT", [NB, D, T], F32, kind="ExternalInput").ap()
    outT = nc.dram_tensor("outT", [NB, D, T], F32, kind="ExternalOutput").ap()
    dbgT = nc.dram_tensor("dbgT", [4, D, T], F32, kind="ExternalOutput").ap()

    def dump(idx, src_ap_fn, nt=CT):
        snap2 = P.snapshot()
        for ct in range(nt):
            sap = src_ap_fn(ct)
            p, w = sap.shape
            P.op("gpsimd", "dma",
                 out=AP(dbgT.tensor, idx * D * T + ct * 128 * T, [[T, p], [1, w]]),
                 in_=sap, wait=[(n, v) for n, v in snap2.items()],
                 inc=("dOg", 16))
    Wa = nc.dram_tensor("Wa", [D, 3 * D], F32, kind="ExternalInput").ap()
    ba_d = nc.dram_tensor("ba", [3 * D], F32, kind="ExternalInput").ap()
    WqT = nc.dram_tensor("WqT", [D, D], F32, kind="ExternalInput").ap()
    WkT = nc.dram_tensor("WkT", [D, D], F32, kind="ExternalInput").ap()
    WvT = nc.dram_tensor("WvT", [D, D], F32, kind="ExternalInput").ap()
    WoT = nc.dram_tensor("WoT", [D, D], F32, kind="ExternalInput").ap()
    WpkT = nc.dram_tensor("WpkT", [D, D], F32, kind="ExternalInput").ap()
    WpqT = nc.dram_tensor("WpqT", [D, D], F32, kind="ExternalInput").ap()
    relT = nc.dram_tensor("relT", [D, S], F32, kind="ExternalInput").ap()
    relrevT = nc.dram_tensor("relrevT", [D, S], F32, kind="ExternalInput").ap()
    bq_d = nc.dram_tensor("bq", [D], F32, kind="ExternalInput").ap()
    bk_d = nc.dram_tensor("bk", [D], F32, kind="ExternalInput").ap()
    bv_d = nc.dram_tensor("bv", [D], F32, kind="ExternalInput").ap()
    bo_d = nc.dram_tensor("bo", [D], F32, kind="ExternalInput").ap()
    bpk_d = nc.dram_tensor("bpk", [D], F32, kind="ExternalInput").ap()
    bpq_d = nc.dram_tensor("bpq", [D], F32, kind="ExternalInput").ap()
    gwT = nc.dram_tensor("gwT", [D, E], F32, kind="ExternalInput").ap()
    WeT = nc.dram_tensor("WeT", [E, D, D], F32, kind="ExternalInput").ap()
    I128c = nc.dram_tensor("I128c", [128, 128], F32, kind="ExternalInput").ap()
    SEL8c = nc.dram_tensor("SEL8c", [8, E * 128], F32, kind="ExternalInput").ap()
    ONESF = nc.dram_tensor("ONESF", [128, 384], F32, kind="ExternalInput").ap()

    cpad_d = nc.dram_tensor("cpad_d", [2, 2, TT, 128, PADW], BF16, kind="Internal")
    pk_d = nc.dram_tensor("pk_d", [HP, 128, S], BF16, kind="Internal")
    pqr_d = nc.dram_tensor("pqr_d", [HP, 128, S], BF16, kind="Internal")
    ctxn_d = nc.dram_tensor("ctxn_d", [HP, 128, T], BF16, kind="Internal")

    P = Prog(nc)

    ARENA = 209920
    _arena = nc.alloc_sbuf_tensor("manual_arena", [128, ARENA], mybir.dt.uint8)
    SB_BASE = nc.lookup_mloc(_arena).addr

    def at(name, shape, dtype, off):
        esz = 2 if dtype == BF16 else 4
        per = int(np.prod(shape[1:])) * esz
        assert off + per <= ARENA, (name, off + per)
        return nc.alloc_sbuf_tensor_at(name, shape, dtype, offset=off + SB_BASE)

    I128 = at("I128", [128, 128], BF16, 0)
    ones_b = at("ones_b", [128, 128], BF16, 256)
    onesf = at("onesf", [128, 384], F32, 512)
    sel8 = at("sel8", [8, E * 128], BF16, 2048)
    ba_sb = at("ba_sb", [128, 24], F32, 4096)
    bq_sb = at("bq_sb", [128, 8], F32, 4192)
    bk_sb = at("bk_sb", [128, 8], F32, 4224)
    bo_sb = at("bo_sb", [128, 8], F32, 4256)
    bpk_sb = at("bpk_sb", [128, 8], F32, 4288)
    bpq_sb = at("bpq_sb", [128, 8], F32, 4320)
    bv_sb = at("bv_sb", [128, 8], BF16, 4352)
    gw_sb = at("gw_sb", [128, CT * E], BF16, 4384)
    ec2 = at("ec2", [128, 64], F32, 4544)
    rs_sb = at("rs_sb", [128, 512], F32, 4800)
    edlhsT = at("edlhsT", [128, 97], BF16, 6848)
    ban_sb = at("ban_sb", [128, 8], F32, 7072)
    lnm = at("lnm", [1, 512], F32, 7104)
    lnr = at("lnr", [1, 512], F32, 9152)
    lnt = at("lnt", [1, 512], F32, 11200)
    lnt2 = at("lnt2", [1, 512], F32, 13248)
    xb_buf = at("xb_buf", [128, CT * 512], BF16, 15296)
    sq_buf = at("sq_buf", [128, CT * 512], BF16, 23488)
    t1_buf = at("t1_buf", [128, 512], F32, 31680)
    XOFF = 33728
    x_sb = at("x_sb", [128, CT * 2048], F32, XOFF)
    xn_sb = at("xn_sb", [128, CT * 2048], BF16, XOFF + 65536)
    PH = XOFF + 65536 + 32768
    LIMIT = 212863

    wa_sb = at("wa_sb", [128, 6 * 1024], BF16, PH)
    f_buf = at("f_buf", [128, 2048], F32, PH + 12288)
    g_buf = at("g_buf", [128, 2048], F32, PH + 20480)
    og_buf = at("og_buf", [128, 2048], F32, PH + 28672)
    h_buf = at("h_buf", [128, 2048], F32, PH + 36864)
    ig_t = at("ig_t", [128, 512], F32, PH + 45056)
    in_t = at("in_t", [128, 512], F32, PH + 47104)
    xm_sb = at("xm_sb", [128, 2 * CT * 512], BF16, PH)
    we_sb = at("we_sb", [128, 2 * CT * 1024], BF16, PH + 16384)
    mba_sb = at("mba_sb", [128, E * 512], BF16, PH + 49152)
    mrow_sb = at("mrow_sb", [8, T], BF16, PH + 57344)
    msk_sb = at("msk_sb", [128, 8], BF16, PH + 61440)
    mx_t = at("mx_t", [128, 1], F32, PH + 61472)
    gs_t = at("gs_t", [128, 8], F32, PH + 61504)
    mskf_t = at("mskf_t", [128, 8], F32, PH + 61536)
    mxr_t = at("mxr_t", [128, 8], F32, PH + 71936)
    gsall = at("gsall", [128, 128], F32, PH + 72000)
    gscr = at("gscr", [128, 128], F32, PH + 72512)
    mtree = at("mtree", [128, 112], F32, PH + 73024)
    mskall = at("mskall", [128, 128], BF16, PH + 73536)
    relb_sb = at("relb_sb", [128, CT * 1024], BF16, PH + 16384)
    pre_sg = at("pre_sg", [128, 2 * 512], BF16, PH + 36864)
    qb_sb = at("qb_sb", [128, 2048], BF16, PH)
    kb_sb = at("kb_sb", [128, 2048], BF16, PH + 4096)
    pk_sb = at("pk_sb", [128, 1024], BF16, PH + 8192)
    pqr_sb = at("pqr_sb", [128, 1024], BF16, PH + 10240)
    v_sb = at("v_sb", [128, TT * 194], BF16, PH + 12288)
    cpad_sb = at("cpad_sb", [128, 4 * PADW], BF16, PH + 18496)
    band_sb = at("band_sb", [128, 4 * 1152], BF16, PH + 29760)
    pt_sb = at("pt_sb", [128, 4 * 512], BF16, PH + 38976)
    pt2_sb = at("pt2_sb", [128, 4 * 512], BF16, PH + 73792)
    ctmp_sb = at("ctmp_sb", [128, 512], F32, PH + 43072)
    ctxn_sb = at("ctxn_sb", [128, 2048], BF16, PH + 45120)
    cstr_sb = at("cstr_sb", [128, HP * 512], BF16, PH + 49216)
    edge_sb = at("edge_sb", [128, 2048], BF16, PH + 57408)
    wq_sb = at("wq_sb", [128, 3 * 1024], BF16, PH + 61568)
    wo_sb = at("wo_sb", [128, 2 * 1024], BF16, PH + 67712)
    wofull = at("wofull", [128, CT * 8 * 128], BF16, PH)

    pbT = [nc.alloc_psum_tensor(f"pb{i}", [128, 512], F32) for i in range(8)]
    pb = [t.ap() for t in pbT]

    def xtile(ct, lo=0, hi=T):
        return x_sb.ap()[:, ct * T + lo:ct * T + hi]

    def xntile(ct, lo=0, hi=T):
        return xn_sb.ap()[:, ct * T + lo:ct * T + hi]

    st = {}

    P.op("gpsimd", "dma", out=I128.ap(), in_=I128c, inc=("dCg", 16))
    P.op("gpsimd", "dma", out=sel8.ap(), in_=SEL8c, inc=("dCg", 16))
    P.op("sync", "dma", out=onesf.ap(), in_=ONESF, inc=("dC", 16))
    P.op("gpsimd", "memset", ap=ones_b.ap(), val=1.0, inc=("g", 1))
    _bias_specs = ((ba_d, ba_sb, 24, None), (bq_d, bq_sb, 8, SCALE),
                   (bk_d, bk_sb, 8, None), (bpk_d, bpk_sb, 8, None),
                   (bpq_d, bpq_sb, 8, SCALE), (bo_d, bo_sb, 8, None))
    for (src, dst, n, scl) in _bias_specs:
        P.op("sync", "dma", out=dst.ap()[:, 0:n],
             in_=AP(src.tensor, 0, [[1, 128], [128, n]]), slow=True,
             inc=("dC", 16))
    _dC_all = P.cur("dC")
    for (src, dst, n, scl) in _bias_specs:
        if scl is not None:
            P.op("vector", "ts", out=dst.ap()[:, 0:n], in0=dst.ap()[:, 0:n],
                 s1=scl, wait=[("dC", _dC_all)], inc=("v", 1))
    P.op("gpsimd", "dma", out=bv_sb.ap(),
         in_=AP(bv_d.tensor, 0, [[1, 128], [128, 8]]), slow=True, inc=("dCg", 16))
    P.op("gpsimd", "dma", out=gw_sb.ap().rearrange("p (a b) -> p a b", a=CT),
         in_=AP(gwT.tensor, 0, [[8, 128], [1024, CT], [1, 8]]), slow=True,
         inc=("dCg", 16))
    st["setup_dC"] = P.cur("dC")
    st["setup_dCg"] = P.cur("dCg")
    P.op("vector", "ts", out=ban_sb.ap(), in0=ba_sb.ap()[:, 8:16], s1=-1.0,
         wait=[("dC", st["setup_dC"])], inc=("v", 1))
    st["setup_g"] = P.cur("g")
    st["setup_v"] = P.cur("v")
    P.barrier()

    if 2 in stages:
        relb = relb_sb.ap()
        for mi, (wT, rel_src, dst_d, bias_sb, scl) in enumerate((
                (WpkT, relT, pk_d, bpk_sb, 1.0),
                (WpqT, relrevT, pqr_d, bpq_sb, SCALE))):
            rtg = P.op("gpsimd", "dma",
                       out=relb.rearrange("p (a b) -> p a b", a=CT),
                       in_=AP(rel_src.tensor, 0, [[S, 128], [128 * S, CT], [1, S]]),
                       wait=[("t", P.cur("t"))], inc=("dRel", 16))
            for ft in range(CT):
                ws = (ft % 2) * 1024
                wslot = wq_sb.ap()[:, ws:ws + 1024]
                wtg = P.op("gpsimd", "dma",
                           out=wslot.rearrange("p (a b) -> p a b", a=CT),
                           in_=AP(wT.tensor, ft * 128,
                                  [[D, 128], [128 * D, CT], [1, 128]]),
                           wait=[("t", st.get(f"pre{mi}w{ft % 2}", 0))],
                           inc=(f"dWp{ft % 2}", 16))
                for sc2 in range(2):
                    slot = (ft * 2 + sc2) % 2
                    bank = pb[slot]
                    for ct in range(CT):
                        P.op("tensor", "mm", out=bank,
                             lhsT=wslot[:, ct * 128:ct * 128 + 128],
                             rhs=relb[:, ct * 1024 + sc2 * 512:ct * 1024 + sc2 * 512 + 512],
                             start=(ct == 0), stop=(ct == CT - 1),
                             wait=([(f"dWp{ft % 2}", P.cur(f"dWp{ft % 2}")),
                                    ("dRel", P.cur("dRel")),
                                    ("v", st.get(f"preb{slot}", 0)),
                                    ("dC", st["setup_dC"]),
                                    ("dCg", st["setup_dCg"]),
                                    ("v", st["setup_v"])] if ct == 0 else []),
                             inc=(("t", 1) if ct == CT - 1 else None))
                    mmt = P.cur("t")
                    ev = P.op("vector", "stt",
                              out=pre_sg.ap()[:, slot * 512:slot * 512 + 512],
                              in0=bank, s=scl,
                              in1=bias_sb.ap()[:, ft:ft + 1].broadcast_to([128, 512]),
                              op0=ALU.mult, op1=ALU.add,
                              wait=[("t", mmt), ("dP", st.get(f"prew{slot}", 0))],
                              inc=("v", 1))
                    st[f"preb{slot}"] = ev
                    st[f"prew{slot}"] = P.op(
                        "sync", "dma",
                        out=AP(dst_d, ft * 128 * S + sc2 * 512, [[S, 128], [1, 512]]),
                        in_=pre_sg.ap()[:, slot * 512:slot * 512 + 512],
                        wait=[("v", ev)], inc=("dP", 16))
                st[f"pre{mi}w{ft % 2}"] = P.cur("t")
        st["pre_dP"] = P.cur("dP")
        P.barrier()

    def layer_norm():
        for qc in range(QCN):
            lo = qc * 512
            ctag = None
            for ct in range(CT):
                ctag = P.op("vector", "copy",
                            out=xb_buf.ap()[:, ct * 512:(ct + 1) * 512],
                            in_=xtile(ct, lo, lo + 512),
                            wait=([("t", st.get("ln_sums", 0))] if ct == 0 else []),
                            inc=("v", 1))
            sqtag = None
            for ct in range(CT):
                sqtag = P.op("scalar", "act",
                             out=sq_buf.ap()[:, ct * 512:(ct + 1) * 512],
                             in_=xtile(ct, lo, lo + 512), func=AF.Square,
                             wait=([("t", st.get("ln_sums", 0))] if ct == 0 else []),
                             inc=("sc", 1))
            for ct in range(CT):
                P.op("tensor", "mm", out=pb[0][0:1, :], lhsT=ones_b.ap()[:, 0:1],
                     rhs=xb_buf.ap()[:, ct * 512:(ct + 1) * 512],
                     start=(ct == 0), stop=(ct == CT - 1),
                     wait=([("v", ctag), ("v", st.get("ln_statv", 0))]
                           if ct == 0 else []),
                     inc=(("t", 1) if ct == CT - 1 else None))
            for ct in range(CT):
                P.op("tensor", "mm", out=pb[1][0:1, :], lhsT=ones_b.ap()[:, 0:1],
                     rhs=sq_buf.ap()[:, ct * 512:(ct + 1) * 512],
                     start=(ct == 0), stop=(ct == CT - 1),
                     wait=([("sc", sqtag)] if ct == 0 else []),
                     inc=(("t", 1) if ct == CT - 1 else None))
            st["ln_sums"] = P.cur("t")
            P.op("vector", "ts", out=lnm.ap(), in0=pb[0][0:1, :], s1=1.0 / D,
                 wait=[("t", st["ln_sums"])], inc=("v", 1))
            P.op("vector", "tt", out=lnt.ap(), a=lnm.ap(), b=lnm.ap(),
                 alu=ALU.mult, inc=("v", 1))
            vv = P.op("vector", "stt", out=lnt2.ap(), in0=pb[1][0:1, :], s=1.0 / D,
                      in1=lnt.ap(), op0=ALU.mult, op1=ALU.subtract, inc=("v", 1))
            vv = P.op("vector", "ts", out=lnt2.ap(), in0=lnt2.ap(), s1=EPS,
                      op0=ALU.add, inc=("v", 1))
            sd = P.op("scalar", "act", out=lnt.ap(), in_=lnt2.ap(), func=AF.Sqrt,
                      bias=0.0, wait=[("v", vv)], inc=("sc", 1))
            rr = P.op("vector", "recip", out=lnr.ap(), in_=lnt.ap(),
                      wait=[("sc", sd)], inc=("v", 1))
            mr = P.op("vector", "tt", out=lnt2.ap(), a=lnm.ap(), b=lnr.ap(),
                      alu=ALU.mult, inc=("v", 1))
            st["ln_statv"] = mr
            r1 = P.op("tensor", "mm", out=pb[2], lhsT=onesf.ap()[0:1, 256:384],
                      rhs=lnr.ap(), start=True, stop=True,
                      wait=[("v", mr), ("v", st.get("ln_normv", 0))], inc=("t", 1))
            r2 = P.op("tensor", "mm", out=pb[3], lhsT=onesf.ap()[0:1, 256:384],
                      rhs=lnt2.ap(), start=True, stop=True, inc=("t", 1))
            vn = None
            for ct in range(CT):
                P.op("vector", "tt", out=t1_buf.ap(), a=xtile(ct, lo, lo + 512),
                     b=pb[2], alu=ALU.mult, wait=[("t", r2)], inc=("v", 1))
                vn = P.op("vector", "tt", out=xntile(ct, lo, lo + 512),
                          a=t1_buf.ap(), b=pb[3], alu=ALU.subtract, inc=("v", 1))
            st["ln_normv"] = vn
        st["ln_done_v"] = P.cur("v")
        P.barrier()

    def attn_a():
        wtags = {}

        def load_w(g):
            for fam in range(3):
                slot = fam * 2 + (g % 2)
                wtags[(g, fam)] = P.op(
                    "gpsimd", "dma",
                    out=wa_sb.ap()[:, slot * 1024:(slot + 1) * 1024]
                    .rearrange("p (a b) -> p a b", a=CT),
                    in_=AP(Wa.tensor, fam * 1024 + g * 128,
                           [[3 * D, 128], [128 * 3 * D, CT], [1, 128]]),
                    wait=[("t", st.get(f"aa_mm_g{g - 2}", 0))],
                    inc=(f"dWa{g % 2}", 16))

        load_w(0)
        load_w(1)
        for g in range(CT):
            for tc in range(QCN):
                bs = 0 if (g * QCN + tc) % 2 == 0 else 3
                for fam in range(3):
                    slot = fam * 2 + (g % 2)
                    for ct in range(CT):
                        P.op("tensor", "mm", out=pb[bs + fam],
                             lhsT=wa_sb.ap()[:, slot * 1024 + ct * 128:
                                             slot * 1024 + ct * 128 + 128],
                             rhs=xntile(ct, tc * 512, tc * 512 + 512),
                             start=(ct == 0), stop=(ct == CT - 1),
                             wait=([(f"dWa{g % 2}", P.cur(f"dWa{g % 2}")),
                                    ("sc", st.get(f"aa_ev{bs}", 0)),
                                    ("v", st.get(f"aa_gv{bs}", 0))]
                                   if ct == 0 else []),
                             inc=(("t", 1) if ct == CT - 1 else None))
                mmt = P.cur("t")
                st[f"aa_mm_g{g}"] = mmt
                lo = tc * 512
                w0 = [("t", mmt), ("v", st.get("aa_gv_last", 0))]
                if tc == 0:
                    w0.append(("v", st.get("aa_gdone", 0)))
                P.op("scalar", "act", out=in_t.ap(), in_=pb[bs + 0], func=AF.Tanh,
                     bias=ba_sb.ap()[:, g:g + 1], wait=w0, inc=("sc", 1))
                P.op("scalar", "act", out=ig_t.ap(), in_=pb[bs + 1], func=AF.Sigmoid,
                     bias=ba_sb.ap()[:, 8 + g:8 + g + 1], inc=("sc", 1))
                P.op("scalar", "act", out=f_buf.ap()[:, lo:lo + 512], in_=pb[bs + 1],
                     func=AF.Sigmoid, scale=-1.0, bias=ban_sb.ap()[:, g:g + 1],
                     inc=("sc", 1))
                ev3 = P.op("scalar", "act", out=og_buf.ap()[:, lo:lo + 512],
                           in_=pb[bs + 2], func=AF.Sigmoid,
                           bias=ba_sb.ap()[:, 16 + g:16 + g + 1], inc=("sc", 1))
                st[f"aa_ev{bs}"] = ev3
                gv = P.op("vector", "tt", out=g_buf.ap()[:, lo:lo + 512],
                          a=in_t.ap(), b=ig_t.ap(), alu=ALU.mult,
                          wait=[("sc", ev3)], inc=("v", 1))
                st[f"aa_gv{bs}"] = gv
                st["aa_gv_last"] = gv
            if g + 2 < CT:
                load_w(g + 2)
            scv = P.op("vector", "scan", out=h_buf.ap(), d0=f_buf.ap(),
                       d1=g_buf.ap(), init=0.0, op0=ALU.mult, op1=ALU.add,
                       wait=[("sc", st.get("aa_thp", 0))], inc=("v", 1))
            if SKIP_AA_TAIL:
                continue
            th = P.op("scalar", "act", out=f_buf.ap(), in_=h_buf.ap(),
                      func=AF.Tanh, wait=[("v", scv)], inc=("sc", 1))
            st["aa_thp"] = th
            P.op("vector", "tt", out=og_buf.ap(), a=f_buf.ap(), b=og_buf.ap(),
                 alu=ALU.mult, wait=[("sc", th)], inc=("v", 1))
            st["aa_gdone"] = P.op("vector", "tt", out=xtile(g), a=xtile(g),
                                  b=og_buf.ap(), alu=ALU.add, inc=("v", 1))
        P.barrier()

    wecnt = [0]

    def ffn():
        for tt in range(TT):
            bank = pb[6] if tt % 2 == 0 else pb[7]
            for ct in range(CT):
                P.op("tensor", "mm", out=bank[:, 0:8],
                     lhsT=xntile(ct, tt * 128, tt * 128 + 128),
                     rhs=gw_sb.ap()[:, ct * E:(ct + 1) * E],
                     start=(ct == 0), stop=(ct == CT - 1),
                     wait=([("v", st.get(f"ffn_gv{tt % 2}", 0)),
                            ("dCg", st["setup_dCg"])] if ct == 0 else []),
                     inc=(("t", 1) if ct == CT - 1 else None))
            gt = P.cur("t")
            gv = P.op("vector", "copy", out=gsall.ap()[:, tt * 8:(tt + 1) * 8],
                      in_=bank[:, 0:8], wait=[("t", gt)], inc=("v", 1))
            st[f"ffn_gv{tt % 2}"] = gv
        ga = gsall
        mt = mtree
        P.op("vector", "copy", out=gscr.ap(), in_=gsall.ap(), inc=("v", 1))
        P.op("vector", "tt",
             out=mtree.ap()[:, 0:64].rearrange("p (a b) -> p a b", a=16),
             a=AP(ga, 0, [[128, 128], [8, 16], [1, 4]]),
             b=AP(ga, 4, [[128, 128], [8, 16], [1, 4]]),
             alu=ALU.max, inc=("v", 1))
        P.op("vector", "copy", out=gscr.ap(), in_=gsall.ap(), inc=("v", 1))
        P.op("vector", "tt",
             out=mtree.ap()[:, 64:96].rearrange("p (a b) -> p a b", a=16),
             a=AP(mt, 0, [[112, 128], [4, 16], [1, 2]]),
             b=AP(mt, 2, [[112, 128], [4, 16], [1, 2]]),
             alu=ALU.max, inc=("v", 1))
        P.op("vector", "copy", out=gscr.ap(), in_=gsall.ap(), inc=("v", 1))
        P.op("vector", "tt", out=mtree.ap()[:, 96:112],
             a=AP(mt, 64, [[112, 128], [2, 16], [1, 1]]),
             b=AP(mt, 65, [[112, 128], [2, 16], [1, 1]]),
             alu=ALU.max, inc=("v", 1))
        P.op("vector", "copy", out=gscr.ap(), in_=gsall.ap(), inc=("v", 1))
        P.op("vector", "copy",
             out=gscr.ap().rearrange("p (a b) -> p a b", a=16),
             in_=AP(mt, 96, [[112, 128], [1, 16], [0, 8]]), inc=("v", 1))
        P.op("vector", "copy", out=mtree.ap()[:, 0:64], in_=mtree.ap()[:, 0:64],
             inc=("v", 1))
        P.op("vector", "tt", out=gsall.ap(), a=gsall.ap(), b=gscr.ap(),
             alu=ALU.is_ge, inc=("v", 1))
        P.op("vector", "copy", out=gscr.ap()[:, 0:64], in_=mtree.ap()[:, 0:64],
             inc=("v", 1))
        mv = P.op("vector", "copy", out=mskall.ap(), in_=gsall.ap(), inc=("v", 1))
        for tt in range(TT):
            bank2 = pb[4] if tt % 2 == 0 else pb[5]
            tr = P.op("tensor", "mm", out=bank2[0:8, 0:128],
                      lhsT=mskall.ap()[:, tt * 8:(tt + 1) * 8],
                      rhs=I128.ap(), start=True, stop=True,
                      wait=[("v", mv), ("sc", st.get(f"ffn_mev{tt % 2}", 0)),
                            ("dCg", st["setup_dCg"])],
                      inc=("t", 1))
            st[f"ffn_mev{tt % 2}"] = P.op(
                "scalar", "copy", out=mrow_sb.ap()[:, tt * 128:(tt + 1) * 128],
                in_=bank2[0:8, 0:128], wait=[("t", tr)], inc=("sc", 1))
        mrow_done = P.cur("sc")

        for tc in range(QCN):
            lo = tc * 512
            for e in range(E):
                bank = pb[6] if e % 2 == 0 else pb[7]
                mmt = P.op("tensor", "mm", out=bank,
                           lhsT=sel8.ap()[:, e * 128:(e + 1) * 128],
                           rhs=mrow_sb.ap()[:, lo:lo + 512], start=True, stop=True,
                           wait=[("sc", mrow_done),
                                 ("sc", st.get(f"ffn_mbev{e % 2}", 0)),
                                 ("v", st.get("ffn_evacv", 0))],
                           inc=("t", 1))
                st[f"ffn_mbev{e % 2}"] = P.op(
                    "scalar", "copy", out=mba_sb.ap()[:, e * 512:(e + 1) * 512],
                    in_=bank, wait=[("t", mmt)], inc=("sc", 1))
            mb_done = P.cur("sc")
            for e in range(E):
                xo = (e % 2) * CT * 512
                xmv = None
                for ct in range(CT):
                    xmv = P.op("vector", "tt",
                               out=xm_sb.ap()[:, xo + ct * 512:xo + (ct + 1) * 512],
                               a=xntile(ct, lo, lo + 512),
                               b=mba_sb.ap()[:, e * 512:(e + 1) * 512],
                               alu=ALU.mult,
                               wait=([("sc", mb_done),
                                      ("t", st.get(f"ffn_xm{e % 2}", 0))]
                                     if ct == 0 else []),
                               inc=("v", 1))
                wet = P.op("gpsimd", "dma",
                           out=we_sb.ap()[:, (e % 2) * CT * 1024:
                                          ((e % 2) + 1) * CT * 1024]
                           .rearrange("p (a b) -> p a b", a=CT),
                           in_=AP(WeT.tensor, e * D * D,
                                  [[D, 128], [128 * D, CT], [1, D]]),
                           wait=[("t", st.get(f"ffn_xm{e % 2}", 0))],
                           inc=(f"dWe{e % 2}", 16))
                for ft in range(CT):
                    for ct in range(CT):
                        w = []
                        if ct == 0 and ft == 0:
                            w = [("v", xmv), (f"dWe{e % 2}", P.cur(f"dWe{e % 2}"))]
                            if e == 0:
                                w.append(("v", st.get("ffn_evacv", 0)))
                        elif ct == 0 and e == 0:
                            w = [("v", st.get("ffn_evacv", 0))]
                        P.op("tensor", "mm", out=pb[ft],
                             lhsT=we_sb.ap()[:, (e % 2) * CT * 1024 + ct * 1024 +
                                             ft * 128:(e % 2) * CT * 1024 +
                                             ct * 1024 + ft * 128 + 128],
                             rhs=xm_sb.ap()[:, xo + ct * 512:xo + (ct + 1) * 512],
                             start=(e == 0 and ct == 0),
                             stop=(e == E - 1 and ct == CT - 1),
                             wait=w,
                             inc=(("t", 1) if ct == CT - 1 else None))
                st[f"ffn_xm{e % 2}"] = P.cur("t")
            expt = P.cur("t")
            for ft in range(CT):
                P.op("vector", "stt", out=t1_buf.ap(), in0=pb[ft], s=0.0,
                     in1=xntile(ft, lo, lo + 512), op0=ALU.add, op1=ALU.add,
                     wait=[("t", expt)], inc=("v", 1))
                P.op("vector", "tt", out=xtile(ft, lo, lo + 512),
                     a=xtile(ft, lo, lo + 512), b=t1_buf.ap(), alu=ALU.add,
                     inc=("v", 1))
            st["ffn_evacv"] = P.cur("v")
        P.barrier()

    def win(kt):
        k0 = kt * 128
        return max(0, k0 - 512), min(T, k0 + 640)

    def attn_b(b):
        for hp in range(HP):
            attn_b_hp(b, hp)
        P.barrier()
        if ABLIM <= 5:
            return
        st["wo_prev_t"] = P.cur("t")
        for tc in range(QCN):
            lo = tc * 512
            cs = 0
            ctag = None
            for hp in range(HP):
                ctag = P.op("sync", "dma",
                            out=cstr_sb.ap()[:, cs + hp * 512:cs + (hp + 1) * 512],
                            in_=AP(ctxn_d, hp * 128 * T + lo, [[T, 128], [1, 512]]),
                            wait=([("v", st.get("wo_used_last", 0)),
                                   ("dCXo", P.cur("dCXo"))] if hp == 0 else []),
                            inc=(f"dCX{tc % 2}", 16))
            if tc == 0:
                st["wo_wf"] = P.op(
                    "gpsimd", "dma",
                    out=wofull.ap().rearrange("p (a b c) -> p a b c", a=8, b=CT),
                    in_=AP(WoT.tensor, 0,
                           [[D, 128], [128 * D, 8], [128, CT], [1, 128]]),
                    wait=[("t", st.get("wo_prev_t", 0))],
                    inc=("dWof", 16))
            for ft in range(CT):
                ii = tc * CT + ft
                bank = pb[6] if ft % 2 == 0 else pb[7]
                for hp in range(HP):
                    P.op("tensor", "mm", out=bank,
                         lhsT=wofull.ap()[:, (hp * CT + ft) * 128:
                                          (hp * CT + ft) * 128 + 128],
                         rhs=cstr_sb.ap()[:, cs + hp * 512:cs + (hp + 1) * 512],
                         start=(hp == 0), stop=(hp == HP - 1),
                         wait=([("dWof", P.cur("dWof")),
                                (f"dCX{tc % 2}", P.cur(f"dCX{tc % 2}")),
                                ("v", st.get(f"wo_ev{ft % 2}", 0))]
                               if hp == 0 else []),
                         inc=(("t", 1) if hp == HP - 1 else None))
                mmt = P.cur("t")
                st[f"wo_mm{ft}"] = mmt
                st[f"wo_mmi{ii}"] = mmt
                if b == 0 and tc == 0:
                    for hp2 in range(HP):
                        P.op("tensor", "mm", out=pb[5][:, 0:1],
                             lhsT=wofull.ap()[:, (hp2 * CT + ft) * 128:
                                              (hp2 * CT + ft) * 128 + 128],
                             rhs=bv_sb.ap()[:, hp2:hp2 + 1],
                             start=(hp2 == 0), stop=(hp2 == HP - 1),
                             wait=([("v", st.get("bo_v", 0)),
                                    ("dCg", st["setup_dCg"])] if hp2 == 0 else []),
                             inc=(("t", 1) if hp2 == HP - 1 else None))
                    bot = P.cur("t")
                    st["bo_v"] = P.op("vector", "tt", out=bo_sb.ap()[:, ft:ft + 1],
                                      a=bo_sb.ap()[:, ft:ft + 1], b=pb[5][:, 0:1],
                                      alu=ALU.add, wait=[("t", bot)], inc=("v", 1))
                    st[f"bo_rdy{ft}"] = st["bo_v"]
                P.op("vector", "tt", out=t1_buf.ap(), a=bank,
                     b=bo_sb.ap()[:, ft:ft + 1].broadcast_to([128, 512]),
                     alu=ALU.add,
                     wait=[("t", mmt),
                           (("v", st.get(f"bo_rdy{ft}", 0)) if b == 0 else None)],
                     inc=("v", 1))
                ev = P.op("vector", "tt", out=xtile(ft, lo, lo + 512),
                          a=xtile(ft, lo, lo + 512), b=t1_buf.ap(), alu=ALU.add,
                          inc=("v", 1))
                st[f"wo_ev{ft % 2}"] = ev
            st["wo_used_last"] = P.cur("v")
        P.barrier()

    def attn_b_hp(b, hp):
        if hp == 0:
            P.op("gpsimd", "memset", ap=v_sb.ap(), val=0.0, inc=("g", 1))
            for kt in range(TT):
                b0 = kt * 194
                P.op("gpsimd", "memset", ap=v_sb.ap()[:, b0 + 64:b0 + 65], val=1.0,
                     inc=("g", 1))
                P.op("gpsimd", "memset", ap=v_sb.ap()[:, b0 + 98:b0 + 99], val=1.0,
                     inc=("g", 1))
            st["vconst_g"] = P.cur("g")
        wq_tags = []
        for i, wT in enumerate((WqT, WkT, WvT)):
            wq_tags.append(P.op(
                "gpsimd", "dma",
                out=wq_sb.ap()[:, i * 1024:(i + 1) * 1024]
                .rearrange("p (a b) -> p a b", a=CT),
                in_=AP(wT.tensor, hp * 128, [[D, 128], [128 * D, CT], [1, 128]]),
                wait=[("t", st.get(f"ab_qkv{hp - 1}", 0))], inc=("dWq", 16)))
        pkt = P.op("sync", "dma", out=pk_sb.ap(),
                   in_=AP(pk_d, hp * 128 * S, [[S, 128], [1, S]]),
                   wait=[("t", st.get(f"ab_cc{hp - 1}", 0)),
                         ("v", st.get(f"ab_ed{hp - 1}", 0))], inc=("dPK", 16))
        pqt = P.op("sync", "dma", out=pqr_sb.ap(),
                   in_=AP(pqr_d, hp * 128 * S, [[S, 128], [1, S]]),
                   inc=("dPK", 16))
        for (wi, dst, bias, scl) in ((0, qb_sb, bq_sb, SCALE), (1, kb_sb, bk_sb, 1.0)):
            for tc in range(QCN):
                bank = pb[6] if (wi * QCN + tc) % 2 == 0 else pb[7]
                for ct in range(CT):
                    P.op("tensor", "mm", out=bank,
                         lhsT=wq_sb.ap()[:, wi * 1024 + ct * 128:
                                         wi * 1024 + ct * 128 + 128],
                         rhs=xntile(ct, tc * 512, tc * 512 + 512),
                         start=(ct == 0), stop=(ct == CT - 1),
                         wait=([("dWq", P.cur("dWq")),
                                ("v", st.get(f"ab_pev{(wi * QCN + tc) % 2}", 0)),
                                ("t", st.get(f"ab_qkuse{hp - 1}", 0))]
                               if ct == 0 else []),
                         inc=(("t", 1) if ct == CT - 1 else None))
                mmt = P.cur("t")
                st[f"ab_pev{(wi * QCN + tc) % 2}"] = P.op(
                    "vector", "stt", out=dst.ap()[:, tc * 512:(tc + 1) * 512],
                    in0=bank, s=scl,
                    in1=bias.ap()[:, hp:hp + 1].broadcast_to([128, 512]),
                    op0=ALU.mult, op1=ALU.add,
                    wait=[("t", mmt)], inc=("v", 1))
        for kt in range(TT):
            bank = pb[6] if kt % 2 == 0 else pb[7]
            for ct in range(CT):
                P.op("tensor", "mm", out=bank[:, 0:128],
                     lhsT=xntile(ct, kt * 128, kt * 128 + 128),
                     rhs=wq_sb.ap()[:, 2 * 1024 + ct * 128:2 * 1024 + ct * 128 + 128],
                     start=(ct == 0), stop=(ct == CT - 1),
                     wait=([("dWq", P.cur("dWq")), ("g", st["vconst_g"]),
                            ("sc", st.get(f"ab_pevV{kt % 2}", 0)),
                            ("v", st.get(f"ab_pev{kt % 2}", 0)),
                            ("t", st.get(f"ab_vuse{hp - 1}", 0))]
                           if ct == 0 else []),
                     inc=(("t", 1) if ct == CT - 1 else None))
            mmt = P.cur("t")
            b0 = kt * 194
            P.op("scalar", "copy", out=v_sb.ap()[:, b0:b0 + 64], in_=bank[:, 0:64],
                 wait=[("t", mmt)], inc=("sc", 1))
            st[f"ab_pevV{kt % 2}"] = P.op(
                "scalar", "copy", out=v_sb.ap()[:, b0 + 130:b0 + 194],
                in_=bank[:, 64:128], inc=("sc", 1))
        st[f"ab_qkv{hp}"] = P.cur("t")
        qkv_sc = P.cur("sc")
        if ABLIM <= 1:
            return

        edv = P.op("vector", "memset", ap=edlhsT.ap(), val=0.0,
                   wait=[("dPK", P.cur("dPK")),
                         ("t", st.get(f"ab_eduse{hp - 1}", 0))],
                   inc=("v", 1))
        P.op("vector", "copy", out=edlhsT.ap()[0:64, 0:1], in_=pk_sb.ap()[0:64, 0:1],
             inc=("v", 1))
        P.op("vector", "copy", out=edlhsT.ap()[0:64, 32:33],
             in_=pk_sb.ap()[0:64, 1023:1024], inc=("v", 1))
        P.op("vector", "copy", out=edlhsT.ap()[64:128, 64:65],
             in_=pk_sb.ap()[64:128, 0:1], inc=("v", 1))
        edv = P.op("vector", "copy", out=edlhsT.ap()[64:128, 96:97],
                   in_=pk_sb.ap()[64:128, 1023:1024], inc=("v", 1))
        for qc in range(QCN if ABLIM >= 3 else 0):
            emm = P.op("tensor", "mm", out=pb[4][0:97, :], lhsT=edlhsT.ap(),
                       rhs=qb_sb.ap()[:, qc * 512:(qc + 1) * 512],
                       start=True, stop=True,
                       wait=[("v", edv), ("sc", st.get("ab_edev", 0)),
                             ("sc", qkv_sc)],
                       inc=("t", 1))
            st["ab_edev"] = P.op("scalar", "copy",
                                 out=edge_sb.ap()[0:97, qc * 512:(qc + 1) * 512],
                                 in_=pb[4][0:97, :], wait=[("t", emm)], inc=("sc", 1))
        st[f"ab_ed{hp}"] = P.cur("v")
        ed_sc = P.cur("sc")
        if ABLIM <= 2.2:
            return

        wrtags = {}
        for mi in range(2):
            lb = qb_sb if mi == 0 else kb_sb
            rb = pk_sb if mi == 0 else pqr_sb
            for qt in range(TT):
                slot2 = qt % 2
                for h in range(2):
                    slot = slot2 * 2 + h
                    soff = slot * PADW
                    li = mi * TT + qt
                    for sc2 in range(2):
                        bank = pb[h * 2 + sc2]
                        P.op("tensor", "mm", out=bank,
                             lhsT=lb.ap()[h * 64:h * 64 + 64,
                                          qt * 128:qt * 128 + 128],
                             rhs=rb.ap()[h * 64:h * 64 + 64,
                                         sc2 * 512:sc2 * 512 + 512],
                             start=True, stop=True,
                             wait=([("sc", qkv_sc), ("dPK", P.cur("dPK")),
                                    st.get(f"ab_cbank{h}", None),
                                    ("v", st.get(f"ab_cpv{slot}", 0)),
                                    ("sc", st.get(f"ab_cpev{slot}", 0)),
                                    ("v", st.get("ab_ctxnv", 0)),
                                    (f"dP{slot}", P.cur(f"dP{slot}"))]
                                   if sc2 == 0 else []),
                             inc=("t", 1))
                    mmt = P.cur("t")
                    if ABLIM <= 2.3:
                        continue
                    if mi == 0:
                        P.op("vector", "copy",
                             out=cpad_sb.ap()[:, soff + 256:soff + 256 + 512],
                             in_=pb[h * 2], wait=[("t", mmt)], inc=("v", 1))
                        cpv2 = P.op("vector", "copy",
                                    out=cpad_sb.ap()[:, soff + 768:soff + 768 + 512],
                                    in_=pb[h * 2 + 1], inc=("v", 1))
                        P.op("vector", "copy",
                             out=cpad_sb.ap()[:, soff:soff + 256],
                             in_=cpad_sb.ap()[:, soff + 256:soff + 257]
                             .broadcast_to([128, 256]),
                             wait=[("v", cpv2)], inc=("v", 1))
                        pv = P.op("vector", "copy",
                                  out=cpad_sb.ap()[:, soff + 1280:soff + 1408],
                                  in_=cpad_sb.ap()[:, soff + 1279:soff + 1280]
                                  .broadcast_to([128, 128]), inc=("v", 1))
                        st[f"ab_cpv{slot}"] = pv
                        st[f"ab_cbank{h}"] = ("v", pv)
                        wdep = ("v", pv)
                    else:
                        P.op("scalar", "copy",
                             out=cpad_sb.ap()[:, soff + 256:soff + 256 + 512],
                             in_=pb[h * 2], wait=[("t", mmt)], inc=("sc", 1))
                        P.op("scalar", "copy",
                             out=cpad_sb.ap()[:, soff + 768:soff + 768 + 512],
                             in_=pb[h * 2 + 1], inc=("sc", 1))
                        P.op("scalar", "copy", out=ec2.ap()[:, h * 32 + 2 * qt:
                                                            h * 32 + 2 * qt + 1],
                             in_=pb[h * 2][:, 0:1], inc=("sc", 1))
                        P.op("scalar", "copy", out=ec2.ap()[:, h * 32 + 2 * qt + 1:
                                                            h * 32 + 2 * qt + 2],
                             in_=pb[h * 2 + 1][:, 511:512], inc=("sc", 1))
                        P.op("scalar", "copy",
                             out=cpad_sb.ap()[:, soff:soff + 256],
                             in_=pb[h * 2][:, 0:1].broadcast_to([128, 256]),
                             inc=("sc", 1))
                        pv = P.op("scalar", "copy",
                                  out=cpad_sb.ap()[:, soff + 1280:soff + 1408],
                                  in_=pb[h * 2 + 1][:, 511:512]
                                  .broadcast_to([128, 128]), inc=("sc", 1))
                        st[f"ab_cpev{slot}"] = pv
                        st[f"ab_cbank{h}"] = ("sc", pv)
                        wdep = ("sc", pv)
                    if ABLIM <= 2.4:
                        continue
                    wrtags[(li, h)] = P.op(
                        "sync", "dma",
                        out=AP(cpad_d, ((mi * 2 + h) * TT + qt) * 128 * PADW,
                               [[PADW, 128], [1, PADW]]),
                        in_=cpad_sb.ap()[:, soff:soff + PADW],
                        wait=[wdep], inc=(f"dP{slot}", 16))
        if ABLIM <= 3:
            return
        cc_sc, cc_v = P.cur("sc"), P.cur("v")

        for half in range(2):
            qlo_h, qhi_h = half * 1024, half * 1024 + 1024

            def emit_bands(kt):
                A, B = win(kt)
                a, bb = max(A, qlo_h), min(B, qhi_h)
                if a >= bb:
                    return None
                k0 = kt * 128
                for h in range(2):
                    bslot = h * 2 + (kt % 2)
                    boff = bslot * 1152
                    src = AP(cpad_d,
                             (0 * 2 + h) * TT * 128 * PADW +
                             a * (PADW - 1) + k0 + 768,
                             [[PADW - 1, bb - a], [1, 128]])
                    wdeps = []
                    for qt in range(a // 128, (bb - 1) // 128 + 1):
                        li = 0 * TT + qt
                        slot_w = (qt % 2) * 2 + h
                        wdeps.append((f"dP{slot_w}", wrtags.get((li, h), 0)))
                    wdeps.append(("t", st.get(f"ab_buse{bslot}", 0)))
                    P.op("sync", "dma",
                         out=band_sb.ap()[:, boff:boff + (bb - a)],
                         in_=src, transpose=True,
                         wait=wdeps, inc=(f"dB{kt % 2}", 16))
                    src2 = AP(cpad_d,
                              (1 * 2 + h) * TT * 128 * PADW +
                              k0 * (PADW - 1) + a + 767,
                              [[PADW - 1, 128], [1, bb - a]])
                    li2 = 1 * TT + kt
                    slot_w2 = (kt % 2) * 2 + h
                    P.op("gpsimd", "dma",
                         out=band_sb.ap()[:, boff:boff + (bb - a)],
                         in_=src2, accum=True,
                         wait=[(f"dB{kt % 2}", P.cur(f"dB{kt % 2}")),
                               (f"dP{slot_w2}", wrtags.get((li2, h), 0))],
                         inc=(f"dB2{kt % 2}", 16))
                return (a, bb)

            def emit_qk_exp(kt, ab):
                A, B = win(kt)
                for qci in range(2):
                    qc = half * 2 + qci
                    qlo = qc * 512
                    for h in range(2):
                        sbank = pb[(kt % 2) * 2 + h]
                        adds = []
                        if ab is not None:
                            a, bb = ab
                            a2, b2 = max(a, qlo), min(bb, qlo + 512)
                            if a2 < b2:
                                adds.append(("band", a2, b2, h * 2 + (kt % 2)))
                        ecut1 = min(max(A, qlo), qlo + 512)
                        ecut2 = min(max(B, qlo), qlo + 512)
                        if qlo < ecut1:
                            adds.append(("edgeR", qlo, ecut1, None))
                        if ecut2 < qlo + 512:
                            adds.append(("edgeL", ecut2, qlo + 512, None))
                        qkw = [("sc", st.get(f"ab_sbank{h * 2 + (kt % 2)}", 0))]
                        if kt == 1:
                            qkw.append(("v", st.get(f"ab_repuse{h}", 0)))
                        if kt < 2:
                            qkw.append(("sc", cc_sc))
                            qkw.append(("v", cc_v))
                        P.op("tensor", "mm", out=sbank,
                             lhsT=kb_sb.ap()[h * 64:h * 64 + 64,
                                             kt * 128:kt * 128 + 128],
                             rhs=qb_sb.ap()[h * 64:h * 64 + 64, qlo:qlo + 512],
                             start=True, stop=(len(adds) == 0),
                             wait=qkw, inc=("t", 1))
                        for ai, (knd, ra, rb2, bs_) in enumerate(adds):
                            lastf = (ai == len(adds) - 1)
                            if knd == "band":
                                boff = bs_ * 1152
                                P.op("tensor", "mm",
                                     out=sbank[:, ra - qlo:rb2 - qlo],
                                     lhsT=I128.ap(),
                                     rhs=band_sb.ap()[:, boff + ra - ab[0]:
                                                      boff + rb2 - ab[0]],
                                     start=False, stop=lastf,
                                     wait=[(f"dB2{kt % 2}",
                                            P.cur(f"dB2{kt % 2}")),
                                           ("dCg", st["setup_dCg"])],
                                     inc=("t", 1))
                            else:
                                r = (32 if knd == "edgeR" else 0) + h * 64
                                tp = (96, 0) if r == 96 else None
                                P.op("tensor", "mm",
                                     out=sbank[:, ra - qlo:rb2 - qlo],
                                     lhsT=ones_b.ap()[r:r + 1, 0:128],
                                     rhs=edge_sb.ap()[r:r + 1, ra:rb2],
                                     start=False, stop=lastf, tp=tp,
                                     wait=[("sc", ed_sc)], inc=("t", 1))
                        smm = P.cur("t")
                        st[f"ab_buse{h * 2 + (kt % 2)}"] = smm
                        pslot = (kt % 2) * 4 + qci * 2 + h
                        ptbuf = pt_sb if pslot < 4 else pt2_sb
                        poff = (pslot % 4) * 512
                        edeps = [("t", smm), ("t", st.get(f"ab_ptuse{pslot}", 0))]
                        cut1 = min(max(A, qlo), qlo + 512)
                        cut2 = min(max(B, qlo), qlo + 512)
                        segs = []
                        if qlo < cut1:
                            segs.append((qlo, cut1,
                                         ec2.ap()[:, h * 32 + 2 * kt:
                                                  h * 32 + 2 * kt + 1]))
                        if cut1 < cut2:
                            segs.append((cut1, cut2, 0.0))
                        if cut2 < qlo + 512:
                            segs.append((cut2, qlo + 512,
                                         ec2.ap()[:, h * 32 + 2 * kt + 1:
                                                  h * 32 + 2 * kt + 2]))
                        lastt = None
                        for (ra, rb_, bias) in segs:
                            lastt = P.op("scalar", "act",
                                         out=ptbuf.ap()[:, poff + ra - qlo:
                                                        poff + rb_ - qlo],
                                         in_=sbank[:, ra - qlo:rb_ - qlo],
                                         func=AF.Exp, bias=bias,
                                         wait=edeps, inc=("sc", 1))
                            edeps = []
                        st[f"ab_sbank{h * 2 + (kt % 2)}"] = lastt
                        st[("expk", kt, qci, h)] = lastt

            def emit_ctx(ktc):
                for qci in range(2):
                    for h in range(2):
                        cb = pb[4 + qci * 2 + h]
                        vb = v_sb.ap()
                        if h == 0:
                            lhsT = vb[:, ktc * 194:ktc * 194 + 65]
                        else:
                            lhsT = vb[:, ktc * 194 + 66:ktc * 194 + 194]
                        pslot = (ktc % 2) * 4 + qci * 2 + h
                        ptbuf = pt_sb if pslot < 4 else pt2_sb
                        poff = (pslot % 4) * 512
                        ctxw = [("sc", st[("expk", ktc, qci, h)])]
                        if ktc == 0:
                            ctxw.append(("sc", st.get(f"ab_ctmp{qci * 2 + h}", 0)))
                            ctxw.append(("sc", ed_sc))
                        P.op("tensor", "mm", out=cb[0:65 if h == 0 else 128, :],
                             lhsT=lhsT, rhs=ptbuf.ap()[:, poff:poff + 512],
                             start=(ktc == 0), stop=(ktc == TT - 1),
                             wait=ctxw, inc=("t", 1))
                        st[f"ab_ptuse{pslot}"] = P.cur("t")

            BPREF = int(_os.environ.get("BPREF", "1"))
            CLAG = int(_os.environ.get("CLAG", "1"))
            if BPREF:
                ab_cur = emit_bands(0)
            for kt in range(TT):
                if BPREF:
                    ab_next = emit_bands(kt + 1) if kt + 1 < TT else None
                else:
                    ab_cur = emit_bands(kt)
                emit_qk_exp(kt, ab_cur)
                if CLAG:
                    if kt > 0:
                        emit_ctx(kt - 1)
                else:
                    emit_ctx(kt)
                if BPREF:
                    ab_cur = ab_next
            if CLAG:
                emit_ctx(TT - 1)
            for qci in range(2 if ABLIM > 4 else 0):
                qc = half * 2 + qci
                for h in range(2):
                    cb = pb[4 + qci * 2 + h]
                    rrow = 64 if h == 0 else 32
                    ctxmm = P.cur("t")
                    P.op("scalar", "copy",
                         out=rs_sb.ap()[rrow:rrow + 1, 0:512],
                         in_=cb[rrow:rrow + 1, :], wait=[("t", ctxmm)],
                         inc=("sc", 1))
                    P.op("scalar", "act",
                         out=rs_sb.ap()[rrow:rrow + 1, 0:512],
                         in_=rs_sb.ap()[rrow:rrow + 1, 0:512], func=AF.Ln,
                         inc=("sc", 1))
                    rv = P.op("scalar", "act",
                              out=rs_sb.ap()[rrow:rrow + 1, 0:512],
                              in_=rs_sb.ap()[rrow:rrow + 1, 0:512], func=AF.Exp,
                              scale=-1.0, inc=("sc", 1))
                    lhsT = (onesf.ap()[64:65, 0:128] if h == 0
                            else onesf.ap()[32:33, 128:256])
                    rep = P.op("tensor", "mm", out=pb[2 + h],
                               lhsT=lhsT, rhs=rs_sb.ap()[rrow:rrow + 1, 0:512],
                               start=True, stop=True,
                               wait=[("sc", rv),
                                     ("sc", st.get(f"ab_sbank{h * 2 + 1}", 0)),
                                     ("v", st.get(f"ab_repuse{h}", 0))],
                               inc=("t", 1))
                    cv = P.op("scalar", "copy",
                              out=(ctmp_sb.ap()[0:64, :] if h == 0
                                   else ctmp_sb.ap()[64:128, :]),
                              in_=(cb[0:64, :] if h == 0 else cb[64:128, :]),
                              wait=[("v", st.get("ab_ctxnv", 0))], inc=("sc", 1))
                    st[f"ab_ctmp{qci * 2 + h}"] = cv
                    vv = P.op("vector", "tt",
                              out=(ctxn_sb.ap()[0:64, qc * 512:(qc + 1) * 512]
                                   if h == 0 else
                                   ctxn_sb.ap()[64:128, qc * 512:(qc + 1) * 512]),
                              a=(ctmp_sb.ap()[0:64, :] if h == 0
                                 else ctmp_sb.ap()[64:128, :]),
                              b=(pb[2 + h][0:64, :] if h == 0
                                 else pb[2 + h][64:128, :]),
                              alu=ALU.mult,
                              wait=[("t", rep), ("sc", cv),
                                    ("dCXo", st.get(f"ab_ctxout{hp - 1}", 0))],
                              inc=("v", 1))
                    st["ab_ctxnv"] = vv
                    st[f"ab_repuse{h}"] = vv
        st[f"ab_vuse{hp}"] = P.cur("t")
        st[f"ab_eduse{hp}"] = P.cur("t")
        st[f"ab_qkuse{hp}"] = P.cur("t")
        st[f"ab_cc{hp}"] = P.cur("t")

        if ABLIM <= 4:
            return
        P.op("sync", "dma", out=AP(ctxn_d, hp * 128 * T, [[T, 128], [1, T]]),
             in_=ctxn_sb.ap(), wait=[("v", st["ab_ctxnv"])], inc=("dCXo", 16))
        st[f"ab_ctxout{hp}"] = P.cur("dCXo")

    for b in range(n_batches):
        xl = None
        for ct in range(CT):
            xl = P.op("sync", "dma", out=xtile(ct),
                      in_=AP(xT.tensor, b * D * T + ct * 128 * T, [[T, 128], [1, T]]),
                      inc=("dX", 16))
        P.op("vector", "noop", wait=[("dX", xl)])
        P.op("tensor", "noop", wait=[("dX", xl)])
        P.op("scalar", "noop", wait=[("dX", xl)])
        if 0 in stages:
            layer_norm()
            if dbg == "ln" and b == 0:
                dump(0, lambda ct: xntile(ct))
            attn_a()
            if dbg in ("aa", "aa2") and b == 0:
                dump(0, lambda ct: f_buf.ap(), nt=1)
                dump(1, lambda ct: g_buf.ap(), nt=1)
                dump(2, lambda ct: h_buf.ap(), nt=1)
                dump(3, lambda ct: og_buf.ap(), nt=1)
        if 1 in stages:
            layer_norm()
            if dbg == "ffn1pre" and b == 0:
                dump(3, lambda ct: xntile(ct))
            ffn()
            if dbg == "ffn1" and b == 0:
                dump(0, lambda ct: mrow_sb.ap(), nt=1)
                dump(1, lambda ct: mba_sb.ap()[0:128, 0:2048], nt=1)
                dump(2, lambda ct: xm_sb.ap()[:, 0:2048], nt=1)
        if 2 in stages:
            layer_norm()
            attn_b(b)
        if 3 in stages:
            layer_norm()
            ffn()
        snap = P.snapshot()
        for ct in range(CT):
            P.op("sync", "dma",
                 out=AP(outT.tensor, b * D * T + ct * 128 * T, [[T, 128], [1, T]]),
                 in_=xtile(ct), wait=[(n, v) for n, v in snap.items()],
                 inc=("dO", 16))
        P.op("sync", "noop", wait=[("dO", P.cur("dO"))])
    P.op("sync", "noop", wait=[("dO", P.cur("dO"))])

    P.emit()
    nc._dbg_cnt = dict(P.cnt)
    return nc


def host_consts():
    I = np.eye(128, dtype=np.float32)
    sel = np.zeros((8, E * 128), np.float32)
    for e in range(E):
        sel[e, e * 128:(e + 1) * 128] = 1.0
    onesf = np.zeros((128, 384), np.float32)
    onesf[:, 0:64] = 1.0
    onesf[:, 128 + 64:256] = 1.0
    onesf[:, 256:384] = 1.0
    return I, sel, onesf


def make_inputs(inp, core):
    x = np.asarray(inp["L0_attn_a_x"])
    xb = np.ascontiguousarray(
        x[core * NB:(core + 1) * NB].transpose(0, 2, 1)).astype(np.float32, copy=False)
    I, sel, onesf = host_consts()
    rel = np.asarray(inp["L0_attn_b_rel_emb"])
    d = {
        "xT": xb,
        "Wa": np.ascontiguousarray(inp["L0_attn_a_W"]),
        "ba": np.ascontiguousarray(inp["L0_attn_a_b"]),
        "WqT": np.ascontiguousarray(np.asarray(inp["L0_attn_b_Wq"]).T),
        "WkT": np.ascontiguousarray(np.asarray(inp["L0_attn_b_Wk"]).T),
        "WvT": np.ascontiguousarray(np.asarray(inp["L0_attn_b_Wv"]).T),
        "WoT": np.ascontiguousarray(np.asarray(inp["L0_attn_b_Wo"]).T),
        "WpkT": np.ascontiguousarray(np.asarray(inp["L0_attn_b_Wpk"]).T),
        "WpqT": np.ascontiguousarray(np.asarray(inp["L0_attn_b_Wpq"]).T),
        "relT": np.ascontiguousarray(rel.T),
        "relrevT": np.ascontiguousarray(rel[::-1].T),
        "bq": np.ascontiguousarray(inp["L0_attn_b_bq"]),
        "bk": np.ascontiguousarray(inp["L0_attn_b_bk"]),
        "bv": np.ascontiguousarray(inp["L0_attn_b_bv"]),
        "bo": np.ascontiguousarray(inp["L0_attn_b_bo"]),
        "bpk": np.ascontiguousarray(inp["L0_attn_b_bpk"]),
        "bpq": np.ascontiguousarray(inp["L0_attn_b_bpq"]),
        "gwT": np.ascontiguousarray(np.asarray(inp["L0_ffn_gate_w"]).T),
        "WeT": np.ascontiguousarray(np.asarray(inp["L0_ffn_expert_w"]).transpose(0, 2, 1)),
        "I128c": I, "SEL8c": sel, "ONESF": onesf,
    }
    return {k: np.asarray(v, dtype=np.float32) for k, v in d.items()}


from concourse.bass_utils import run_bass_kernel_spmd

_NCORES = 8


def kernel(**inputs):
    nc = build(stages=(0, 1, 2, 3), n_batches=NB)
    in_maps = [make_inputs(inputs, c) for c in range(_NCORES)]
    res = run_bass_kernel_spmd(nc, in_maps, core_ids=list(range(_NCORES)))
    out = np.empty((16, T, D), dtype=np.float32)
    for c in range(_NCORES):
        o = np.asarray(res.results[c]["outT"])
        out[c * NB:(c + 1) * NB] = o.transpose(0, 2, 1)
    return out


# version 04
# speedup vs baseline: 1.2514x; 1.0279x over previous
import numpy as np
import concourse.bass as bass
import concourse.mybir as mybir
from concourse.bass import AP

F32 = mybir.dt.float32
BF16 = mybir.dt.bfloat16
AF = mybir.ActivationFunctionType
ALU = mybir.AluOpType
AX = mybir.AxisListType

T = 2048
D = 1024
NB = 2
HP = 8
S = 1024
SCALE = float(1.0 / np.sqrt(np.float32(64 * 3)))
E = 8
EPS = 1e-5
PADW = 1408
CT = D // 128
TT = T // 128
QCN = T // 512


class Prog:
    ENGS = ("sync", "tensor", "vector", "scalar", "gpsimd")

    def __init__(self, nc):
        self.nc = nc
        self.q = {e: [] for e in self.ENGS}
        self.sems = {}
        self.cnt = {}

    def _sem(self, name):
        if name not in self.sems:
            self.sems[name] = self.nc.alloc_semaphore(name)
            self.cnt[name] = 0
        return self.sems[name]

    def op(self, eng, kind, wait=(), inc=None, **kw):
        w = []
        for item in wait:
            if item is None:
                continue
            sn, val = item
            if val is None or val <= 0:
                continue
            self._sem(sn)
            assert val <= self.cnt[sn], f"wait {sn}>={val} but only {self.cnt[sn]}"
            w.append((sn, val))
        tag = None
        if inc is not None:
            sn, k = inc
            self._sem(sn)
            self.cnt[sn] += k
            tag = self.cnt[sn]
        self.q[eng].append((kind, kw, tuple(w), (inc[0], inc[1]) if inc else None))
        return tag

    def cur(self, name):
        self._sem(name)
        return self.cnt[name]

    def snapshot(self):
        return {n: self.cnt[n] for n in ("t", "v", "sc", "g") if n in self.cnt}

    def barrier(self):
        snap = dict(self.cnt)
        for eng in self.ENGS:
            waits = [(n, v) for n, v in snap.items()]
            self.op(eng, "noop", wait=waits)

    def emit(self):
        with self.nc.Block() as block:
            for ename in self.ENGS:
                ops = self.q[ename]
                if ops:
                    getattr(block, ename)(lambda e, ops=ops: self._run(e, ops))

    def _run(self, e, ops):
        for kind, kw, waits, inc in ops:
            for (sn, val) in waits:
                e.wait_ge(self.sems[sn], val)
            if kind == "noop":
                continue
            ins = getattr(self, "_" + kind)(e, **kw)
            if inc is not None:
                sn, k = inc
                ins.then_inc(self.sems[sn], k)

    def _mm(self, e, out, lhsT, rhs, start, stop, tp=None):
        if tp is not None:
            return e.matmul(out, lhsT, rhs, start=start, stop=stop, tile_position=tp)
        return e.matmul(out, lhsT, rhs, start=start, stop=stop)

    def _dma(self, e, out, in_, accum=False, transpose=False, slow=False):
        kw = {}
        if accum:
            kw["accum_op"] = ALU.add
        if transpose:
            kw["transpose"] = True
        if slow:
            kw["allow_slow_non_contiguous"] = True
        return e.dma_start(out=out, in_=in_, **kw)

    def _tt(self, e, out, a, b, alu):
        return e.tensor_tensor(out, a, b, alu)

    def _ts(self, e, out, in0, s1, s2=None, op0=ALU.mult, op1=None):
        if op1 is None:
            return e.tensor_single_scalar(out, in0, s1, op0)
        return e.tensor_scalar(out, in0, s1, s2, op0, op1)

    def _stt(self, e, out, in0, s, in1, op0, op1):
        return e.scalar_tensor_tensor(out, in0, s, in1, op0, op1)

    def _scan(self, e, out, d0, d1, init, op0, op1):
        return e.tensor_tensor_scan(out, d0, d1, init, op0, op1)

    def _copy(self, e, out, in_):
        if hasattr(e, "tensor_copy"):
            return e.tensor_copy(out, in_)
        return e.copy(out, in_)

    def _memset(self, e, ap, val):
        return e.memset(ap, val)

    def _act(self, e, out, in_, func, bias=0.0, scale=1.0, accum_out=None):
        return e.activation(out, in_, func, bias=bias, scale=scale,
                            accum_out=accum_out)

    def _reduce(self, e, out, in_, alu):
        return e.tensor_reduce(out, in_, AX.X, alu)

    def _recip(self, e, out, in_):
        return e.reciprocal(out, in_)


def build(stages=(0, 1, 2, 3), n_batches=NB, dbg=None):
    import os as _os
    ABLIM = float(_os.environ.get("ABLIM", "6"))
    SKIP_AA_TAIL = (dbg == "aa2")
    nc = bass.Bass("TRN2", target_bir_lowering=False, detect_race_conditions=False)

    xT = nc.dram_tensor("xT", [NB, D, T], F32, kind="ExternalInput").ap()
    outT = nc.dram_tensor("outT", [NB, D, T], F32, kind="ExternalOutput").ap()
    dbgT = nc.dram_tensor("dbgT", [4, D, T], F32, kind="ExternalOutput").ap()

    def dump(idx, src_ap_fn, nt=CT):
        snap2 = P.snapshot()
        for ct in range(nt):
            sap = src_ap_fn(ct)
            p, w = sap.shape
            P.op("gpsimd", "dma",
                 out=AP(dbgT.tensor, idx * D * T + ct * 128 * T, [[T, p], [1, w]]),
                 in_=sap, wait=[(n, v) for n, v in snap2.items()],
                 inc=("dOg", 16))
    Wa = nc.dram_tensor("Wa", [D, 3 * D], F32, kind="ExternalInput").ap()
    ba_d = nc.dram_tensor("ba", [3 * D], F32, kind="ExternalInput").ap()
    WqT = nc.dram_tensor("WqT", [D, D], F32, kind="ExternalInput").ap()
    WkT = nc.dram_tensor("WkT", [D, D], F32, kind="ExternalInput").ap()
    WvT = nc.dram_tensor("WvT", [D, D], F32, kind="ExternalInput").ap()
    WoT = nc.dram_tensor("WoT", [D, D], F32, kind="ExternalInput").ap()
    WpkT = nc.dram_tensor("WpkT", [D, D], F32, kind="ExternalInput").ap()
    WpqT = nc.dram_tensor("WpqT", [D, D], F32, kind="ExternalInput").ap()
    relT = nc.dram_tensor("relT", [D, S], F32, kind="ExternalInput").ap()
    relrevT = nc.dram_tensor("relrevT", [D, S], F32, kind="ExternalInput").ap()
    bq_d = nc.dram_tensor("bq", [D], F32, kind="ExternalInput").ap()
    bk_d = nc.dram_tensor("bk", [D], F32, kind="ExternalInput").ap()
    bv_d = nc.dram_tensor("bv", [D], F32, kind="ExternalInput").ap()
    bo_d = nc.dram_tensor("bo", [D], F32, kind="ExternalInput").ap()
    bpk_d = nc.dram_tensor("bpk", [D], F32, kind="ExternalInput").ap()
    bpq_d = nc.dram_tensor("bpq", [D], F32, kind="ExternalInput").ap()
    gwT = nc.dram_tensor("gwT", [D, E], F32, kind="ExternalInput").ap()
    WeT = nc.dram_tensor("WeT", [E, D, D], F32, kind="ExternalInput").ap()
    I128c = nc.dram_tensor("I128c", [128, 128], F32, kind="ExternalInput").ap()
    SEL8c = nc.dram_tensor("SEL8c", [8, E * 128], F32, kind="ExternalInput").ap()
    ONESF = nc.dram_tensor("ONESF", [128, 384], F32, kind="ExternalInput").ap()

    cpad_d = nc.dram_tensor("cpad_d", [2, 2, TT, 128, PADW], BF16, kind="Internal")
    pk_d = nc.dram_tensor("pk_d", [HP, 128, S], BF16, kind="Internal")
    pqr_d = nc.dram_tensor("pqr_d", [HP, 128, S], BF16, kind="Internal")
    ctxn_d = nc.dram_tensor("ctxn_d", [HP, 128, T], BF16, kind="Internal")

    P = Prog(nc)

    ARENA = 209920
    _arena = nc.alloc_sbuf_tensor("manual_arena", [128, ARENA], mybir.dt.uint8)
    SB_BASE = nc.lookup_mloc(_arena).addr

    def at(name, shape, dtype, off):
        esz = 2 if dtype == BF16 else 4
        per = int(np.prod(shape[1:])) * esz
        assert off + per <= ARENA, (name, off + per)
        return nc.alloc_sbuf_tensor_at(name, shape, dtype, offset=off + SB_BASE)

    I128 = at("I128", [128, 128], BF16, 0)
    ones_b = at("ones_b", [128, 128], BF16, 256)
    onesf = at("onesf", [128, 384], F32, 512)
    sel8 = at("sel8", [8, E * 128], BF16, 2048)
    ba_sb = at("ba_sb", [128, 24], F32, 4096)
    bq_sb = at("bq_sb", [128, 8], F32, 4192)
    bk_sb = at("bk_sb", [128, 8], F32, 4224)
    bo_sb = at("bo_sb", [128, 8], F32, 4256)
    bpk_sb = at("bpk_sb", [128, 8], F32, 4288)
    bpq_sb = at("bpq_sb", [128, 8], F32, 4320)
    bv_sb = at("bv_sb", [128, 8], BF16, 4352)
    gw_sb = at("gw_sb", [128, CT * E], BF16, 4384)
    ec2 = at("ec2", [128, 64], F32, 4544)
    rs_sb = at("rs_sb", [128, 512], F32, 4800)
    edlhsT = at("edlhsT", [128, 97], BF16, 6848)
    ban_sb = at("ban_sb", [128, 8], F32, 7072)
    lnm = at("lnm", [1, 512], F32, 7104)
    lnr = at("lnr", [1, 512], F32, 9152)
    lnt = at("lnt", [1, 512], F32, 11200)
    lnt2 = at("lnt2", [1, 512], F32, 13248)
    xb_buf = at("xb_buf", [128, CT * 512], BF16, 15296)
    sq_buf = at("sq_buf", [128, CT * 512], BF16, 23488)
    t1_buf = at("t1_buf", [128, 512], F32, 31680)
    XOFF = 33728
    x_sb = at("x_sb", [128, CT * 2048], F32, XOFF)
    xn_sb = at("xn_sb", [128, CT * 2048], BF16, XOFF + 65536)
    PH = XOFF + 65536 + 32768
    LIMIT = 212863

    wa_sb = at("wa_sb", [128, 6 * 1024], BF16, PH)
    f_buf = at("f_buf", [128, 2048], F32, PH + 12288)
    g_buf = at("g_buf", [128, 2048], F32, PH + 20480)
    og_buf = at("og_buf", [128, 2048], F32, PH + 28672)
    h_buf = at("h_buf", [128, 2048], F32, PH + 36864)
    ig_t = at("ig_t", [128, 512], F32, PH + 45056)
    in_t = at("in_t", [128, 512], F32, PH + 47104)
    xm_sb = at("xm_sb", [128, 2 * CT * 512], BF16, PH)
    we_sb = at("we_sb", [128, 2 * CT * 1024], BF16, PH + 16384)
    mba_sb = at("mba_sb", [128, E * 512], BF16, PH + 49152)
    mrow_sb = at("mrow_sb", [8, T], BF16, PH + 57344)
    msk_sb = at("msk_sb", [128, 8], BF16, PH + 61440)
    mx_t = at("mx_t", [128, 1], F32, PH + 61472)
    gs_t = at("gs_t", [128, 8], F32, PH + 61504)
    mskf_t = at("mskf_t", [128, 8], F32, PH + 61536)
    mxr_t = at("mxr_t", [128, 8], F32, PH + 71936)
    gsall = at("gsall", [128, 128], F32, PH + 72000)
    gscr = at("gscr", [128, 128], F32, PH + 72512)
    mtree = at("mtree", [128, 112], F32, PH + 73024)
    mskall = at("mskall", [128, 128], BF16, PH + 73536)
    relb_sb = at("relb_sb", [128, CT * 1024], BF16, PH + 16384)
    pre_sg = at("pre_sg", [128, 2 * 512], BF16, PH + 36864)
    qb_sb = at("qb_sb", [128, 2048], BF16, PH)
    kb_sb = at("kb_sb", [128, 2048], BF16, PH + 4096)
    pk_sb = at("pk_sb", [128, 1024], BF16, PH + 8192)
    pqr_sb = at("pqr_sb", [128, 1024], BF16, PH + 10240)
    v_sb = at("v_sb", [128, TT * 194], BF16, PH + 12288)
    cpad_sb = at("cpad_sb", [128, 4 * PADW], BF16, PH + 18496)
    band_sb = at("band_sb", [128, 4 * 1152], BF16, PH + 29760)
    pt_sb = at("pt_sb", [128, 4 * 512], BF16, PH + 38976)
    pt2_sb = at("pt2_sb", [128, 4 * 512], BF16, PH + 73792)
    ctmp_sb = at("ctmp_sb", [128, 512], F32, PH + 43072)
    ctxn_sb = at("ctxn_sb", [128, 2048], BF16, PH + 45120)
    cstr_sb = at("cstr_sb", [128, HP * 512], BF16, PH + 49216)
    edge_sb = at("edge_sb", [128, 2048], BF16, PH + 57408)
    wq_sb = at("wq_sb", [128, 3 * 1024], BF16, PH + 61568)
    wo_sb = at("wo_sb", [128, 2 * 1024], BF16, PH + 67712)
    wofull = at("wofull", [128, CT * 8 * 128], BF16, PH)

    pbT = [nc.alloc_psum_tensor(f"pb{i}", [128, 512], F32) for i in range(8)]
    pb = [t.ap() for t in pbT]

    def xtile(ct, lo=0, hi=T):
        return x_sb.ap()[:, ct * T + lo:ct * T + hi]

    def xntile(ct, lo=0, hi=T):
        return xn_sb.ap()[:, ct * T + lo:ct * T + hi]

    st = {}

    P.op("gpsimd", "dma", out=I128.ap(), in_=I128c, inc=("dCg", 16))
    P.op("gpsimd", "dma", out=sel8.ap(), in_=SEL8c, inc=("dCg", 16))
    P.op("sync", "dma", out=onesf.ap(), in_=ONESF, inc=("dC", 16))
    P.op("gpsimd", "memset", ap=ones_b.ap(), val=1.0, inc=("g", 1))
    _bias_specs = ((ba_d, ba_sb, 24, None), (bq_d, bq_sb, 8, SCALE),
                   (bk_d, bk_sb, 8, None), (bpk_d, bpk_sb, 8, None),
                   (bpq_d, bpq_sb, 8, SCALE), (bo_d, bo_sb, 8, None))
    for (src, dst, n, scl) in _bias_specs:
        P.op("sync", "dma", out=dst.ap()[:, 0:n],
             in_=AP(src.tensor, 0, [[1, 128], [128, n]]), slow=True,
             inc=("dC", 16))
    _dC_all = P.cur("dC")
    for (src, dst, n, scl) in _bias_specs:
        if scl is not None:
            P.op("vector", "ts", out=dst.ap()[:, 0:n], in0=dst.ap()[:, 0:n],
                 s1=scl, wait=[("dC", _dC_all)], inc=("v", 1))
    P.op("gpsimd", "dma", out=bv_sb.ap(),
         in_=AP(bv_d.tensor, 0, [[1, 128], [128, 8]]), slow=True, inc=("dCg", 16))
    P.op("gpsimd", "dma", out=gw_sb.ap().rearrange("p (a b) -> p a b", a=CT),
         in_=AP(gwT.tensor, 0, [[8, 128], [1024, CT], [1, 8]]), slow=True,
         inc=("dCg", 16))
    st["setup_dC"] = P.cur("dC")
    st["setup_dCg"] = P.cur("dCg")
    P.op("vector", "ts", out=ban_sb.ap(), in0=ba_sb.ap()[:, 8:16], s1=-1.0,
         wait=[("dC", st["setup_dC"])], inc=("v", 1))
    st["setup_g"] = P.cur("g")
    st["setup_v"] = P.cur("v")
    P.barrier()

    if 2 in stages:
        relb = relb_sb.ap()
        for mi, (wT, rel_src, dst_d, bias_sb, scl) in enumerate((
                (WpkT, relT, pk_d, bpk_sb, 1.0),
                (WpqT, relrevT, pqr_d, bpq_sb, SCALE))):
            rtg = P.op("gpsimd", "dma",
                       out=relb.rearrange("p (a b) -> p a b", a=CT),
                       in_=AP(rel_src.tensor, 0, [[S, 128], [128 * S, CT], [1, S]]),
                       wait=[("t", P.cur("t"))], inc=("dRel", 16))
            for ft in range(CT):
                ws = (ft % 2) * 1024
                wslot = wq_sb.ap()[:, ws:ws + 1024]
                wtg = P.op("gpsimd", "dma",
                           out=wslot.rearrange("p (a b) -> p a b", a=CT),
                           in_=AP(wT.tensor, ft * 128,
                                  [[D, 128], [128 * D, CT], [1, 128]]),
                           wait=[("t", st.get(f"pre{mi}w{ft % 2}", 0))],
                           inc=(f"dWp{ft % 2}", 16))
                for sc2 in range(2):
                    slot = (ft * 2 + sc2) % 2
                    bank = pb[slot]
                    for ct in range(CT):
                        P.op("tensor", "mm", out=bank,
                             lhsT=wslot[:, ct * 128:ct * 128 + 128],
                             rhs=relb[:, ct * 1024 + sc2 * 512:ct * 1024 + sc2 * 512 + 512],
                             start=(ct == 0), stop=(ct == CT - 1),
                             wait=([(f"dWp{ft % 2}", P.cur(f"dWp{ft % 2}")),
                                    ("dRel", P.cur("dRel")),
                                    ("v", st.get(f"preb{slot}", 0)),
                                    ("dC", st["setup_dC"]),
                                    ("dCg", st["setup_dCg"]),
                                    ("v", st["setup_v"])] if ct == 0 else []),
                             inc=(("t", 1) if ct == CT - 1 else None))
                    mmt = P.cur("t")
                    ev = P.op("vector", "stt",
                              out=pre_sg.ap()[:, slot * 512:slot * 512 + 512],
                              in0=bank, s=scl,
                              in1=bias_sb.ap()[:, ft:ft + 1].broadcast_to([128, 512]),
                              op0=ALU.mult, op1=ALU.add,
                              wait=[("t", mmt), ("dP", st.get(f"prew{slot}", 0))],
                              inc=("v", 1))
                    st[f"preb{slot}"] = ev
                    st[f"prew{slot}"] = P.op(
                        "sync", "dma",
                        out=AP(dst_d, ft * 128 * S + sc2 * 512, [[S, 128], [1, 512]]),
                        in_=pre_sg.ap()[:, slot * 512:slot * 512 + 512],
                        wait=[("v", ev)], inc=("dP", 16))
                st[f"pre{mi}w{ft % 2}"] = P.cur("t")
        st["pre_dP"] = P.cur("dP")
        P.barrier()

    def layer_norm():
        for qc in range(QCN):
            lo = qc * 512
            ctag = None
            for ct in range(CT):
                ctag = P.op("vector", "copy",
                            out=xb_buf.ap()[:, ct * 512:(ct + 1) * 512],
                            in_=xtile(ct, lo, lo + 512),
                            wait=([("t", st.get("ln_sums", 0))] if ct == 0 else []),
                            inc=("v", 1))
            sqtag = None
            for ct in range(CT):
                sqtag = P.op("scalar", "act",
                             out=sq_buf.ap()[:, ct * 512:(ct + 1) * 512],
                             in_=xtile(ct, lo, lo + 512), func=AF.Square,
                             wait=([("t", st.get("ln_sums", 0))] if ct == 0 else []),
                             inc=("sc", 1))
            for ct in range(CT):
                P.op("tensor", "mm", out=pb[0][0:1, :], lhsT=ones_b.ap()[:, 0:1],
                     rhs=xb_buf.ap()[:, ct * 512:(ct + 1) * 512],
                     start=(ct == 0), stop=(ct == CT - 1),
                     wait=([("v", ctag), ("v", st.get("ln_statv", 0))]
                           if ct == 0 else []),
                     inc=(("t", 1) if ct == CT - 1 else None))
            for ct in range(CT):
                P.op("tensor", "mm", out=pb[1][0:1, :], lhsT=ones_b.ap()[:, 0:1],
                     rhs=sq_buf.ap()[:, ct * 512:(ct + 1) * 512],
                     start=(ct == 0), stop=(ct == CT - 1),
                     wait=([("sc", sqtag)] if ct == 0 else []),
                     inc=(("t", 1) if ct == CT - 1 else None))
            st["ln_sums"] = P.cur("t")
            P.op("vector", "ts", out=lnm.ap(), in0=pb[0][0:1, :], s1=1.0 / D,
                 wait=[("t", st["ln_sums"])], inc=("v", 1))
            P.op("vector", "tt", out=lnt.ap(), a=lnm.ap(), b=lnm.ap(),
                 alu=ALU.mult, inc=("v", 1))
            vv = P.op("vector", "stt", out=lnt2.ap(), in0=pb[1][0:1, :], s=1.0 / D,
                      in1=lnt.ap(), op0=ALU.mult, op1=ALU.subtract, inc=("v", 1))
            vv = P.op("vector", "ts", out=lnt2.ap(), in0=lnt2.ap(), s1=EPS,
                      op0=ALU.add, inc=("v", 1))
            sd = P.op("scalar", "act", out=lnt.ap(), in_=lnt2.ap(), func=AF.Sqrt,
                      bias=0.0, wait=[("v", vv)], inc=("sc", 1))
            rr = P.op("vector", "recip", out=lnr.ap(), in_=lnt.ap(),
                      wait=[("sc", sd)], inc=("v", 1))
            mr = P.op("vector", "tt", out=lnt2.ap(), a=lnm.ap(), b=lnr.ap(),
                      alu=ALU.mult, inc=("v", 1))
            st["ln_statv"] = mr
            r1 = P.op("tensor", "mm", out=pb[2], lhsT=onesf.ap()[0:1, 256:384],
                      rhs=lnr.ap(), start=True, stop=True,
                      wait=[("v", mr), ("v", st.get("ln_normv", 0))], inc=("t", 1))
            r2 = P.op("tensor", "mm", out=pb[3], lhsT=onesf.ap()[0:1, 256:384],
                      rhs=lnt2.ap(), start=True, stop=True, inc=("t", 1))
            vn = None
            for ct in range(CT):
                P.op("vector", "tt", out=t1_buf.ap(), a=xtile(ct, lo, lo + 512),
                     b=pb[2], alu=ALU.mult, wait=[("t", r2)], inc=("v", 1))
                vn = P.op("vector", "tt", out=xntile(ct, lo, lo + 512),
                          a=t1_buf.ap(), b=pb[3], alu=ALU.subtract, inc=("v", 1))
            st["ln_normv"] = vn
        st["ln_done_v"] = P.cur("v")
        P.barrier()

    def attn_a():
        wtags = {}

        def load_w(g):
            for fam in range(3):
                slot = fam * 2 + (g % 2)
                wtags[(g, fam)] = P.op(
                    "gpsimd", "dma",
                    out=wa_sb.ap()[:, slot * 1024:(slot + 1) * 1024]
                    .rearrange("p (a b) -> p a b", a=CT),
                    in_=AP(Wa.tensor, fam * 1024 + g * 128,
                           [[3 * D, 128], [128 * 3 * D, CT], [1, 128]]),
                    wait=[("t", st.get(f"aa_mm_g{g - 2}", 0))],
                    inc=(f"dWa{g % 2}", 16))

        load_w(0)
        load_w(1)
        for g in range(CT):
            for tc in range(QCN):
                bs = 0 if (g * QCN + tc) % 2 == 0 else 3
                for fam in range(3):
                    slot = fam * 2 + (g % 2)
                    for ct in range(CT):
                        P.op("tensor", "mm", out=pb[bs + fam],
                             lhsT=wa_sb.ap()[:, slot * 1024 + ct * 128:
                                             slot * 1024 + ct * 128 + 128],
                             rhs=xntile(ct, tc * 512, tc * 512 + 512),
                             start=(ct == 0), stop=(ct == CT - 1),
                             wait=([(f"dWa{g % 2}", P.cur(f"dWa{g % 2}")),
                                    ("sc", st.get(f"aa_ev{bs}", 0)),
                                    ("v", st.get(f"aa_gv{bs}", 0))]
                                   if ct == 0 else []),
                             inc=(("t", 1) if ct == CT - 1 else None))
                mmt = P.cur("t")
                st[f"aa_mm_g{g}"] = mmt
                lo = tc * 512
                w0 = [("t", mmt), ("v", st.get("aa_gv_last", 0))]
                if tc == 0:
                    w0.append(("v", st.get("aa_gdone", 0)))
                P.op("scalar", "act", out=in_t.ap(), in_=pb[bs + 0], func=AF.Tanh,
                     bias=ba_sb.ap()[:, g:g + 1], wait=w0, inc=("sc", 1))
                P.op("scalar", "act", out=ig_t.ap(), in_=pb[bs + 1], func=AF.Sigmoid,
                     bias=ba_sb.ap()[:, 8 + g:8 + g + 1], inc=("sc", 1))
                P.op("scalar", "act", out=f_buf.ap()[:, lo:lo + 512], in_=pb[bs + 1],
                     func=AF.Sigmoid, scale=-1.0, bias=ban_sb.ap()[:, g:g + 1],
                     inc=("sc", 1))
                ev3 = P.op("scalar", "act", out=og_buf.ap()[:, lo:lo + 512],
                           in_=pb[bs + 2], func=AF.Sigmoid,
                           bias=ba_sb.ap()[:, 16 + g:16 + g + 1], inc=("sc", 1))
                st[f"aa_ev{bs}"] = ev3
                gv = P.op("vector", "tt", out=g_buf.ap()[:, lo:lo + 512],
                          a=in_t.ap(), b=ig_t.ap(), alu=ALU.mult,
                          wait=[("sc", ev3)], inc=("v", 1))
                st[f"aa_gv{bs}"] = gv
                st["aa_gv_last"] = gv
            if g + 2 < CT:
                load_w(g + 2)
            scv = P.op("vector", "scan", out=h_buf.ap(), d0=f_buf.ap(),
                       d1=g_buf.ap(), init=0.0, op0=ALU.mult, op1=ALU.add,
                       wait=[("sc", st.get("aa_thp", 0))], inc=("v", 1))
            if SKIP_AA_TAIL:
                continue
            th = P.op("scalar", "act", out=f_buf.ap(), in_=h_buf.ap(),
                      func=AF.Tanh, wait=[("v", scv)], inc=("sc", 1))
            st["aa_thp"] = th
            P.op("vector", "tt", out=og_buf.ap(), a=f_buf.ap(), b=og_buf.ap(),
                 alu=ALU.mult, wait=[("sc", th)], inc=("v", 1))
            st["aa_gdone"] = P.op("vector", "tt", out=xtile(g), a=xtile(g),
                                  b=og_buf.ap(), alu=ALU.add, inc=("v", 1))
        P.barrier()

    wecnt = [0]

    def ffn():
        for tt in range(TT):
            bank = pb[6] if tt % 2 == 0 else pb[7]
            for ct in range(CT):
                P.op("tensor", "mm", out=bank[:, 0:8],
                     lhsT=xntile(ct, tt * 128, tt * 128 + 128),
                     rhs=gw_sb.ap()[:, ct * E:(ct + 1) * E],
                     start=(ct == 0), stop=(ct == CT - 1),
                     wait=([("v", st.get(f"ffn_gv{tt % 2}", 0)),
                            ("dCg", st["setup_dCg"])] if ct == 0 else []),
                     inc=(("t", 1) if ct == CT - 1 else None))
            gt = P.cur("t")
            gv = P.op("vector", "copy", out=gsall.ap()[:, tt * 8:(tt + 1) * 8],
                      in_=bank[:, 0:8], wait=[("t", gt)], inc=("v", 1))
            st[f"ffn_gv{tt % 2}"] = gv
        ga = gsall
        mt = mtree
        P.op("vector", "copy", out=gscr.ap(), in_=gsall.ap(), inc=("v", 1))
        P.op("vector", "tt",
             out=mtree.ap()[:, 0:64].rearrange("p (a b) -> p a b", a=16),
             a=AP(ga, 0, [[128, 128], [8, 16], [1, 4]]),
             b=AP(ga, 4, [[128, 128], [8, 16], [1, 4]]),
             alu=ALU.max, inc=("v", 1))
        P.op("vector", "copy", out=gscr.ap(), in_=gsall.ap(), inc=("v", 1))
        P.op("vector", "tt",
             out=mtree.ap()[:, 64:96].rearrange("p (a b) -> p a b", a=16),
             a=AP(mt, 0, [[112, 128], [4, 16], [1, 2]]),
             b=AP(mt, 2, [[112, 128], [4, 16], [1, 2]]),
             alu=ALU.max, inc=("v", 1))
        P.op("vector", "copy", out=gscr.ap(), in_=gsall.ap(), inc=("v", 1))
        P.op("vector", "tt", out=mtree.ap()[:, 96:112],
             a=AP(mt, 64, [[112, 128], [2, 16], [1, 1]]),
             b=AP(mt, 65, [[112, 128], [2, 16], [1, 1]]),
             alu=ALU.max, inc=("v", 1))
        P.op("vector", "copy", out=gscr.ap(), in_=gsall.ap(), inc=("v", 1))
        P.op("vector", "copy",
             out=gscr.ap().rearrange("p (a b) -> p a b", a=16),
             in_=AP(mt, 96, [[112, 128], [1, 16], [0, 8]]), inc=("v", 1))
        P.op("vector", "copy", out=mtree.ap()[:, 0:64], in_=mtree.ap()[:, 0:64],
             inc=("v", 1))
        P.op("vector", "tt", out=gsall.ap(), a=gsall.ap(), b=gscr.ap(),
             alu=ALU.is_ge, inc=("v", 1))
        P.op("vector", "copy", out=gscr.ap()[:, 0:64], in_=mtree.ap()[:, 0:64],
             inc=("v", 1))
        mv = P.op("vector", "copy", out=mskall.ap(), in_=gsall.ap(), inc=("v", 1))
        for tt in range(TT):
            bank2 = pb[4] if tt % 2 == 0 else pb[5]
            tr = P.op("tensor", "mm", out=bank2[0:8, 0:128],
                      lhsT=mskall.ap()[:, tt * 8:(tt + 1) * 8],
                      rhs=I128.ap(), start=True, stop=True,
                      wait=[("v", mv), ("sc", st.get(f"ffn_mev{tt % 2}", 0)),
                            ("dCg", st["setup_dCg"])],
                      inc=("t", 1))
            st[f"ffn_mev{tt % 2}"] = P.op(
                "scalar", "copy", out=mrow_sb.ap()[:, tt * 128:(tt + 1) * 128],
                in_=bank2[0:8, 0:128], wait=[("t", tr)], inc=("sc", 1))
        mrow_done = P.cur("sc")

        for tc in range(QCN):
            lo = tc * 512
            for e in range(E):
                bank = pb[6] if e % 2 == 0 else pb[7]
                mmt = P.op("tensor", "mm", out=bank,
                           lhsT=sel8.ap()[:, e * 128:(e + 1) * 128],
                           rhs=mrow_sb.ap()[:, lo:lo + 512], start=True, stop=True,
                           wait=[("sc", mrow_done),
                                 ("sc", st.get(f"ffn_mbev{e % 2}", 0)),
                                 ("v", st.get("ffn_evacv", 0))],
                           inc=("t", 1))
                st[f"ffn_mbev{e % 2}"] = P.op(
                    "scalar", "copy", out=mba_sb.ap()[:, e * 512:(e + 1) * 512],
                    in_=bank, wait=[("t", mmt)], inc=("sc", 1))
            mb_done = P.cur("sc")
            for e in range(E):
                xo = (e % 2) * CT * 512
                xmv = None
                for ct in range(CT):
                    xmv = P.op("vector", "tt",
                               out=xm_sb.ap()[:, xo + ct * 512:xo + (ct + 1) * 512],
                               a=xntile(ct, lo, lo + 512),
                               b=mba_sb.ap()[:, e * 512:(e + 1) * 512],
                               alu=ALU.mult,
                               wait=([("sc", mb_done),
                                      ("t", st.get(f"ffn_xm{e % 2}", 0))]
                                     if ct == 0 else []),
                               inc=("v", 1))
                wet = P.op("gpsimd", "dma",
                           out=we_sb.ap()[:, (e % 2) * CT * 1024:
                                          ((e % 2) + 1) * CT * 1024]
                           .rearrange("p (a b) -> p a b", a=CT),
                           in_=AP(WeT.tensor, e * D * D,
                                  [[D, 128], [128 * D, CT], [1, D]]),
                           wait=[("t", st.get(f"ffn_xm{e % 2}", 0))],
                           inc=(f"dWe{e % 2}", 16))
                for ft in range(CT):
                    for ct in range(CT):
                        w = []
                        if ct == 0 and ft == 0:
                            w = [("v", xmv), (f"dWe{e % 2}", P.cur(f"dWe{e % 2}"))]
                            if e == 0:
                                w.append(("v", st.get("ffn_evacv", 0)))
                        elif ct == 0 and e == 0:
                            w = [("v", st.get("ffn_evacv", 0))]
                        P.op("tensor", "mm", out=pb[ft],
                             lhsT=we_sb.ap()[:, (e % 2) * CT * 1024 + ct * 1024 +
                                             ft * 128:(e % 2) * CT * 1024 +
                                             ct * 1024 + ft * 128 + 128],
                             rhs=xm_sb.ap()[:, xo + ct * 512:xo + (ct + 1) * 512],
                             start=(e == 0 and ct == 0),
                             stop=(e == E - 1 and ct == CT - 1),
                             wait=w,
                             inc=(("t", 1) if ct == CT - 1 else None))
                st[f"ffn_xm{e % 2}"] = P.cur("t")
            expt = P.cur("t")
            for ft in range(CT):
                P.op("vector", "stt", out=t1_buf.ap(), in0=pb[ft], s=0.0,
                     in1=xntile(ft, lo, lo + 512), op0=ALU.add, op1=ALU.add,
                     wait=[("t", expt)], inc=("v", 1))
                P.op("vector", "tt", out=xtile(ft, lo, lo + 512),
                     a=xtile(ft, lo, lo + 512), b=t1_buf.ap(), alu=ALU.add,
                     inc=("v", 1))
            st["ffn_evacv"] = P.cur("v")
        P.barrier()

    def win(kt):
        k0 = kt * 128
        return max(0, k0 - 512), min(T, k0 + 640)

    def attn_b(b):
        for hp in range(HP):
            attn_b_hp(b, hp)
        P.barrier()
        if ABLIM <= 5:
            return
        st["wo_prev_t"] = P.cur("t")
        for tc in range(QCN):
            lo = tc * 512
            cs = 0
            ctag = None
            for hp in range(HP):
                ctag = P.op("sync", "dma",
                            out=cstr_sb.ap()[:, cs + hp * 512:cs + (hp + 1) * 512],
                            in_=AP(ctxn_d, hp * 128 * T + lo, [[T, 128], [1, 512]]),
                            wait=([("v", st.get("wo_used_last", 0)),
                                   ("dCXo", P.cur("dCXo"))] if hp == 0 else []),
                            inc=(f"dCX{tc % 2}", 16))
            if tc == 0:
                st["wo_wf"] = P.op(
                    "gpsimd", "dma",
                    out=wofull.ap().rearrange("p (a b c) -> p a b c", a=8, b=CT),
                    in_=AP(WoT.tensor, 0,
                           [[D, 128], [128 * D, 8], [128, CT], [1, 128]]),
                    wait=[("t", st.get("wo_prev_t", 0))],
                    inc=("dWof", 16))
            for ft in range(CT):
                ii = tc * CT + ft
                bank = pb[6] if ft % 2 == 0 else pb[7]
                for hp in range(HP):
                    P.op("tensor", "mm", out=bank,
                         lhsT=wofull.ap()[:, (hp * CT + ft) * 128:
                                          (hp * CT + ft) * 128 + 128],
                         rhs=cstr_sb.ap()[:, cs + hp * 512:cs + (hp + 1) * 512],
                         start=(hp == 0), stop=(hp == HP - 1),
                         wait=([("dWof", P.cur("dWof")),
                                (f"dCX{tc % 2}", P.cur(f"dCX{tc % 2}")),
                                ("v", st.get(f"wo_ev{ft % 2}", 0))]
                               if hp == 0 else []),
                         inc=(("t", 1) if hp == HP - 1 else None))
                mmt = P.cur("t")
                st[f"wo_mm{ft}"] = mmt
                st[f"wo_mmi{ii}"] = mmt
                if b == 0 and tc == 0:
                    for hp2 in range(HP):
                        P.op("tensor", "mm", out=pb[5][:, 0:1],
                             lhsT=wofull.ap()[:, (hp2 * CT + ft) * 128:
                                              (hp2 * CT + ft) * 128 + 128],
                             rhs=bv_sb.ap()[:, hp2:hp2 + 1],
                             start=(hp2 == 0), stop=(hp2 == HP - 1),
                             wait=([("v", st.get("bo_v", 0)),
                                    ("dCg", st["setup_dCg"])] if hp2 == 0 else []),
                             inc=(("t", 1) if hp2 == HP - 1 else None))
                    bot = P.cur("t")
                    st["bo_v"] = P.op("vector", "tt", out=bo_sb.ap()[:, ft:ft + 1],
                                      a=bo_sb.ap()[:, ft:ft + 1], b=pb[5][:, 0:1],
                                      alu=ALU.add, wait=[("t", bot)], inc=("v", 1))
                    st[f"bo_rdy{ft}"] = st["bo_v"]
                P.op("vector", "tt", out=t1_buf.ap(), a=bank,
                     b=bo_sb.ap()[:, ft:ft + 1].broadcast_to([128, 512]),
                     alu=ALU.add,
                     wait=[("t", mmt),
                           (("v", st.get(f"bo_rdy{ft}", 0)) if b == 0 else None)],
                     inc=("v", 1))
                ev = P.op("vector", "tt", out=xtile(ft, lo, lo + 512),
                          a=xtile(ft, lo, lo + 512), b=t1_buf.ap(), alu=ALU.add,
                          inc=("v", 1))
                st[f"wo_ev{ft % 2}"] = ev
            st["wo_used_last"] = P.cur("v")
        P.barrier()

    def attn_b_hp(b, hp):
        if hp == 0:
            P.op("gpsimd", "memset", ap=v_sb.ap(), val=0.0, inc=("g", 1))
            for kt in range(TT):
                b0 = kt * 194
                P.op("gpsimd", "memset", ap=v_sb.ap()[:, b0 + 64:b0 + 65], val=1.0,
                     inc=("g", 1))
                P.op("gpsimd", "memset", ap=v_sb.ap()[:, b0 + 98:b0 + 99], val=1.0,
                     inc=("g", 1))
            st["vconst_g"] = P.cur("g")
        wq_tags = []
        for i, wT in enumerate((WqT, WkT, WvT)):
            wq_tags.append(P.op(
                "gpsimd", "dma",
                out=wq_sb.ap()[:, i * 1024:(i + 1) * 1024]
                .rearrange("p (a b) -> p a b", a=CT),
                in_=AP(wT.tensor, hp * 128, [[D, 128], [128 * D, CT], [1, 128]]),
                wait=[("t", st.get(f"ab_qkv{hp - 1}", 0))], inc=("dWq", 16)))
        pkt = P.op("sync", "dma", out=pk_sb.ap(),
                   in_=AP(pk_d, hp * 128 * S, [[S, 128], [1, S]]),
                   wait=[("t", st.get(f"ab_cc{hp - 1}", 0)),
                         ("v", st.get(f"ab_ed{hp - 1}", 0))], inc=("dPK", 16))
        pqt = P.op("sync", "dma", out=pqr_sb.ap(),
                   in_=AP(pqr_d, hp * 128 * S, [[S, 128], [1, S]]),
                   inc=("dPK", 16))
        for (wi, dst, bias, scl) in ((0, qb_sb, bq_sb, SCALE), (1, kb_sb, bk_sb, 1.0)):
            for tc in range(QCN):
                bank = pb[6] if (wi * QCN + tc) % 2 == 0 else pb[7]
                for ct in range(CT):
                    P.op("tensor", "mm", out=bank,
                         lhsT=wq_sb.ap()[:, wi * 1024 + ct * 128:
                                         wi * 1024 + ct * 128 + 128],
                         rhs=xntile(ct, tc * 512, tc * 512 + 512),
                         start=(ct == 0), stop=(ct == CT - 1),
                         wait=([("dWq", P.cur("dWq")),
                                ("v", st.get(f"ab_pev{(wi * QCN + tc) % 2}", 0)),
                                ("t", st.get(f"ab_qkuse{hp - 1}", 0))]
                               if ct == 0 else []),
                         inc=(("t", 1) if ct == CT - 1 else None))
                mmt = P.cur("t")
                st[f"ab_pev{(wi * QCN + tc) % 2}"] = P.op(
                    "vector", "stt", out=dst.ap()[:, tc * 512:(tc + 1) * 512],
                    in0=bank, s=scl,
                    in1=bias.ap()[:, hp:hp + 1].broadcast_to([128, 512]),
                    op0=ALU.mult, op1=ALU.add,
                    wait=[("t", mmt)], inc=("v", 1))
        for kt in range(TT):
            bank = pb[6] if kt % 2 == 0 else pb[7]
            for ct in range(CT):
                P.op("tensor", "mm", out=bank[:, 0:128],
                     lhsT=xntile(ct, kt * 128, kt * 128 + 128),
                     rhs=wq_sb.ap()[:, 2 * 1024 + ct * 128:2 * 1024 + ct * 128 + 128],
                     start=(ct == 0), stop=(ct == CT - 1),
                     wait=([("dWq", P.cur("dWq")), ("g", st["vconst_g"]),
                            ("sc", st.get(f"ab_pevV{kt % 2}", 0)),
                            ("v", st.get(f"ab_pev{kt % 2}", 0)),
                            ("t", st.get(f"ab_vuse{hp - 1}", 0))]
                           if ct == 0 else []),
                     inc=(("t", 1) if ct == CT - 1 else None))
            mmt = P.cur("t")
            b0 = kt * 194
            P.op("scalar", "copy", out=v_sb.ap()[:, b0:b0 + 64], in_=bank[:, 0:64],
                 wait=[("t", mmt)], inc=("sc", 1))
            st[f"ab_pevV{kt % 2}"] = P.op(
                "scalar", "copy", out=v_sb.ap()[:, b0 + 130:b0 + 194],
                in_=bank[:, 64:128], inc=("sc", 1))
        st[f"ab_qkv{hp}"] = P.cur("t")
        qkv_sc = P.cur("sc")
        if ABLIM <= 1:
            return

        edv = P.op("vector", "memset", ap=edlhsT.ap(), val=0.0,
                   wait=[("dPK", P.cur("dPK")),
                         ("t", st.get(f"ab_eduse{hp - 1}", 0))],
                   inc=("v", 1))
        P.op("vector", "copy", out=edlhsT.ap()[0:64, 0:1], in_=pk_sb.ap()[0:64, 0:1],
             inc=("v", 1))
        P.op("vector", "copy", out=edlhsT.ap()[0:64, 32:33],
             in_=pk_sb.ap()[0:64, 1023:1024], inc=("v", 1))
        P.op("vector", "copy", out=edlhsT.ap()[64:128, 64:65],
             in_=pk_sb.ap()[64:128, 0:1], inc=("v", 1))
        edv = P.op("vector", "copy", out=edlhsT.ap()[64:128, 96:97],
                   in_=pk_sb.ap()[64:128, 1023:1024], inc=("v", 1))
        for qc in range(QCN if ABLIM >= 3 else 0):
            emm = P.op("tensor", "mm", out=pb[4][0:97, :], lhsT=edlhsT.ap(),
                       rhs=qb_sb.ap()[:, qc * 512:(qc + 1) * 512],
                       start=True, stop=True,
                       wait=[("v", edv), ("sc", st.get("ab_edev", 0)),
                             ("sc", qkv_sc)],
                       inc=("t", 1))
            st["ab_edev"] = P.op("scalar", "copy",
                                 out=edge_sb.ap()[0:97, qc * 512:(qc + 1) * 512],
                                 in_=pb[4][0:97, :], wait=[("t", emm)], inc=("sc", 1))
        st[f"ab_ed{hp}"] = P.cur("v")
        ed_sc = P.cur("sc")
        if ABLIM <= 2.2:
            return

        wrtags = {}
        for mi in range(2):
            lb = qb_sb if mi == 0 else kb_sb
            rb = pk_sb if mi == 0 else pqr_sb
            for qt in range(TT):
                slot2 = qt % 2
                for h in range(2):
                    slot = slot2 * 2 + h
                    soff = slot * PADW
                    li = mi * TT + qt
                    for sc2 in range(2):
                        bank = pb[h * 2 + sc2]
                        P.op("tensor", "mm", out=bank,
                             lhsT=lb.ap()[h * 64:h * 64 + 64,
                                          qt * 128:qt * 128 + 128],
                             rhs=rb.ap()[h * 64:h * 64 + 64,
                                         sc2 * 512:sc2 * 512 + 512],
                             start=True, stop=True,
                             wait=([("sc", qkv_sc), ("dPK", P.cur("dPK")),
                                    st.get(f"ab_cbank{h}", None),
                                    ("v", st.get(f"ab_cpv{slot}", 0)),
                                    ("sc", st.get(f"ab_cpev{slot}", 0)),
                                    ("v", st.get("ab_ctxnv", 0)),
                                    (f"dP{slot}", P.cur(f"dP{slot}"))]
                                   if sc2 == 0 else []),
                             inc=("t", 1))
                    mmt = P.cur("t")
                    if ABLIM <= 2.3:
                        continue
                    if mi == 0:
                        P.op("vector", "copy",
                             out=cpad_sb.ap()[:, soff + 256:soff + 256 + 512],
                             in_=pb[h * 2], wait=[("t", mmt)], inc=("v", 1))
                        cpv2 = P.op("vector", "copy",
                                    out=cpad_sb.ap()[:, soff + 768:soff + 768 + 512],
                                    in_=pb[h * 2 + 1], inc=("v", 1))
                        P.op("vector", "copy",
                             out=cpad_sb.ap()[:, soff:soff + 256],
                             in_=cpad_sb.ap()[:, soff + 256:soff + 257]
                             .broadcast_to([128, 256]),
                             wait=[("v", cpv2)], inc=("v", 1))
                        pv = P.op("vector", "copy",
                                  out=cpad_sb.ap()[:, soff + 1280:soff + 1408],
                                  in_=cpad_sb.ap()[:, soff + 1279:soff + 1280]
                                  .broadcast_to([128, 128]), inc=("v", 1))
                        st[f"ab_cpv{slot}"] = pv
                        st[f"ab_cbank{h}"] = ("v", pv)
                        wdep = ("v", pv)
                    else:
                        P.op("scalar", "copy",
                             out=cpad_sb.ap()[:, soff + 256:soff + 256 + 512],
                             in_=pb[h * 2], wait=[("t", mmt)], inc=("sc", 1))
                        P.op("scalar", "copy",
                             out=cpad_sb.ap()[:, soff + 768:soff + 768 + 512],
                             in_=pb[h * 2 + 1], inc=("sc", 1))
                        P.op("scalar", "copy", out=ec2.ap()[:, h * 32 + 2 * qt:
                                                            h * 32 + 2 * qt + 1],
                             in_=pb[h * 2][:, 0:1], inc=("sc", 1))
                        P.op("scalar", "copy", out=ec2.ap()[:, h * 32 + 2 * qt + 1:
                                                            h * 32 + 2 * qt + 2],
                             in_=pb[h * 2 + 1][:, 511:512], inc=("sc", 1))
                        P.op("scalar", "copy",
                             out=cpad_sb.ap()[:, soff:soff + 256],
                             in_=pb[h * 2][:, 0:1].broadcast_to([128, 256]),
                             inc=("sc", 1))
                        pv = P.op("scalar", "copy",
                                  out=cpad_sb.ap()[:, soff + 1280:soff + 1408],
                                  in_=pb[h * 2 + 1][:, 511:512]
                                  .broadcast_to([128, 128]), inc=("sc", 1))
                        st[f"ab_cpev{slot}"] = pv
                        st[f"ab_cbank{h}"] = ("sc", pv)
                        wdep = ("sc", pv)
                    if ABLIM <= 2.4:
                        continue
                    wrtags[(li, h)] = P.op(
                        "sync", "dma",
                        out=AP(cpad_d, ((mi * 2 + h) * TT + qt) * 128 * PADW,
                               [[PADW, 128], [1, PADW]]),
                        in_=cpad_sb.ap()[:, soff:soff + PADW],
                        wait=[wdep], inc=(f"dP{slot}", 16))
        if ABLIM <= 3:
            return
        cc_sc, cc_v = P.cur("sc"), P.cur("v")

        for half in range(2):
            qlo_h, qhi_h = half * 1024, half * 1024 + 1024

            def emit_bands(kt):
                A, B = win(kt)
                a, bb = max(A, qlo_h), min(B, qhi_h)
                if a >= bb:
                    return None
                k0 = kt * 128
                for h in range(2):
                    bslot = h * 2 + (kt % 2)
                    boff = bslot * 1152
                    src = AP(cpad_d,
                             (0 * 2 + h) * TT * 128 * PADW +
                             a * (PADW - 1) + k0 + 768,
                             [[PADW - 1, bb - a], [1, 128]])
                    wdeps = []
                    for qt in range(a // 128, (bb - 1) // 128 + 1):
                        li = 0 * TT + qt
                        slot_w = (qt % 2) * 2 + h
                        wdeps.append((f"dP{slot_w}", wrtags.get((li, h), 0)))
                    wdeps.append(("t", st.get(f"ab_buse{bslot}", 0)))
                    P.op("sync", "dma",
                         out=band_sb.ap()[:, boff:boff + (bb - a)],
                         in_=src, transpose=True,
                         wait=wdeps, inc=(f"dB{kt % 2}", 16))
                    src2 = AP(cpad_d,
                              (1 * 2 + h) * TT * 128 * PADW +
                              k0 * (PADW - 1) + a + 767,
                              [[PADW - 1, 128], [1, bb - a]])
                    li2 = 1 * TT + kt
                    slot_w2 = (kt % 2) * 2 + h
                    P.op("gpsimd", "dma",
                         out=band_sb.ap()[:, boff:boff + (bb - a)],
                         in_=src2, accum=True,
                         wait=[(f"dB{kt % 2}", P.cur(f"dB{kt % 2}")),
                               (f"dP{slot_w2}", wrtags.get((li2, h), 0))],
                         inc=(f"dB2{kt % 2}", 16))
                return (a, bb)

            def emit_qk_exp(kt, ab):
                A, B = win(kt)
                for qci in range(2):
                    qc = half * 2 + qci
                    qlo = qc * 512
                    for h in range(2):
                        sbank = pb[(kt % 2) * 2 + h]
                        adds = []
                        if ab is not None:
                            a, bb = ab
                            a2, b2 = max(a, qlo), min(bb, qlo + 512)
                            if a2 < b2:
                                adds.append(("band", a2, b2, h * 2 + (kt % 2)))
                        ecut1 = min(max(A, qlo), qlo + 512)
                        ecut2 = min(max(B, qlo), qlo + 512)
                        if qlo < ecut1:
                            adds.append(("edgeR", qlo, ecut1, None))
                        if ecut2 < qlo + 512:
                            adds.append(("edgeL", ecut2, qlo + 512, None))
                        qkw = [("sc", st.get(f"ab_sbank{h * 2 + (kt % 2)}", 0))]
                        if kt == 1:
                            qkw.append(("v", st.get(f"ab_repuse{h}", 0)))
                        if kt < 2:
                            qkw.append(("sc", cc_sc))
                            qkw.append(("v", cc_v))
                        P.op("tensor", "mm", out=sbank,
                             lhsT=kb_sb.ap()[h * 64:h * 64 + 64,
                                             kt * 128:kt * 128 + 128],
                             rhs=qb_sb.ap()[h * 64:h * 64 + 64, qlo:qlo + 512],
                             start=True, stop=(len(adds) == 0),
                             wait=qkw, inc=("t", 1))
                        for ai, (knd, ra, rb2, bs_) in enumerate(adds):
                            lastf = (ai == len(adds) - 1)
                            if knd == "band":
                                boff = bs_ * 1152
                                P.op("tensor", "mm",
                                     out=sbank[:, ra - qlo:rb2 - qlo],
                                     lhsT=I128.ap(),
                                     rhs=band_sb.ap()[:, boff + ra - ab[0]:
                                                      boff + rb2 - ab[0]],
                                     start=False, stop=lastf,
                                     wait=[(f"dB2{kt % 2}",
                                            P.cur(f"dB2{kt % 2}")),
                                           ("dCg", st["setup_dCg"])],
                                     inc=("t", 1))
                            else:
                                r = (32 if knd == "edgeR" else 0) + h * 64
                                tp = (96, 0) if r == 96 else None
                                P.op("tensor", "mm",
                                     out=sbank[:, ra - qlo:rb2 - qlo],
                                     lhsT=ones_b.ap()[r:r + 1, 0:128],
                                     rhs=edge_sb.ap()[r:r + 1, ra:rb2],
                                     start=False, stop=lastf, tp=tp,
                                     wait=[("sc", ed_sc)], inc=("t", 1))
                        smm = P.cur("t")
                        st[f"ab_buse{h * 2 + (kt % 2)}"] = smm
                        pslot = (kt % 2) * 4 + qci * 2 + h
                        ptbuf = pt_sb if pslot < 4 else pt2_sb
                        poff = (pslot % 4) * 512
                        edeps = [("t", smm), ("t", st.get(f"ab_ptuse{pslot}", 0))]
                        cut1 = min(max(A, qlo), qlo + 512)
                        cut2 = min(max(B, qlo), qlo + 512)
                        segs = []
                        if qlo < cut1:
                            segs.append((qlo, cut1,
                                         ec2.ap()[:, h * 32 + 2 * kt:
                                                  h * 32 + 2 * kt + 1]))
                        if cut1 < cut2:
                            segs.append((cut1, cut2, 0.0))
                        if cut2 < qlo + 512:
                            segs.append((cut2, qlo + 512,
                                         ec2.ap()[:, h * 32 + 2 * kt + 1:
                                                  h * 32 + 2 * kt + 2]))
                        lastt = None
                        for (ra, rb_, bias) in segs:
                            lastt = P.op("scalar", "act",
                                         out=ptbuf.ap()[:, poff + ra - qlo:
                                                        poff + rb_ - qlo],
                                         in_=sbank[:, ra - qlo:rb_ - qlo],
                                         func=AF.Exp, bias=bias,
                                         wait=edeps, inc=("sc", 1))
                            edeps = []
                        st[f"ab_sbank{h * 2 + (kt % 2)}"] = lastt
                        st[("expk", kt, qci, h)] = lastt

            def emit_ctx(ktc):
                for qci in range(2):
                    for h in range(2):
                        cb = pb[4 + qci * 2 + h]
                        vb = v_sb.ap()
                        if h == 0:
                            lhsT = vb[:, ktc * 194:ktc * 194 + 65]
                        else:
                            lhsT = vb[:, ktc * 194 + 66:ktc * 194 + 194]
                        pslot = (ktc % 2) * 4 + qci * 2 + h
                        ptbuf = pt_sb if pslot < 4 else pt2_sb
                        poff = (pslot % 4) * 512
                        ctxw = [("sc", st[("expk", ktc, qci, h)])]
                        if ktc == 0:
                            ctxw.append(("v", st.get(f"ab_ctmp{qci * 2 + h}", 0)))
                            ctxw.append(("sc", st.get(f"ab_lnrd{qci * 2 + h}", 0)))
                            ctxw.append(("sc", ed_sc))
                        P.op("tensor", "mm", out=cb[0:65 if h == 0 else 128, :],
                             lhsT=lhsT, rhs=ptbuf.ap()[:, poff:poff + 512],
                             start=(ktc == 0), stop=(ktc == TT - 1),
                             wait=ctxw, inc=("t", 1))
                        st[f"ab_ptuse{pslot}"] = P.cur("t")

            BPREF = int(_os.environ.get("BPREF", "1"))
            CLAG = int(_os.environ.get("CLAG", "1"))
            if BPREF:
                ab_cur = emit_bands(0)
            for kt in range(TT):
                if BPREF:
                    ab_next = emit_bands(kt + 1) if kt + 1 < TT else None
                else:
                    ab_cur = emit_bands(kt)
                emit_qk_exp(kt, ab_cur)
                if CLAG:
                    if kt > 0:
                        emit_ctx(kt - 1)
                else:
                    emit_ctx(kt)
                if BPREF:
                    ab_cur = ab_next
            if CLAG:
                emit_ctx(TT - 1)
            ctx_all = P.cur("t")
            for qci in range(2 if ABLIM > 4 else 0):
                for h in range(2):
                    i = qci * 2 + h
                    cb = pb[4 + i]
                    rrow = 64 if h == 0 else 32
                    rbuf = rs_sb if qci == 0 else t1_buf
                    ln = P.op("scalar", "act",
                              out=rbuf.ap()[rrow:rrow + 1, 0:512],
                              in_=cb[rrow:rrow + 1, :], func=AF.Ln,
                              wait=[("t", ctx_all),
                                    ("t", st.get(f"ab_rsrep{i}", 0))],
                              inc=("sc", 1))
                    st[f"ab_lnrd{i}"] = ln
                    rv = P.op("scalar", "act",
                              out=rbuf.ap()[rrow:rrow + 1, 0:512],
                              in_=rbuf.ap()[rrow:rrow + 1, 0:512], func=AF.Exp,
                              scale=-1.0, inc=("sc", 1))
                    st[f"ab_rsexp{i}"] = rv
            for qci in range(2 if ABLIM > 4 else 0):
                qc = half * 2 + qci
                for h in range(2):
                    i = qci * 2 + h
                    cb = pb[4 + i]
                    rrow = 64 if h == 0 else 32
                    rbuf = rs_sb if qci == 0 else t1_buf
                    lhsT = (onesf.ap()[64:65, 0:128] if h == 0
                            else onesf.ap()[32:33, 128:256])
                    rep = P.op("tensor", "mm", out=pb[2 + h],
                               lhsT=lhsT, rhs=rbuf.ap()[rrow:rrow + 1, 0:512],
                               start=True, stop=True,
                               wait=[("sc", st[f"ab_rsexp{i}"]),
                                     ("sc", st.get(f"ab_sbank{h * 2 + 1}", 0)),
                                     ("v", st.get(f"ab_repuse{h}", 0))],
                               inc=("t", 1))
                    st[f"ab_rsrep{i}"] = rep
                    cv = P.op("vector", "copy",
                              out=(ctmp_sb.ap()[0:64, :] if h == 0
                                   else ctmp_sb.ap()[64:128, :]),
                              in_=(cb[0:64, :] if h == 0 else cb[64:128, :]),
                              wait=[("t", ctx_all)], inc=("v", 1))
                    st[f"ab_ctmp{i}"] = cv
                    vv = P.op("vector", "tt",
                              out=(ctxn_sb.ap()[0:64, qc * 512:(qc + 1) * 512]
                                   if h == 0 else
                                   ctxn_sb.ap()[64:128, qc * 512:(qc + 1) * 512]),
                              a=(ctmp_sb.ap()[0:64, :] if h == 0
                                 else ctmp_sb.ap()[64:128, :]),
                              b=(pb[2 + h][0:64, :] if h == 0
                                 else pb[2 + h][64:128, :]),
                              alu=ALU.mult,
                              wait=[("t", rep),
                                    ("dCXo", st.get(f"ab_ctxout{hp - 1}", 0))],
                              inc=("v", 1))
                    st["ab_ctxnv"] = vv
                    st[f"ab_repuse{h}"] = vv
        st[f"ab_vuse{hp}"] = P.cur("t")
        st[f"ab_eduse{hp}"] = P.cur("t")
        st[f"ab_qkuse{hp}"] = P.cur("t")
        st[f"ab_cc{hp}"] = P.cur("t")

        if ABLIM <= 4:
            return
        P.op("sync", "dma", out=AP(ctxn_d, hp * 128 * T, [[T, 128], [1, T]]),
             in_=ctxn_sb.ap(), wait=[("v", st["ab_ctxnv"])], inc=("dCXo", 16))
        st[f"ab_ctxout{hp}"] = P.cur("dCXo")

    for b in range(n_batches):
        xl = None
        for ct in range(CT):
            xl = P.op("sync", "dma", out=xtile(ct),
                      in_=AP(xT.tensor, b * D * T + ct * 128 * T, [[T, 128], [1, T]]),
                      inc=("dX", 16))
        P.op("vector", "noop", wait=[("dX", xl)])
        P.op("tensor", "noop", wait=[("dX", xl)])
        P.op("scalar", "noop", wait=[("dX", xl)])
        if 0 in stages:
            layer_norm()
            if dbg == "ln" and b == 0:
                dump(0, lambda ct: xntile(ct))
            attn_a()
            if dbg in ("aa", "aa2") and b == 0:
                dump(0, lambda ct: f_buf.ap(), nt=1)
                dump(1, lambda ct: g_buf.ap(), nt=1)
                dump(2, lambda ct: h_buf.ap(), nt=1)
                dump(3, lambda ct: og_buf.ap(), nt=1)
        if 1 in stages:
            layer_norm()
            if dbg == "ffn1pre" and b == 0:
                dump(3, lambda ct: xntile(ct))
            ffn()
            if dbg == "ffn1" and b == 0:
                dump(0, lambda ct: mrow_sb.ap(), nt=1)
                dump(1, lambda ct: mba_sb.ap()[0:128, 0:2048], nt=1)
                dump(2, lambda ct: xm_sb.ap()[:, 0:2048], nt=1)
        if 2 in stages:
            layer_norm()
            attn_b(b)
        if 3 in stages:
            layer_norm()
            ffn()
        snap = P.snapshot()
        for ct in range(CT):
            P.op("sync", "dma",
                 out=AP(outT.tensor, b * D * T + ct * 128 * T, [[T, 128], [1, T]]),
                 in_=xtile(ct), wait=[(n, v) for n, v in snap.items()],
                 inc=("dO", 16))
        P.op("sync", "noop", wait=[("dO", P.cur("dO"))])
    P.op("sync", "noop", wait=[("dO", P.cur("dO"))])

    P.emit()
    nc._dbg_cnt = dict(P.cnt)
    return nc


def host_consts():
    I = np.eye(128, dtype=np.float32)
    sel = np.zeros((8, E * 128), np.float32)
    for e in range(E):
        sel[e, e * 128:(e + 1) * 128] = 1.0
    onesf = np.zeros((128, 384), np.float32)
    onesf[:, 0:64] = 1.0
    onesf[:, 128 + 64:256] = 1.0
    onesf[:, 256:384] = 1.0
    return I, sel, onesf


def make_inputs(inp, core):
    x = np.asarray(inp["L0_attn_a_x"])
    xb = np.ascontiguousarray(
        x[core * NB:(core + 1) * NB].transpose(0, 2, 1)).astype(np.float32, copy=False)
    I, sel, onesf = host_consts()
    rel = np.asarray(inp["L0_attn_b_rel_emb"])
    d = {
        "xT": xb,
        "Wa": np.ascontiguousarray(inp["L0_attn_a_W"]),
        "ba": np.ascontiguousarray(inp["L0_attn_a_b"]),
        "WqT": np.ascontiguousarray(np.asarray(inp["L0_attn_b_Wq"]).T),
        "WkT": np.ascontiguousarray(np.asarray(inp["L0_attn_b_Wk"]).T),
        "WvT": np.ascontiguousarray(np.asarray(inp["L0_attn_b_Wv"]).T),
        "WoT": np.ascontiguousarray(np.asarray(inp["L0_attn_b_Wo"]).T),
        "WpkT": np.ascontiguousarray(np.asarray(inp["L0_attn_b_Wpk"]).T),
        "WpqT": np.ascontiguousarray(np.asarray(inp["L0_attn_b_Wpq"]).T),
        "relT": np.ascontiguousarray(rel.T),
        "relrevT": np.ascontiguousarray(rel[::-1].T),
        "bq": np.ascontiguousarray(inp["L0_attn_b_bq"]),
        "bk": np.ascontiguousarray(inp["L0_attn_b_bk"]),
        "bv": np.ascontiguousarray(inp["L0_attn_b_bv"]),
        "bo": np.ascontiguousarray(inp["L0_attn_b_bo"]),
        "bpk": np.ascontiguousarray(inp["L0_attn_b_bpk"]),
        "bpq": np.ascontiguousarray(inp["L0_attn_b_bpq"]),
        "gwT": np.ascontiguousarray(np.asarray(inp["L0_ffn_gate_w"]).T),
        "WeT": np.ascontiguousarray(np.asarray(inp["L0_ffn_expert_w"]).transpose(0, 2, 1)),
        "I128c": I, "SEL8c": sel, "ONESF": onesf,
    }
    return {k: np.asarray(v, dtype=np.float32) for k, v in d.items()}


from concourse.bass_utils import run_bass_kernel_spmd

_NCORES = 8


def kernel(**inputs):
    nc = build(stages=(0, 1, 2, 3), n_batches=NB)
    in_maps = [make_inputs(inputs, c) for c in range(_NCORES)]
    res = run_bass_kernel_spmd(nc, in_maps, core_ids=list(range(_NCORES)))
    out = np.empty((16, T, D), dtype=np.float32)
    for c in range(_NCORES):
        o = np.asarray(res.results[c]["outT"])
        out[c * NB:(c + 1) * NB] = o.transpose(0, 2, 1)
    return out
